# version 14
# baseline (speedup 1.0000x reference)
import contextlib
import numpy as np
import ml_dtypes
import concourse.bass as bass
import concourse.mybir as mybir
from concourse.bass_utils import run_bass_kernel_spmd

F32 = mybir.dt.float32
BF16 = mybir.dt.bfloat16
AF = mybir.ActivationFunctionType
ALU = mybir.AluOpType
AX = mybir.AxisListType
BF = ml_dtypes.bfloat16

NT = 4096
D = 1024
FF = 4096
NE = 8
M = 8192
EPS = 1e-5
GAMMA = 0.99
ENGS = ("sync", "gpsimd", "tensor", "vector", "scalar")


def _consts():
    t = {}
    a = np.arange(64); b = np.arange(128); k2 = np.arange(128); k1 = np.arange(64)
    m2 = np.arange(64)
    tp = 2 * np.pi
    t["f1c"] = np.cos(tp * np.outer(b, k2) / 128).astype(BF)
    t["f1s"] = (-np.sin(tp * np.outer(b, k2) / 128)).astype(BF)
    ec = np.cos(tp * np.outer(a, k1) / 64).astype(np.float32)
    es = np.sin(tp * np.outer(a, k1) / 64).astype(np.float32)
    bd = lambda mtx: np.kron(np.eye(2, dtype=np.float32), mtx)
    t["bdec"] = bd(ec).astype(BF)
    t["bdes"] = bd(es).astype(BF)
    t["bdesn"] = bd(-es).astype(BF)
    t["c2c"] = np.cos(tp * np.outer(k2, m2) / 128).astype(BF)
    t["c2s"] = (-np.sin(tp * np.outer(k2, m2) / 128)).astype(BF)
    twc1 = np.cos(tp * np.outer(a, k2) / M).astype(np.float32)
    tws1 = (-np.sin(tp * np.outer(a, k2) / M)).astype(np.float32)
    t["twc"] = np.concatenate([twc1, twc1], 0).astype(BF)
    t["tws"] = np.concatenate([tws1, tws1], 0).astype(BF)
    tw2c = (np.cos(tp * np.outer(k2, a) / M) / M).astype(np.float32)
    tw2s = (np.sin(tp * np.outer(k2, a) / M) / M).astype(np.float32)
    t["tw2c"] = np.concatenate([tw2c, tw2c], 1).astype(BF)
    t["tw2s"] = np.concatenate([tw2s, tw2s], 1).astype(BF)
    t["identf"] = np.eye(128, dtype=np.float32)
    t["identb"] = np.eye(128, dtype=BF)
    mm_ = (np.arange(64)[None, :] * 128 + np.arange(128)[:, None]).astype(np.int64)
    pos = np.where(mm_ <= 4095, mm_, mm_ - M).astype(np.float32)
    pos[mm_ == 4096] = 0.0
    dec = (GAMMA ** np.abs(pos)).astype(np.float32)
    dec[mm_ == 4096] = 0.0
    t["posx"] = np.repeat(pos[:, :, None], 32, axis=2).reshape(128, 2048).astype(np.float32)
    t["decayx"] = np.repeat(dec[:, :, None], 32, axis=2).reshape(128, 2048).astype(np.float32)
    t["onesrow"] = np.ones((1, 128), np.float32)
    t["ones11"] = np.ones((1, 1), np.float32)
    t["onescol_b"] = np.ones((128, 1), BF)
    t["iota8"] = np.arange(8, dtype=np.float32).reshape(1, 8)
    gb_ = np.zeros((128, 48), np.float32)
    gs_ = np.zeros((128, 48), np.float32)
    p_ = np.arange(128, dtype=np.float32)
    for c in range(8):
        gb_[:, c] = p_ + c * 128
        gs_[:, c] = 1024.0
    for c in range(32):
        gb_[:, 8 + c] = p_ + c * 128
        gs_[:, 8 + c] = 4096.0
    gb_[0:32, 40] = np.arange(32)
    gs_[0:32, 40] = 32.0
    t["gidx_base"] = gb_
    t["gscale"] = gs_
    return t


class Sem:
    def __init__(self, stack, nc, name):
        self.h = stack.enter_context(nc.semaphore(name))
        self.n = 0
        self.name = name

    def bump(self, k=1):
        self.n += k
        return self.n

    def tok(self):
        return (self, self.n)


class Phase:

    def __init__(self, nc):
        self.nc = nc
        self.q = {e: [] for e in ENGS}

    def op(self, eng, sem, deps, fn, inc=1):
        waits = [(s.h, v) for t_ in deps if t_ is not None for (s, v) in [t_] if v > 0]
        tok = None
        h = None
        if sem is not None:
            tok = (sem, sem.bump(inc))
            h = sem.h

        def th(E, waits=waits, h=h, inc=inc, fn=fn):
            for hh, vv in waits:
                E.wait_ge(hh, vv)
            ins = fn(E)
            if h is not None and ins is not None:
                ins.then_inc(h, inc)
        self.q[eng].append(th)
        return tok

    def run(self, name):
        with self.nc.Block(name) as blk:
            for eng in ENGS:
                ops = self.q[eng]
                if ops:
                    getattr(blk, eng)(lambda E, ops=ops: [f(E) for f in ops])


def build_nc(debug=False):
    nc = bass.Bass("TRN2", target_bir_lowering=False)
    T = _consts()

    def din(name, shape):
        return nc.dram_tensor(name, shape, F32, kind="ExternalInput").ap()

    x = din("x", [NT, D])
    uv_w = din("uv_w", [D, 2 * D])
    o_w = din("o_w", [D, D])
    rpe_ln_w = din("rpe_ln_w", [3, 32])
    rpe_ln_b = din("rpe_ln_b", [3, 32])
    rpe_fc_w = din("rpe_fc_w", [3, 32, 32])
    rpe_oln_w = din("rpe_oln_w", [1, 32])
    rpe_oln_b = din("rpe_oln_b", [1, 32])
    rpe_pos_w = din("rpe_pos_w", [1, 32])
    rpe_out_w = din("rpe_out_w", [32, D])
    W1 = din("W1", [NE, D, FF])
    b1 = din("b1", [NE, FF])
    W2 = din("W2", [NE, FF, D])
    b2 = din("b2", [NE, D])
    Wr = din("Wr", [D, NE])
    br = din("br", [1, NE])
    gamma = din("gamma", [1, D])
    beta = din("beta", [1, D])
    out = nc.dram_tensor("out", [NT, D], F32, kind="ExternalOutput").ap()

    sk = "ExternalOutput" if debug else "Internal"
    xn_dram = nc.dram_tensor("xn_dram", [NT, D], F32, kind=sk).ap()
    u_dram = nc.dram_tensor("u_dram", [D, NT], F32, kind=sk).ap()
    v_dram = nc.dram_tensor("v_dram", [D, NT], BF16, kind=sk).ap()
    t_dram = nc.dram_tensor("t_dram", [D, M], BF16, kind=sk).ap()
    y_dram = nc.dram_tensor("y_dram", [D, NT], F32, kind=sk).ap()
    s_dram = nc.dram_tensor("s_dram", [NT, D], F32, kind=sk).ap()
    moe_dram = nc.dram_tensor("moe_dram", [NT, D], F32, kind=sk).ap()
    xnT_dram = nc.dram_tensor("xnT_dram", [128, 8 * NT], BF16, kind="Internal").ap()
    b2mix_dram = nc.dram_tensor("b2mix_dram", [1, D], F32, kind="Internal").ap()

    ci = {k: nc.inline_tensor(v, f"c_{k}").ap() for k, v in T.items()}

    st = contextlib.ExitStack()
    with st:
        def sb(stk, name, shape, dt):
            return stk.enter_context(nc.sbuf_tensor(name, shape, dt))

        xnT_stack = contextlib.ExitStack()
        xnT = xnT_stack.enter_context(nc.sbuf_tensor("xnT", [128, 8 * NT], BF16, side="right"))
        C = {}
        for k, shape, dt in [
            ("f1c", [128, 128], BF16), ("f1s", [128, 128], BF16),
            ("bdec", [128, 128], BF16), ("bdes", [128, 128], BF16),
            ("bdesn", [128, 128], BF16), ("c2c", [128, 64], BF16),
            ("c2s", [128, 64], BF16), ("twc", [128, 128], BF16),
            ("tws", [128, 128], BF16), ("tw2c", [128, 128], BF16),
            ("tw2s", [128, 128], BF16), ("identf", [128, 128], F32),
            ("identb", [128, 128], BF16), ("onesrow", [1, 128], F32),
            ("ones11", [1, 1], F32), ("onescol_b", [128, 1], BF16),
            ("iota8", [1, 8], F32), ("gidx_base", [128, 48], F32),
            ("gscale", [128, 48], F32),
        ]:
            C[k] = sb(st, "cc_" + k, shape, dt)
        ebc = sb(st, "ebc", [128, 2], F32)
        cwslbc = sb(st, "cwslbc", [128, 2], F32)

        sP = Sem(st, nc, "sP")
        sA = Sem(st, nc, "sA")
        sV = Sem(st, nc, "sV")
        S = {}

        def dsem(name):
            if name not in S:
                S[name] = Sem(st, nc, "d_" + name)
            return S[name]

        ph = Phase(nc)
        cst = dsem("cst")
        const_ld = None
        for k in C:
            const_ld = ph.op("sync", cst, [], lambda sy, k=k: sy.dma_start(C[k][:], ci[k]), 16)

        with contextlib.ExitStack() as pp:
            ps_xsum = pp.enter_context(nc.psum_tensor("ps_xsum", [1, 1024], F32))
            ps_tr = [pp.enter_context(nc.psum_tensor(f"ps_tr{i}", [128, 128], BF16))
                     for i in range(2)]
            with contextlib.ExitStack() as sb0:
                xt = [sb(sb0, f"xt{i}", [128, D], F32) for i in range(2)]
                xnb = [sb(sb0, f"xnb{i}", [128, D], BF16) for i in range(2)]
                xnf = [sb(sb0, f"xnf{i}", [128, D], F32) for i in range(2)]
                stt = [sb(sb0, f"stt{i}", [128, 8], F32) for i in range(2)]
                ph.run("p0a")
                ph = Phase(nc)
                xts = [dsem("xt0"), dsem("xt1")]
                xns = [dsem("xns0"), dsem("xns1")]
                tile_pe_done = {}
                tile_store = {}
                ev_hist = []
                r8_hist = {}

                def p0_chain(t_):
                    bi = t_ % 2
                    deps = [tile_pe_done.get(t_ - 2), r8_hist.get(t_ - 2)] if t_ >= 2 else []
                    ld = ph.op("sync", xts[bi], deps,
                               lambda sy, t_=t_, bi=bi: sy.dma_start(
                                   xt[bi][:], x[t_ * 128:(t_ + 1) * 128, :]), 16)
                    yield
                    r1 = ph.op("vector", sV, [ld],
                               lambda v, bi=bi: v.tensor_reduce(stt[bi][:, 0:1], xt[bi][:], AX.X, ALU.add))
                    sqdeps = [ld]
                    if t_ >= 2:
                        sqdeps.append(tile_store[t_ - 2])
                    a1 = ph.op("scalar", sA, sqdeps,
                               lambda sc, bi=bi: sc.activation(xnf[bi][:], xt[bi][:], AF.Square,
                                                               accum_out=stt[bi][:, 1:2]))
                    yield
                    r2 = ph.op("vector", sV, [r1],
                               lambda v, bi=bi: v.tensor_scalar_mul(stt[bi][:, 2:3], stt[bi][:, 0:1], 1.0 / D))
                    yield
                    r3 = ph.op("vector", sV, [a1, r2],
                               lambda v, bi=bi: v.tensor_scalar(stt[bi][:, 3:4], stt[bi][:, 1:2], 1.0 / D, EPS,
                                                                ALU.mult, ALU.add))
                    yield
                    r4 = ph.op("vector", sV, [r2],
                               lambda v, bi=bi: v.tensor_tensor(stt[bi][:, 4:5], stt[bi][:, 2:3],
                                                                stt[bi][:, 2:3], ALU.mult))
                    yield
                    r5 = ph.op("vector", sV, [r4, r3],
                               lambda v, bi=bi: v.tensor_tensor(stt[bi][:, 3:4], stt[bi][:, 3:4],
                                                                stt[bi][:, 4:5], ALU.subtract))
                    yield
                    a2 = ph.op("scalar", sA, [r5],
                               lambda sc, bi=bi: sc.activation(stt[bi][:, 5:6], stt[bi][:, 3:4], AF.Sqrt))
                    yield
                    r6 = ph.op("vector", sV, [a2],
                               lambda v, bi=bi: v.reciprocal(stt[bi][:, 6:7], stt[bi][:, 5:6]))
                    yield
                    r7 = ph.op("vector", sV, [r6],
                               lambda v, bi=bi: v.tensor_scalar(xnb[bi][:], xt[bi][:], stt[bi][:, 2:3],
                                                                stt[bi][:, 6:7], ALU.subtract, ALU.mult))
                    yield
                    r8 = ph.op("vector", sV, [r7],
                               lambda v, bi=bi: v.tensor_scalar(xnf[bi][:], xt[bi][:], stt[bi][:, 2:3],
                                                                stt[bi][:, 6:7], ALU.subtract, ALU.mult))
                    r8_hist[t_] = r8
                    tile_store[t_] = ph.op("scalar", xns[bi], [r8],
                                           lambda sc, t_=t_, bi=bi: sc.dma_start(
                                               xn_dram[t_ * 128:(t_ + 1) * 128, :], xnf[bi][:]), 16)
                    yield
                    for db in range(8):
                        pdeps = [r7]
                        if not ev_hist:
                            pdeps.append(const_ld)
                        if len(ev_hist) >= 2:
                            pdeps.append(ev_hist[-2])
                        pm = ph.op("tensor", sP, pdeps,
                                   lambda t, bi=bi, db=db: t.transpose(
                                       ps_tr[db % 2][:], xnb[bi][:, db * 128:(db + 1) * 128],
                                       C["identb"][:]))
                        ev_hist.append(ph.op("scalar", sA, [pm],
                                             lambda sc, t_=t_, db=db: sc.activation(
                                                 xnT[:, db * NT + t_ * 128: db * NT + (t_ + 1) * 128],
                                                 ps_tr[db % 2][:], AF.Copy)))
                    ph.op("tensor", sP, [r7, const_ld],
                          lambda t, t_=t_, bi=bi: t.matmul(
                              ps_xsum[0:1, 0:512], C["onescol_b"][:], xnb[bi][:, 0:512],
                              start=(t_ == 0), stop=(t_ == 31)))
                    m2 = ph.op("tensor", sP, [],
                               lambda t, t_=t_, bi=bi: t.matmul(
                                   ps_xsum[0:1, 512:1024], C["onescol_b"][:], xnb[bi][:, 512:1024],
                                   start=(t_ == 0), stop=(t_ == 31)))
                    tile_pe_done[t_] = m2

                prev = None
                for pr in range(16):
                    ga = p0_chain(2 * pr)
                    gb = p0_chain(2 * pr + 1)
                    alive = [ga, gb]
                    while alive:
                        nxt = []
                        for g_ in alive:
                            try:
                                next(g_)
                                nxt.append(g_)
                            except StopIteration:
                                pass
                        alive = nxt
                xsum_mm = tile_pe_done[31]
                xn_stores_l = [xns[0].tok(), xns[1].tok()]
                ph.run("p0b")

            with contextlib.ExitStack() as sb1:
                xsum_sb = sb(sb1, "xsum_sb", [1, 1024], F32)
                xmT = sb(sb1, "xmT", [128, 8], F32)
                wr_sb = sb(sb1, "wr_sb", [128, 64], F32)
                b2sb = sb(sb1, "b2sb", [8, 1024], F32)
                brr = sb(sb1, "brr", [1, 8], F32)
                rw = sb(sb1, "rw", [1, 8], F32)
                rt = sb(sb1, "rt", [1, 8], F32)
                eq1 = sb(sb1, "eq1", [1, 8], F32)
                eq2 = sb(sb1, "eq2", [1, 8], F32)
                s4 = sb(sb1, "s4", [1, 4], F32)
                cw8 = sb(sb1, "cw8", [1, 8], F32)
                cw8T = sb(sb1, "cw8T", [8, 1], F32)
                b2mix = sb(sb1, "b2mix", [1, D], F32)
                eidx = sb(sb1, "eidx", [1, 2], F32)
                cwrow = sb(sb1, "cwrow", [1, 2], F32)
                with contextlib.ExitStack() as pp1:
                    ps_m = pp1.enter_context(nc.psum_tensor("ps_m", [128, 512], F32))
                    ps_r = pp1.enter_context(nc.psum_tensor("ps_r", [1, 512], F32))
                    ps_c8 = pp1.enter_context(nc.psum_tensor("ps_c8", [8, 8], F32))
                    ph = Phase(nc)
                    p1s = dsem("p1s")
                    for apdst, apsrc in [(wr_sb[:], Wr.rearrange("(c p) e -> p c e", p=128)),
                                         (b2sb[:], b2), (brr[:], br)]:
                        p1_all = ph.op("sync", p1s, xn_stores_l,
                                       lambda sy, apdst=apdst, apsrc=apsrc: sy.dma_start(apdst, apsrc), 16)
                    e1 = ph.op("vector", sV, [xsum_mm] + xn_stores_l,
                               lambda v: v.tensor_copy(xsum_sb[:, 0:512], ps_xsum[0:1, 0:512]))
                    e2 = ph.op("vector", sV, [e1],
                               lambda v: v.tensor_copy(xsum_sb[:, 512:1024], ps_xsum[0:1, 512:1024]))
                    mt = None
                    for k in range(8):
                        mt = ph.op("tensor", sP, [e2, const_ld],
                                   lambda t, k=k: t.matmul(ps_m[:, k:k + 1],
                                                           xsum_sb[0:1, k * 128:(k + 1) * 128],
                                                           C["ones11"][:], start=True, stop=True))
                    v1 = ph.op("vector", sV, [mt],
                               lambda v: v.tensor_scalar_mul(xmT[:], ps_m[:, 0:8], 1.0 / NT))
                    rmm = None
                    for k in range(8):
                        rmm = ph.op("tensor", sP, [v1, p1_all],
                                    lambda t, k=k: t.matmul(ps_r[0:1, 0:8], xmT[:, k:k + 1],
                                                            wr_sb[:, k * 8:(k + 1) * 8],
                                                            start=(k == 0), stop=(k == 7)))
                    c1 = ph.op("vector", sV, [rmm],
                               lambda v: v.tensor_copy(rw[:], ps_r[0:1, 0:8]))
                    c2 = ph.op("vector", sV, [c1],
                               lambda v: v.tensor_tensor(rw[:], rw[:], brr[:], ALU.add))
                    c3 = ph.op("vector", sV, [c2],
                               lambda v: v.tensor_reduce(s4[:, 0:1], rw[:], AX.X, ALU.max))
                    c4 = ph.op("vector", sV, [c3],
                               lambda v: v.tensor_scalar(rt[:], rw[:], s4[:, 0:1], None, ALU.subtract))
                    c5 = ph.op("scalar", sA, [c4],
                               lambda sc: sc.activation(rt[:], rt[:], AF.Exp))
                    c6 = ph.op("vector", sV, [c5],
                               lambda v: v.tensor_reduce(s4[:, 1:2], rt[:], AX.X, ALU.add))
                    c7 = ph.op("vector", sV, [c6],
                               lambda v: v.reciprocal(s4[:, 2:3], s4[:, 1:2]))
                    c8 = ph.op("vector", sV, [c7],
                               lambda v: v.tensor_scalar(rw[:], rt[:], s4[:, 2:3], None, ALU.mult))
                    d1 = ph.op("vector", sV, [c8],
                               lambda v: v.tensor_reduce(s4[:, 0:1], rw[:], AX.X, ALU.max))
                    d2 = ph.op("vector", sV, [d1],
                               lambda v: v.tensor_scalar(eq1[:], rw[:], s4[:, 0:1], None, ALU.is_equal))
                    d3 = ph.op("vector", sV, [d2],
                               lambda v: v.tensor_scalar_mul(rt[:], eq1[:], -3.0))
                    d4 = ph.op("vector", sV, [d3],
                               lambda v: v.tensor_tensor(rt[:], rw[:], rt[:], ALU.add))
                    d5 = ph.op("vector", sV, [d4],
                               lambda v: v.tensor_reduce(s4[:, 1:2], rt[:], AX.X, ALU.max))
                    d6 = ph.op("vector", sV, [d5],
                               lambda v: v.tensor_scalar(eq2[:], rt[:], s4[:, 1:2], None, ALU.is_equal))
                    i1 = ph.op("vector", sV, [d6],
                               lambda v: v.tensor_tensor(rt[:], eq1[:], C["iota8"][:], ALU.mult))
                    i2 = ph.op("vector", sV, [i1],
                               lambda v: v.tensor_reduce(eidx[:, 0:1], rt[:], AX.X, ALU.add))
                    i3 = ph.op("vector", sV, [i2],
                               lambda v: v.tensor_tensor(rt[:], eq2[:], C["iota8"][:], ALU.mult))
                    i4 = ph.op("vector", sV, [i3],
                               lambda v: v.tensor_reduce(eidx[:, 1:2], rt[:], AX.X, ALU.add))
                    d7 = ph.op("vector", sV, [i4],
                               lambda v: v.tensor_tensor(s4[:, 2:3], s4[:, 1:2], s4[:, 0:1], ALU.subtract))
                    d8 = ph.op("scalar", sA, [d7],
                               lambda sc: sc.activation(s4[:, 2:3], s4[:, 2:3], AF.Exp))
                    d9 = ph.op("vector", sV, [d8],
                               lambda v: v.tensor_scalar(s4[:, 3:4], s4[:, 2:3], 1.0, None, ALU.add))
                    d10 = ph.op("vector", sV, [d9],
                                lambda v: v.reciprocal(s4[:, 3:4], s4[:, 3:4]))
                    d11 = ph.op("vector", sV, [d10],
                                lambda v: v.tensor_scalar(eq1[:], eq1[:], s4[:, 3:4], None, ALU.mult))
                    d12 = ph.op("vector", sV, [d11],
                                lambda v: v.tensor_scalar(s4[:, 2:3], s4[:, 3:4], -1.0, 1.0,
                                                          ALU.mult, ALU.add))
                    d13 = ph.op("vector", sV, [d12],
                                lambda v: v.tensor_scalar(eq2[:], eq2[:], s4[:, 2:3], None, ALU.mult))
                    d14 = ph.op("vector", sV, [d13],
                                lambda v: v.tensor_tensor(cw8[:], eq1[:], eq2[:], ALU.add))
                    d15 = ph.op("vector", sV, [d12],
                                lambda v: v.tensor_copy(cwrow[:, 0:1], s4[:, 3:4]))
                    d16 = ph.op("vector", sV, [d15],
                                lambda v: v.tensor_copy(cwrow[:, 1:2], s4[:, 2:3]))
                    g1 = ph.op("tensor", sP, [d16],
                               lambda t: t.matmul(ps_m[:, 0:2], C["onesrow"][:], cwrow[:],
                                                  start=True, stop=True))
                    g1b = ph.op("tensor", sP, [d16],
                                lambda t: t.matmul(ps_m[:, 8:10], C["onesrow"][:], eidx[:],
                                                   start=True, stop=True))
                    g2 = ph.op("tensor", sP, [d14],
                               lambda t: t.matmul(ps_c8[:, 0:1], cw8[:], C["ones11"][:],
                                                  start=True, stop=True))
                    g3 = ph.op("vector", sV, [g1b],
                               lambda v: v.tensor_copy(cwslbc[:], ps_m[:, 0:2]))
                    g3b = ph.op("vector", sV, [g3],
                                lambda v: v.tensor_copy(ebc[:], ps_m[:, 8:10]))
                    g4 = ph.op("vector", sV, [g3b, g2],
                               lambda v: v.tensor_copy(cw8T[:], ps_c8[:, 0:1]))
                    h1m = ph.op("tensor", sP, [g4],
                                lambda t: t.matmul(ps_r[0:1, 0:512], cw8T[:], b2sb[:, 0:512],
                                                   start=True, stop=True))
                    h1e = ph.op("vector", sV, [h1m],
                                lambda v: v.tensor_copy(b2mix[:, 0:512], ps_r[0:1, 0:512]))
                    h2m = ph.op("tensor", sP, [h1e],
                                lambda t: t.matmul(ps_r[0:1, 0:512], cw8T[:], b2sb[:, 512:1024],
                                                   start=True, stop=True))
                    h2e = ph.op("vector", sV, [h2m],
                                lambda v: v.tensor_copy(b2mix[:, 512:1024], ps_r[0:1, 0:512]))
                    b2s = dsem("b2s")
                    b2mix_store = ph.op("sync", b2s, [h2e],
                                        lambda sy: sy.dma_start(b2mix_dram, b2mix[:]), 16)
                    ph.run("p1")

        with contextlib.ExitStack() as pp:
            ps_uv = [pp.enter_context(nc.psum_tensor(f"ps_uv{i}", [128, 512], F32)) for i in range(2)]
            with contextlib.ExitStack() as sb2:
                wt = [sb(sb2, f"uvw{i}", [128, 8 * 128], BF16) for i in range(2)]
                ev = [sb(sb2, f"uvev{i}", [128, 512], F32) for i in range(2)]
                evb = [sb(sb2, f"uvevb{i}", [128, 512], BF16) for i in range(2)]
                ph = Phase(nc)
                wts = [dsem("wt0"), dsem("wt1")]
                evs_s = [dsem("evs0"), dsem("evs1")]
                evbs_s = [dsem("evbs0"), dsem("evbs1")]
                mm_last = {}
                ev_hist2 = {}
                st_hist = {}
                for fb in range(16):
                    wdeps = [mm_last.get(fb - 2)] if fb >= 2 else [b2mix_store]
                    wl = ph.op("gpsimd", wts[fb % 2], wdeps,
                               lambda g, fb=fb: g.dma_start(
                                   wt[fb % 2][:],
                                   uv_w[:, fb * 128:(fb + 1) * 128].rearrange("(c p) f -> p c f", p=128)), 16)
                    for tc in range(8):
                        idx = fb * 8 + tc
                        mdeps = [wl]
                        if idx >= 2:
                            mdeps.append(ev_hist2[idx - 2])
                        last = None
                        for dc in range(8):
                            last = ph.op("tensor", sP, mdeps if dc == 0 else [],
                                         lambda t, fb=fb, tc=tc, dc=dc: t.matmul(
                                             ps_uv[tc % 2][:], wt[fb % 2][:, dc * 128:(dc + 1) * 128],
                                             xnT[:, dc * NT + tc * 512: dc * NT + (tc + 1) * 512],
                                             start=(dc == 0), stop=(dc == 7)))
                        mm_last[fb] = last
                        edeps = [last]
                        if idx >= 2:
                            edeps.append(st_hist[idx - 2])
                        dst = ev if fb < 8 else evb
                        em = ph.op("scalar", sA, edeps,
                                   lambda sc, tc=tc, dst=dst: sc.activation(
                                       dst[tc % 2][:], ps_uv[tc % 2][:], AF.Silu))
                        if fb < 8:
                            stm = ph.op("sync", evs_s[tc % 2], [em],
                                        lambda sy, fb=fb, tc=tc: sy.dma_start(
                                            u_dram[fb * 128:(fb + 1) * 128, tc * 512:(tc + 1) * 512],
                                            ev[tc % 2][:]), 16)
                        else:
                            stm = ph.op("sync", evbs_s[tc % 2], [em],
                                        lambda sy, fb=fb, tc=tc: sy.dma_start(
                                            v_dram[(fb - 8) * 128:(fb - 7) * 128, tc * 512:(tc + 1) * 512],
                                            evb[tc % 2][:]), 16)
                        ev_hist2[idx] = em
                        st_hist[idx] = stm
                u_stores = [evs_s[0].tok(), evs_s[1].tok()]
                v_stores = [evbs_s[0].tok(), evbs_s[1].tok()]
                ph.run("p2")

        with contextlib.ExitStack() as pp:
            ps_t32 = [pp.enter_context(nc.psum_tensor(f"ps_t32_{i}", [32, 128], F32)) for i in range(2)]
            ps_h32 = [pp.enter_context(nc.psum_tensor(f"ps_h32_{i}", [128, 32], F32)) for i in range(2)]
            ps_tg = [pp.enter_context(nc.psum_tensor(f"ps_tg{i}", [128, 512], F32)) for i in range(2)]
            with contextlib.ExitStack() as sb3:
                posx = sb(sb3, "posx", [128, 2048], F32)
                decayx = sb(sb3, "decayx", [128, 2048], F32)
                h1 = sb(sb3, "h1", [128, 2048], F32)
                h2 = sb(sb3, "h2", [128, 2048], F32)
                hsq = sb(sb3, "hsq", [128, 2048], F32)
                st64 = sb(sb3, "st64", [128, 256], F32)
                hT = sb(sb3, "hT", [32, M], BF16)
                fcw = sb(sb3, "fcw", [32, 96], BF16)
                woutb = sb(sb3, "woutb", [32, D], BF16)
                lnrow = sb(sb3, "lnrow", [1, 9 * 32], F32)
                lnbc = sb(sb3, "lnbc", [128, 9 * 32], F32)
                tg = [sb(sb3, f"tg{i}", [128, 512], BF16) for i in range(2)]
                ph = Phase(nc)
                p3s = dsem("p3s")
                xds = dsem("xds")
                tgs = [dsem("tgs0"), dsem("tgs1")]
                xnT_dump = None
                for k in range(8):
                    xnT_dump = ph.op("sync", xds, [],
                                     lambda sy, k=k: sy.dma_start(
                                         xnT_dram[:, k * NT:(k + 1) * NT],
                                         xnT[:, k * NT:(k + 1) * NT]), 16)
                ph.op("sync", p3s, u_stores + v_stores,
                      lambda sy: sy.dma_start(posx[:], ci["posx"]), 16)
                ph.op("sync", p3s, [], lambda sy: sy.dma_start(decayx[:], ci["decayx"]), 16)
                p3g = dsem("p3g")
                for i in range(3):
                    ph.op("gpsimd", p3g, u_stores + v_stores if i == 0 else [],
                          lambda g, i=i: g.dma_start(fcw[:, i * 32:(i + 1) * 32],
                                                     rpe_fc_w[i]), 16)
                ph.op("gpsimd", p3g, [], lambda g: g.dma_start(woutb[:], rpe_out_w), 16)
                rows = [(0, rpe_ln_w, 0), (1, rpe_ln_w, 1), (2, rpe_ln_w, 2),
                        (3, rpe_ln_b, 0), (4, rpe_ln_b, 1), (5, rpe_ln_b, 2),
                        (6, rpe_oln_w, None), (7, rpe_oln_b, None), (8, rpe_pos_w, None)]
                p3_all = None
                for slot, src, ri in rows:
                    ap_src = src[ri:ri + 1, :] if ri is not None else src
                    p3_all = ph.op("sync", p3s, [], lambda sy, slot=slot, ap_src=ap_src: sy.dma_start(
                        lnrow[:, slot * 32:(slot + 1) * 32], ap_src), 16)
                bcd = None
                for slot in range(9):
                    bdeps = [p3_all]
                    if bcd is not None:
                        bdeps.append(bcd)
                    bm = ph.op("tensor", sP, bdeps,
                               lambda t, slot=slot: t.matmul(ps_tg[slot % 2][:, 0:32], C["onesrow"][:],
                                                             lnrow[0:1, slot * 32:(slot + 1) * 32],
                                                             start=True, stop=True))
                    bcd = ph.op("vector", sV, [bm],
                                lambda v, slot=slot: v.tensor_copy(lnbc[:, slot * 32:(slot + 1) * 32],
                                                                   ps_tg[slot % 2][:, 0:32]))

                def r3ap(t_ap):
                    return t_ap.rearrange("p (g j) -> p g j", j=32)

                def ln_silu(hin, wslot, bslot, dep0):
                    q1 = ph.op("vector", sV, [dep0, bcd],
                               lambda v: v.tensor_reduce(st64[:, 0:64], r3ap(hin.ap()), AX.X, ALU.add))
                    q2 = ph.op("vector", sV, [q1],
                               lambda v: v.tensor_scalar_mul(st64[:, 0:64], st64[:, 0:64], 1.0 / 32))
                    q3 = ph.op("vector", sV, [q2],
                               lambda v: v.tensor_tensor(
                                   r3ap(hin.ap()), r3ap(hin.ap()),
                                   st64[:, 0:64].unsqueeze(2).broadcast_to([128, 64, 32]), ALU.subtract))
                    q4 = ph.op("scalar", sA, [q3],
                               lambda sc: sc.activation(hsq[:], hin[:], AF.Square))
                    q5 = ph.op("vector", sV, [q4],
                               lambda v: v.tensor_reduce(st64[:, 64:128], r3ap(hsq.ap()), AX.X, ALU.add))
                    q6 = ph.op("vector", sV, [q5],
                               lambda v: v.tensor_scalar(st64[:, 64:128], st64[:, 64:128], 1.0 / 32,
                                                         EPS, ALU.mult, ALU.add))
                    q7 = ph.op("scalar", sA, [q6],
                               lambda sc: sc.activation(st64[:, 128:192], st64[:, 64:128], AF.Sqrt))
                    q8 = ph.op("vector", sV, [q7],
                               lambda v: v.reciprocal(st64[:, 192:256], st64[:, 128:192]))
                    q9 = ph.op("vector", sV, [q8],
                               lambda v: v.tensor_tensor(
                                   r3ap(hin.ap()), r3ap(hin.ap()),
                                   st64[:, 192:256].unsqueeze(2).broadcast_to([128, 64, 32]), ALU.mult))
                    q10 = ph.op("vector", sV, [q9],
                                lambda v: v.tensor_tensor(
                                    r3ap(hin.ap()), r3ap(hin.ap()),
                                    lnbc[:, wslot * 32:(wslot + 1) * 32].unsqueeze(1).broadcast_to([128, 64, 32]),
                                    ALU.mult))
                    q11 = ph.op("vector", sV, [q10],
                                lambda v: v.tensor_tensor(
                                    r3ap(hin.ap()), r3ap(hin.ap()),
                                    lnbc[:, bslot * 32:(bslot + 1) * 32].unsqueeze(1).broadcast_to([128, 64, 32]),
                                    ALU.add))
                    q12 = ph.op("scalar", sA, [q11],
                                lambda sc: sc.activation(hin[:], hin[:], AF.Silu))
                    return q12

                h0 = ph.op("vector", sV, [bcd, p3_all, p3g.tok()],
                           lambda v: v.tensor_tensor(
                               r3ap(h1.ap()), r3ap(posx.ap()),
                               lnbc[:, 8 * 32:9 * 32].unsqueeze(1).broadcast_to([128, 64, 32]), ALU.mult))
                dep = h0
                hin = h1
                for layer in range(3):
                    hout = h2 if hin is h1 else h1
                    sdone = ln_silu(hin, layer, 3 + layer, dep)
                    evl = {}
                    for pt in range(64):
                        pd = [sdone] if pt == 0 else []
                        if pt >= 2:
                            pd.append(evl[pt - 2])
                        pm = ph.op("tensor", sP, pd,
                                   lambda t, pt=pt, hin=hin: t.transpose(
                                       ps_t32[pt % 2][:], hin[:, pt * 32:(pt + 1) * 32], C["identf"][:]))
                        evl[pt] = ph.op("scalar", sA, [pm],
                                        lambda sc, pt=pt: sc.activation(
                                            hT[:, pt * 128:(pt + 1) * 128], ps_t32[pt % 2][:], AF.Copy))
                    ev2 = {}
                    for pt in range(64):
                        pd = [evl[63]] if pt == 0 else []
                        if pt >= 2:
                            pd.append(ev2[pt - 2])
                        pm = ph.op("tensor", sP, pd,
                                   lambda t, pt=pt, layer=layer: t.matmul(
                                       ps_h32[pt % 2][:], hT[:, pt * 128:(pt + 1) * 128],
                                       fcw[:, layer * 32:(layer + 1) * 32], start=True, stop=True))
                        ev2[pt] = ph.op("scalar", sA, [pm],
                                        lambda sc, pt=pt, hout=hout: sc.activation(
                                            hout[:, pt * 32:(pt + 1) * 32], ps_h32[pt % 2][:], AF.Copy))
                    dep = ev2[63]
                    hin = hout
                sdone = ln_silu(hin, 6, 7, dep)
                pg = ph.op("vector", sV, [sdone],
                           lambda v, hin=hin: v.tensor_tensor(hin[:], hin[:], decayx[:], ALU.mult))
                evl = {}
                for pt in range(64):
                    pd = [pg] if pt == 0 else []
                    if pt >= 2:
                        pd.append(evl[pt - 2])
                    pm = ph.op("tensor", sP, pd,
                               lambda t, pt=pt, hin=hin: t.transpose(
                                   ps_t32[pt % 2][:], hin[:, pt * 32:(pt + 1) * 32], C["identf"][:]))
                    evl[pt] = ph.op("scalar", sA, [pm],
                                    lambda sc, pt=pt: sc.activation(
                                        hT[:, pt * 128:(pt + 1) * 128], ps_t32[pt % 2][:], AF.Copy))
                tev = {}
                tstv = {}
                for cb in range(8):
                    for mc2 in range(16):
                        i = cb * 16 + mc2
                        pd = [evl[63]] if i == 0 else []
                        if i >= 2:
                            pd.append(tev[i - 2])
                        pm = ph.op("tensor", sP, pd,
                                   lambda t, cb=cb, mc2=mc2, i=i: t.matmul(
                                       ps_tg[i % 2][:], woutb[:, cb * 128:(cb + 1) * 128],
                                       hT[:, mc2 * 512:(mc2 + 1) * 512], start=True, stop=True))
                        ed = [pm]
                        if i >= 2:
                            ed.append(tstv[i - 2])
                        em = ph.op("scalar", sA, ed,
                                   lambda sc, i=i: sc.activation(tg[i % 2][:], ps_tg[i % 2][:], AF.Copy))
                        sm = ph.op("sync", tgs[i % 2], [em],
                                   lambda sy, cb=cb, mc2=mc2, i=i: sy.dma_start(
                                       t_dram[cb * 128:(cb + 1) * 128, mc2 * 512:(mc2 + 1) * 512],
                                       tg[i % 2][:]), 16)
                        tev[i] = em
                        tstv[i] = sm
                t_stores = [tgs[0].tok(), tgs[1].tok()]
                xd_total = xds.tok()
                ph.run("p3")

        xnT_stack.close()
        with contextlib.ExitStack() as pp:
            psE4 = [pp.enter_context(nc.psum_tensor(f"psE{i}", [128, 512], F32)) for i in range(4)]
            psY4 = pp.enter_context(nc.psum_tensor("psY4", [64, 2048], F32))
            with contextlib.ExitStack() as sb4:
                vsl = [sb(sb4, f"vsl{i}", [64, 4096], BF16) for i in range(2)]
                tsl = [sb(sb4, f"tsl{i}", [128, 4096], BF16) for i in range(2)]
                bA = [[sb(sb4, f"bA{p}_{i}", [128, 4096], BF16) for i in range(2)] for p in range(2)]
                bB = [[sb(sb4, f"bB{p}_{i}", [128, 4096], BF16) for i in range(2)] for p in range(2)]
                bC = [[sb(sb4, f"bC{p}_{i}", [128, 4096], BF16) for i in range(2)] for p in range(2)]
                ysb = [sb(sb4, f"ysb{p}", [64, 4096], F32) for p in range(2)]
                ph = Phase(nc)
                vls = [dsem("vl0"), dsem("vl1")]
                tls = [dsem("tl0"), dsem("tl1")]
                yss = dsem("ys")
                st_ = {}
                ydma_hist = {}
                g3 = lambda ap_: ap_.rearrange("p (g j) -> p g j", j=128)
                twcB = C["twc"].ap().unsqueeze(1).broadcast_to([128, 32, 128])
                twsB = C["tws"].ap().unsqueeze(1).broadcast_to([128, 32, 128])
                t2cB = C["tw2c"].ap().unsqueeze(1).broadcast_to([128, 32, 128])
                t2sB = C["tw2s"].ap().unsqueeze(1).broadcast_to([128, 32, 128])

                def step1(slab, nb, ldtok, dst, psfree, bufree):
                    evs = []
                    last = None
                    for gb in range(8):
                        pe0 = psE4[2 * (gb % 2)]
                        pe1 = psE4[2 * (gb % 2) + 1]
                        for g4 in range(4):
                            g = gb * 4 + g4
                            dep = []
                            if gb == 0 and g4 == 0:
                                dep = [ldtok] + psfree
                            if g4 == 0 and gb >= 2:
                                dep.append(evs[gb - 2])
                            lhs = slab.ap().rearrange("b (d a) -> b d a", a=64)[:, 2 * g:2 * g + 2, :]
                            ph.op("tensor", sP, dep,
                                  lambda t, lhs=lhs, g4=g4, nb=nb, pe0=pe0: t.matmul(
                                      pe0[:, g4 * 128:(g4 + 1) * 128], lhs,
                                      C["f1c"][0:nb, :], start=True, stop=True))
                            last = ph.op("tensor", sP, [],
                                         lambda t, lhs=lhs, g4=g4, nb=nb, pe1=pe1: t.matmul(
                                             pe1[:, g4 * 128:(g4 + 1) * 128], lhs,
                                             C["f1s"][0:nb, :], start=True, stop=True))
                        ed0 = [last] + (bufree if gb == 0 else [])
                        ph.op("scalar", sA, ed0,
                              lambda sc, gb=gb, dst=dst, pe0=pe0: sc.activation(
                                  dst[0][:, gb * 512:(gb + 1) * 512], pe0[:], AF.Copy))
                        e2_ = ph.op("scalar", sA, [],
                                    lambda sc, gb=gb, dst=dst, pe1=pe1: sc.activation(
                                        dst[1][:, gb * 512:(gb + 1) * 512], pe1[:], AF.Copy))
                        evs.append(e2_)
                    return last, evs[-1]

                def cmul6(deps, ar, ai, br_c, bs_c, outr, outi, tmp):
                    w1 = ph.op("vector", sV, deps,
                               lambda v: v.tensor_tensor(g3(outr.ap()), g3(ar.ap()), br_c, ALU.mult))
                    w2 = ph.op("vector", sV, [w1],
                               lambda v: v.tensor_tensor(g3(outi.ap()), g3(ai.ap()), bs_c, ALU.mult))
                    w3 = ph.op("vector", sV, [w2],
                               lambda v: v.tensor_tensor(outr[:], outr[:], outi[:], ALU.subtract))
                    w4 = ph.op("vector", sV, [w3],
                               lambda v: v.tensor_tensor(g3(outi.ap()), g3(ar.ap()), bs_c, ALU.mult))
                    w5 = ph.op("vector", sV, [w4],
                               lambda v: v.tensor_tensor(g3(ar.ap()), g3(ai.ap()), br_c, ALU.mult))
                    w6 = ph.op("vector", sV, [w5],
                               lambda v: v.tensor_tensor(outi[:], outi[:], ar[:], ALU.add))
                    return w6, w5

                def stepC(src, dst, dep0):
                    evs = []
                    for gb in range(8):
                        pe0 = psE4[2 * (gb % 2)]
                        pe1 = psE4[2 * (gb % 2) + 1]
                        last = None
                        for g4 in range(4):
                            g = gb * 4 + g4
                            dep = []
                            if g4 == 0:
                                dep = list(dep0) if gb == 0 else [evs[gb - 2]] if gb >= 2 else []
                            sl_re = src[0][:, g * 128:(g + 1) * 128]
                            sl_im = src[1][:, g * 128:(g + 1) * 128]
                            ph.op("tensor", sP, dep,
                                  lambda t, sl_re=sl_re, g4=g4, pe0=pe0: t.matmul(
                                      pe0[:, g4 * 128:(g4 + 1) * 128], C["bdec"][:], sl_re,
                                      start=True, stop=False))
                            ph.op("tensor", sP, [],
                                  lambda t, sl_im=sl_im, g4=g4, pe0=pe0: t.matmul(
                                      pe0[:, g4 * 128:(g4 + 1) * 128], C["bdes"][:], sl_im,
                                      start=False, stop=True))
                            ph.op("tensor", sP, [],
                                  lambda t, sl_im=sl_im, g4=g4, pe1=pe1: t.matmul(
                                      pe1[:, g4 * 128:(g4 + 1) * 128], C["bdec"][:], sl_im,
                                      start=True, stop=False))
                            last = ph.op("tensor", sP, [],
                                         lambda t, sl_re=sl_re, g4=g4, pe1=pe1: t.matmul(
                                             pe1[:, g4 * 128:(g4 + 1) * 128], C["bdesn"][:], sl_re,
                                             start=False, stop=True))
                        ph.op("scalar", sA, [last],
                              lambda sc, gb=gb, dst=dst, pe0=pe0: sc.activation(
                                  dst[0][:, gb * 512:(gb + 1) * 512], pe0[:], AF.Copy))
                        e2_ = ph.op("scalar", sA, [],
                                    lambda sc, gb=gb, dst=dst, pe1=pe1: sc.activation(
                                        dst[1][:, gb * 512:(gb + 1) * 512], pe1[:], AF.Copy))
                        evs.append(e2_)
                    return last, evs[-1]

                def stepA(src, dst, dep0):
                    evs = []
                    for gb in range(8):
                        pe0 = psE4[2 * (gb % 2)]
                        pe1 = psE4[2 * (gb % 2) + 1]
                        last = None
                        for g4 in range(4):
                            g = gb * 4 + g4
                            dep = []
                            if g4 == 0:
                                dep = list(dep0) if gb == 0 else [evs[gb - 2]] if gb >= 2 else []
                            yre = src[0][:, g * 128:(g + 1) * 128]
                            yim = src[1][:, g * 128:(g + 1) * 128]
                            ph.op("tensor", sP, dep,
                                  lambda t, yre=yre, g4=g4, pe0=pe0: t.matmul(
                                      pe0[:, g4 * 128:(g4 + 1) * 128], yre, C["bdec"][:],
                                      start=True, stop=False))
                            ph.op("tensor", sP, [],
                                  lambda t, yim=yim, g4=g4, pe0=pe0: t.matmul(
                                      pe0[:, g4 * 128:(g4 + 1) * 128], yim, C["bdesn"][:],
                                      start=False, stop=True))
                            ph.op("tensor", sP, [],
                                  lambda t, yre=yre, g4=g4, pe1=pe1: t.matmul(
                                      pe1[:, g4 * 128:(g4 + 1) * 128], yre, C["bdes"][:],
                                      start=True, stop=False))
                            last = ph.op("tensor", sP, [],
                                         lambda t, yim=yim, g4=g4, pe1=pe1: t.matmul(
                                             pe1[:, g4 * 128:(g4 + 1) * 128], yim, C["bdec"][:],
                                             start=False, stop=True))
                        ph.op("scalar", sA, [last],
                              lambda sc, gb=gb, dst=dst, pe0=pe0: sc.activation(
                                  dst[0][:, gb * 512:(gb + 1) * 512], pe0[:], AF.Copy))
                        e2_ = ph.op("scalar", sA, [],
                                    lambda sc, gb=gb, dst=dst, pe1=pe1: sc.activation(
                                        dst[1][:, gb * 512:(gb + 1) * 512], pe1[:], AF.Copy))
                        evs.append(e2_)
                    return last, evs[-1]

                def emit_A_B(cp):
                    pr = cp % 2
                    stc = st_[cp]
                    la, ea = stepA(bB[pr], bA[pr], [stc["p6"], stc["psfree_for_A"]])
                    stc["stA_last"] = la
                    stc["stA_ev"] = ea
                    return la, ea

                def emit_B(cp):
                    pr = cp % 2
                    stc = st_[cp]
                    for hb in range(2):
                        lastY = None
                        dep0 = [stc["iv6"]]
                        pv = ydma_hist.get((cp, hb - 1), ydma_hist.get((cp - 1, 1)))
                        if pv:
                            dep0.append(pv)
                        for g16 in range(16):
                            g = hb * 16 + g16
                            ph.op("tensor", sP, dep0 if g16 == 0 else [],
                                  lambda t, g=g, g16=g16, pr=pr: t.matmul(
                                      psY4[:, g16 * 128:(g16 + 1) * 128], C["c2c"][:],
                                      bC[pr][0][:, g * 128:(g + 1) * 128], start=True, stop=False))
                            lastY = ph.op("tensor", sP, [],
                                          lambda t, g=g, g16=g16, pr=pr: t.matmul(
                                              psY4[:, g16 * 128:(g16 + 1) * 128], C["c2s"][:],
                                              bC[pr][1][:, g * 128:(g + 1) * 128], start=False, stop=True))
                        yed = [lastY]
                        pv2 = ydma_hist.get((cp - 2, 1))
                        if pv2:
                            yed.append(pv2)
                        ye = ph.op("scalar", sA, yed,
                                   lambda sc, hb=hb, pr=pr: sc.activation(
                                       ysb[pr][:, hb * 2048:(hb + 1) * 2048], psY4[:], AF.Copy))
                        ydma = ph.op("gpsimd", yss, [ye],
                                     lambda g_, cp=cp, hb=hb, pr=pr: g_.dma_start(
                                         bass.AP(y_dram.tensor, (cp * 64 + hb * 32) * NT,
                                                 [[64, 64], [2 * NT, 16], [NT, 2], [1, 64]]),
                                         bass.AP(ysb[pr], hb * 2048,
                                                 [[4096, 64], [128, 16], [64, 2], [1, 64]])), 16)
                        ydma_hist[(cp, hb)] = ydma

                for ct in range(16):
                    par = ct % 2
                    st_[ct] = {}
                    vdep = [st_[ct - 2]["s1v_last"]] if ct >= 2 else list(v_stores) + [xd_total]
                    vld = ph.op("gpsimd", vls[par], vdep,
                                lambda g, ct=ct, par=par: g.dma_start(
                                    vsl[par].ap().rearrange("b (d a) -> b d a", a=64),
                                    bass.AP(v_dram.tensor, ct * 64 * NT,
                                            [[64, 64], [NT, 64], [1, 64]])), 16)
                    tdep = [st_[ct - 2]["s1t_last"]] if ct >= 2 else list(t_stores) + [xd_total]
                    tld = ph.op("sync", tls[par], tdep,
                                lambda sy, ct=ct, par=par: sy.dma_start(
                                    tsl[par].ap().rearrange("b (d a) -> b d a", a=64),
                                    bass.AP(t_dram.tensor, ct * 64 * M,
                                            [[64, 128], [M, 64], [1, 64]])), 16)
                    psfree = [st_[ct - 1]["sCT_ev"]] if ct >= 1 else [const_ld]
                    bufree = [st_[ct - 2]["iv5"]] if ct >= 2 else []
                    l1, e1 = step1(vsl[par], 64, vld, bA[par], psfree, bufree)
                    st_[ct]["s1v_last"] = l1
                    twdeps = [e1]
                    if ct >= 2:
                        twdeps.append(st_[ct - 2]["stA_last"])
                    w6, _ = cmul6(twdeps, bA[par][0], bA[par][1], twcB, twsB,
                                  bB[par][0], bB[par][1], None)
                    lcv, xev = stepC(bB[par], bA[par], [w6])
                    st_[ct]["sCv_last"] = lcv
                    if ct >= 1:
                        st_[ct - 1]["psfree_for_A"] = xev
                        emit_A_B(ct - 1)
                        pr = (ct - 1) % 2
                        iv6, iv5 = cmul6([st_[ct - 1]["stA_ev"]],
                                         bA[pr][0], bA[pr][1], t2cB, t2sB,
                                         bC[pr][0], bC[pr][1], None)
                        st_[ct - 1]["iv6"] = iv6
                        st_[ct - 1]["iv5"] = iv5
                    psfreeT = [st_[ct - 1]["stA_ev"]] if ct >= 1 else [xev]
                    l2, e2 = step1(tsl[par], 128, tld, bC[par], psfreeT, [])
                    st_[ct]["s1t_last"] = l2
                    w6t, _ = cmul6([e2], bC[par][0], bC[par][1], twcB, twsB,
                                   bB[par][0], bB[par][1], None)
                    lct, aev = stepC(bB[par], bC[par], [w6t])
                    st_[ct]["sCT_last"] = lct
                    st_[ct]["sCT_ev"] = aev
                    if ct >= 1:
                        emit_B(ct - 1)
                    pwdeps = [xev, aev]
                    if ct >= 1:
                        pwdeps.append(st_[ct - 1]["stA_last"])
                    p1_ = ph.op("vector", sV, pwdeps,
                                lambda v, par=par: v.tensor_tensor(bB[par][0][:], bA[par][0][:],
                                                                   bC[par][0][:], ALU.mult))
                    p2_ = ph.op("vector", sV, [p1_],
                                lambda v, par=par: v.tensor_tensor(bB[par][1][:], bA[par][1][:],
                                                                   bC[par][1][:], ALU.mult))
                    p3_ = ph.op("vector", sV, [p2_],
                                lambda v, par=par: v.tensor_tensor(bB[par][0][:], bB[par][0][:],
                                                                   bB[par][1][:], ALU.subtract))
                    p4_ = ph.op("vector", sV, [p3_],
                                lambda v, par=par: v.tensor_tensor(bB[par][1][:], bA[par][0][:],
                                                                   bC[par][1][:], ALU.mult))
                    p5_ = ph.op("vector", sV, [p4_],
                                lambda v, par=par: v.tensor_tensor(bA[par][0][:], bA[par][1][:],
                                                                   bC[par][0][:], ALU.mult))
                    p6_ = ph.op("vector", sV, [p5_],
                                lambda v, par=par: v.tensor_tensor(bB[par][1][:], bB[par][1][:],
                                                                   bA[par][0][:], ALU.add))
                    st_[ct]["p6"] = p6_
                st_[15]["psfree_for_A"] = st_[15]["sCT_ev"]
                emit_A_B(15)
                iv6, iv5 = cmul6([st_[15]["stA_ev"]], bA[1][0], bA[1][1], t2cB, t2sB,
                                 bC[1][0], bC[1][1], None)
                st_[15]["iv6"] = iv6
                st_[15]["iv5"] = iv5
                emit_B(15)
                y_stores = yss.tok()
                ph.run("p4")

        with contextlib.ExitStack() as pp:
            ps_s5 = [pp.enter_context(nc.psum_tensor(f"ps_s5{i}", [128, 512], F32)) for i in range(2)]
            ps_t5 = [pp.enter_context(nc.psum_tensor(f"ps_t5{i}", [128, 128], F32)) for i in range(2)]
            with contextlib.ExitStack() as sb5:
                o_sb = sb(sb5, "o_sb", [128, 8 * D], BF16)
                ysl = [sb(sb5, f"ysl{i}", [128, 512], F32) for i in range(2)]
                usl = [sb(sb5, f"usl{i}", [128, 512], F32) for i in range(2)]
                prod = sb(sb5, "prod", [128, 8 * 512], BF16)
                sev = [sb(sb5, f"sev{i}", [128, 512], F32) for i in range(2)]
                stt5 = [sb(sb5, f"stt5{i}", [128, 128], F32) for i in range(2)]
                ph = Phase(nc)
                ows = dsem("ow")
                yus = [dsem("yu0"), dsem("yu1")]
                sss = [dsem("ss0"), dsem("ss1")]
                owl = ph.op("gpsimd", ows, [],
                            lambda g: g.dma_start(o_sb[:], o_w.rearrange("(cc p) d -> p cc d", p=128)), 16)
                sd_list = []
                em_by_slot = {}
                smm_last = None
                prev_pr = None
                for mc in range(8):
                    pr = {}
                    for cb in range(8):
                        ldd = []
                        if mc == 0 and cb == 0:
                            ldd = [y_stores] + u_stores
                        if cb >= 2:
                            ldd.append(pr[cb - 2])
                        elif mc >= 1:
                            ldd.append(prev_pr[6 + cb])
                        ph.op("sync", yus[cb % 2], ldd,
                              lambda sy, mc=mc, cb=cb: sy.dma_start(
                                  ysl[cb % 2][:],
                                  y_dram[cb * 128:(cb + 1) * 128, mc * 512:(mc + 1) * 512]), 16)
                        ul = ph.op("sync", yus[cb % 2], [],
                                   lambda sy, mc=mc, cb=cb: sy.dma_start(
                                       usl[cb % 2][:],
                                       u_dram[cb * 128:(cb + 1) * 128, mc * 512:(mc + 1) * 512]), 16)
                        pdep = [ul]
                        if mc >= 1 and cb == 0:
                            pdep.append(smm_last)
                        pr[cb] = ph.op("vector", sV, pdep,
                                       lambda v, cb=cb: v.tensor_tensor(
                                           prod[:, cb * 512:(cb + 1) * 512], ysl[cb % 2][:],
                                           usl[cb % 2][:], ALU.mult))
                    prev_pr = pr
                    for db in range(8):
                        last = None
                        mdep = [pr[7], owl]
                        if em_by_slot.get(db % 2):
                            mdep.append(em_by_slot[db % 2])
                        for cb in range(8):
                            last = ph.op("tensor", sP, mdep if cb == 0 else [],
                                         lambda t, db=db, cb=cb: t.matmul(
                                             ps_s5[db % 2][:],
                                             o_sb[:, cb * D + db * 128: cb * D + (db + 1) * 128],
                                             prod[:, cb * 512:(cb + 1) * 512],
                                             start=(cb == 0), stop=(cb == 7)))
                        smm_last = last
                        em = ph.op("scalar", sA, [last],
                                   lambda sc, db=db: sc.activation(sev[db % 2][:], ps_s5[db % 2][:], AF.Copy))
                        st_ev = {}
                        for sub in range(4):
                            td = [em] if sub == 0 else []
                            if sub >= 2:
                                td.append(st_ev[sub - 2])
                            pm = ph.op("tensor", sP, td,
                                       lambda t, db=db, sub=sub: t.transpose(
                                           ps_t5[sub % 2][:], sev[db % 2][:, sub * 128:(sub + 1) * 128],
                                           C["identf"][:]))
                            ed = [pm]
                            if len(sd_list) >= 2:
                                ed.append(sd_list[-2])
                            e2 = ph.op("scalar", sA, ed,
                                       lambda sc, sub=sub: sc.activation(stt5[sub % 2][:],
                                                                         ps_t5[sub % 2][:], AF.Copy))
                            st_ev[sub] = e2
                            sd = ph.op("sync", sss[sub % 2], [e2],
                                       lambda sy, mc=mc, db=db, sub=sub: sy.dma_start(
                                           s_dram[mc * 512 + sub * 128: mc * 512 + (sub + 1) * 128,
                                                  db * 128:(db + 1) * 128], stt5[sub % 2][:]), 16)
                            sd_list.append(sd)
                        em_by_slot[db % 2] = em
                s_stores = [sss[0].tok(), sss[1].tok()]
                ph.run("p5")

        with contextlib.ExitStack() as pp:
            ps_a6 = [pp.enter_context(nc.psum_tensor(f"ps_a6{i}", [128, 256], F32)) for i in range(2)]
            ps_b6 = [pp.enter_context(nc.psum_tensor(f"ps_b6{i}", [128, 512], F32)) for i in range(4)]
            ps_b1t = pp.enter_context(nc.psum_tensor("ps_b1t", [128, 32], F32))
            with contextlib.ExitStack() as sb6:
                w1sb = sb(sb6, "w1sb", [128, 8 * FF], BF16)
                w2sb = sb(sb6, "w2sb", [128, 32 * 1024], BF16)
                stage = [sb(sb6, f"stage{i}", [128, FF], F32) for i in range(2)]
                xtb = [sb(sb6, f"xtb{i}", [128, 2048], BF16) for i in range(2)]
                hcur = [sb(sb6, f"hcur{i}", [128, 256], BF16) for i in range(2)]
                b1T = sb(sb6, "b1T", [32, 128], F32)
                b1sb = sb(sb6, "b1sb", [128, 32], F32)
                mev = sb(sb6, "mev", [128, 2048], F32)
                gixf = sb(sb6, "gixf", [128, 48], F32)
                gixi = sb(sb6, "gixi", [128, 2 * 48], mybir.dt.int32)
                ph = Phase(nc)
                gst = [dsem("gst0"), dsem("gst1")]
                b1s = dsem("b1s")
                xtbs = [dsem("xtb0"), dsem("xtb1")]
                mas = dsem("mas")
                gix_done = None
                for j in range(2):
                    q1 = ph.op("vector", sV, [gix_done] if gix_done else [const_ld] + list(s_stores),
                               lambda v, j=j: v.tensor_scalar(
                                   gixf[:], C["gscale"][:], ebc[:, j:j + 1], None, ALU.mult))
                    q2 = ph.op("vector", sV, [q1],
                               lambda v: v.tensor_tensor(gixf[:], gixf[:], C["gidx_base"][:], ALU.add))
                    gix_done = ph.op("vector", sV, [q2],
                                     lambda v, j=j: v.tensor_copy(gixi[:, j * 48:(j + 1) * 48], gixf[:]))
                W1v = W1.rearrange("e d f -> (e d) f")
                W2v = W2.rearrange("e f d -> (e f) d")
                b1v = b1.rearrange("e (fc p) -> (e fc) p", p=128)
                prev_expert_pe = None
                prev_bev = None
                moe_hist = {}
                mev_done = None
                for e in range(2):
                    cast_last = None
                    gmark = {}
                    for c in range(40):
                        si = c % 2
                        gdeps = [gix_done] if c == 0 else []
                        if prev_expert_pe is not None and c <= 1:
                            gdeps.append(prev_expert_pe)
                        if c >= 2:
                            gdeps.append(gmark[c - 2])
                        if c < 8:
                            gl = ph.op("gpsimd", gst[si], gdeps,
                                       lambda g, e=e, c=c, si=si: g.indirect_dma_start(
                                           stage[si][:], None, W1v,
                                           bass.IndirectOffsetOnAxis(gixi[:, e * 48 + c: e * 48 + c + 1], 0)), 16)
                            cm = ph.op("vector" if c % 4 < 2 else "scalar",
                                       sV if c % 4 < 2 else sA, [gl],
                                       (lambda v, c=c, si=si: v.tensor_copy(
                                           w1sb[:, c * FF:(c + 1) * FF], stage[si][:]))
                                       if c % 4 < 2 else
                                       (lambda sc, c=c, si=si: sc.activation(
                                           w1sb[:, c * FF:(c + 1) * FF], stage[si][:], AF.Copy)))
                        else:
                            fc = c - 8
                            gl = ph.op("gpsimd", gst[si], gdeps,
                                       lambda g, e=e, fc=fc, si=si: g.indirect_dma_start(
                                           stage[si][:, 0:1024], None, W2v,
                                           bass.IndirectOffsetOnAxis(gixi[:, e * 48 + 8 + fc: e * 48 + 9 + fc], 0)), 16)
                            cm = ph.op("vector" if c % 4 < 2 else "scalar",
                                       sV if c % 4 < 2 else sA, [gl],
                                       (lambda v, fc=fc, si=si: v.tensor_copy(
                                           w2sb[:, fc * 1024:(fc + 1) * 1024], stage[si][:, 0:1024]))
                                       if c % 4 < 2 else
                                       (lambda sc, fc=fc, si=si: sc.activation(
                                           w2sb[:, fc * 1024:(fc + 1) * 1024], stage[si][:, 0:1024], AF.Copy)))
                        gmark[c] = cm
                        if c % 4 < 2:
                            cast_dve_last = cm
                        else:
                            cast_act_last = cm
                    wl1 = cast_dve_last
                    wl2 = cast_act_last
                    bl = ph.op("gpsimd", b1s, [gix_done] + ([prev_bev] if prev_bev else []),
                               lambda g, e=e: g.indirect_dma_start(
                                   b1T[:], None, b1v,
                                   bass.IndirectOffsetOnAxis(gixi[0:32, e * 48 + 40: e * 48 + 41], 0)), 16)
                    bdeps = [bl]
                    if prev_bev is not None:
                        bdeps.append(prev_bev)
                    btr = ph.op("tensor", sP, bdeps,
                                lambda t: t.transpose(ps_b1t[:], b1T[:], C["identf"][0:32, 0:32]))
                    bev = ph.op("vector", sV, [btr],
                                lambda v: v.tensor_copy(b1sb[:], ps_b1t[:]))
                    prev_bev = bev
                    last_b_mm = None
                    lastA_by_tb = {}
                    for TB in range(16):
                        xdeps = [xd_total] + list(s_stores) if (e == 0 and TB == 0) else []
                        if TB >= 2:
                            xdeps.append(lastA_by_tb[TB - 2])
                        elif e > 0:
                            xdeps.append(prev_expert_pe)
                        xld = ph.op("sync", xtbs[TB % 2], xdeps,
                                    lambda sy, TB=TB: sy.dma_start(
                                        xtb[TB % 2].ap().rearrange("p (dc t) -> p dc t", t=256),
                                        xnT_dram.rearrange("p (dc m) -> p dc m", m=NT)[:, :,
                                                                                       TB * 256:(TB + 1) * 256]), 16)
                        gel = {}

                        def emit_B(fc, TB=TB):
                            bdeps2 = [gel[fc]]
                            if fc == 0:
                                bdeps2.append(wl2)
                                if mev_done is not None:
                                    bdeps2.append(mev_done)
                            lastB = None
                            for tc2 in range(2):
                                for dh in range(2):
                                    j = tc2 * 2 + dh
                                    lastB = ph.op("tensor", sP, bdeps2 if j == 0 else [],
                                                  lambda t, fc=fc, tc2=tc2, dh=dh, j=j: t.matmul(
                                                      ps_b6[j][:],
                                                      hcur[fc % 2][:, tc2 * 128:(tc2 + 1) * 128],
                                                      w2sb[:, fc * 1024 + dh * 512: fc * 1024 + (dh + 1) * 512],
                                                      start=(fc == 0), stop=(fc == 31)))
                            return lastB

                        for fc in range(32):
                            adeps = []
                            if fc == 0:
                                adeps = [wl1, wl2, bev, xld]
                            if fc >= 2:
                                adeps.append(gel[fc - 2])
                            lastA = None
                            for dc in range(8):
                                lastA = ph.op("tensor", sP, adeps if dc == 0 else [],
                                              lambda t, TB=TB, fc=fc, dc=dc: t.matmul(
                                                  ps_a6[fc % 2][:],
                                                  w1sb[:, dc * FF + fc * 128: dc * FF + (fc + 1) * 128],
                                                  xtb[TB % 2][:, dc * 256:(dc + 1) * 256],
                                                  start=(dc == 0), stop=(dc == 7)))
                            lastA_by_tb[TB] = lastA
                            if fc >= 1:
                                emit_B(fc - 1)
                            gel[fc] = ph.op("scalar", sA, [lastA],
                                            lambda sc, fc=fc: sc.activation(
                                                hcur[fc % 2][:], ps_a6[fc % 2][:], AF.Gelu,
                                                bias=b1sb[:, fc:fc + 1]))
                        last_b_mm = emit_B(31)
                        for j in range(4):
                            mdep = [last_b_mm]
                            pv = moe_hist.get((e, TB - 1), moe_hist.get((e - 1, 15)))
                            if pv:
                                mdep.append(pv)
                            mev_done = ph.op("vector", sV, mdep,
                                             lambda v, j=j, e=e: v.tensor_scalar(
                                                 mev[:, j * 512:(j + 1) * 512], ps_b6[j][:],
                                                 cwslbc[:, e:e + 1], None, ALU.mult))
                        md = ph.op("gpsimd", mas, [mev_done],
                                   lambda g, e=e, TB=TB: g.dma_start(
                                       bass.AP(moe_dram.tensor, TB * 256 * D,
                                               [[D, 128], [128 * D, 2], [512, 2], [1, 512]]),
                                       mev.ap().rearrange("p (tc dh c) -> p tc dh c", dh=2, c=512),
                                       accum_op=(ALU.add if e > 0 else ALU.bypass)), 16)
                        moe_hist[(e, TB)] = md
                    prev_expert_pe = last_b_mm
                moe_done = mas.tok()
                ph.run("p6")

        with contextlib.ExitStack() as pp:
            ps_bc7 = pp.enter_context(nc.psum_tensor("ps_bc7", [128, 512], F32))
            with contextlib.ExitStack() as sb7:
                gbc = sb(sb7, "gbc", [128, D], F32)
                bbc = sb(sb7, "bbc", [128, D], F32)
                b2bc = sb(sb7, "b2bc", [128, D], F32)
                ytile = [sb(sb7, f"ytile{i}", [128, D], F32) for i in range(2)]
                snt = [sb(sb7, f"snt{i}", [128, D], F32) for i in range(2)]
                xt7 = [sb(sb7, f"xt7{i}", [128, D], F32) for i in range(2)]
                gt = [sb(sb7, f"gt{i}", [128, D], F32) for i in range(2)]
                ot = [sb(sb7, f"ot{i}", [128, D], F32) for i in range(2)]
                st7 = [sb(sb7, f"st7{i}", [128, 8], F32) for i in range(2)]
                grow = sb(sb7, "grow", [1, D], F32)
                brow = sb(sb7, "brow", [1, D], F32)
                b2r = sb(sb7, "b2r", [1, D], F32)
                ph = Phase(nc)
                p7s = dsem("p7s")
                yts = [dsem("yt70"), dsem("yt71")]
                sns = [dsem("sn0"), dsem("sn1")]
                x7s = [dsem("x70"), dsem("x71")]
                oss = [dsem("os0"), dsem("os1")]
                ph.op("sync", p7s, [], lambda sy: sy.dma_start(grow[:], gamma), 16)
                ph.op("sync", p7s, [], lambda sy: sy.dma_start(brow[:], beta), 16)
                rows_ld = ph.op("sync", p7s, [b2mix_store],
                                lambda sy: sy.dma_start(b2r[:], b2mix_dram), 16)
                bcv = None
                for row, dstt in [(grow, gbc), (brow, bbc), (b2r, b2bc)]:
                    for half in range(2):
                        bdeps = [rows_ld]
                        if bcv is not None:
                            bdeps.append(bcv)
                        bm = ph.op("tensor", sP, bdeps,
                                   lambda t, row=row, half=half: t.matmul(
                                       ps_bc7[:], C["onesrow"][:], row[0:1, half * 512:(half + 1) * 512],
                                       start=True, stop=True))
                        bcv = ph.op("vector", sV, [bm],
                                    lambda v, dstt=dstt, half=half: v.tensor_copy(
                                        dstt[:, half * 512:(half + 1) * 512], ps_bc7[:]))
                out_hist = {}
                r12_hist = {}
                r13_hist = {}
                a3_hist = {}

                def tile_chain(t_):
                    bi = t_ % 2
                    ldd = []
                    if t_ <= 1:
                        ldd.append(moe_done)
                    if t_ >= 2:
                        ldd.append(a3_hist[t_ - 2])
                    mld = ph.op("sync", yts[bi], ldd,
                                lambda sy, t_=t_, bi=bi: sy.dma_start(
                                    ytile[bi][:], moe_dram[t_ * 128:(t_ + 1) * 128, :]), 16)
                    yield
                    xld = []
                    if t_ <= 1:
                        xld = xld + xn_stores_l
                    if t_ >= 2:
                        xld.append(r12_hist[t_ - 2])
                    xl = ph.op("sync", sns[bi], xld,
                               lambda sy, t_=t_, bi=bi: sy.dma_start(
                                   snt[bi][:], xn_dram[t_ * 128:(t_ + 1) * 128, :]), 16)
                    yield
                    r1 = ph.op("vector", sV, [mld, xl, bcv],
                               lambda v, bi=bi: v.tensor_tensor(ytile[bi][:], ytile[bi][:], snt[bi][:], ALU.add))
                    yield
                    r2 = ph.op("vector", sV, [r1],
                               lambda v, bi=bi: v.tensor_tensor(ytile[bi][:], ytile[bi][:], b2bc[:], ALU.add))
                    yield
                    r3_ = ph.op("vector", sV, [r2],
                                lambda v, bi=bi: v.tensor_reduce(st7[bi][:, 0:1], ytile[bi][:], AX.X, ALU.add))
                    a1d = [r2]
                    if t_ >= 2:
                        a1d.append(r12_hist[t_ - 2])
                    a1 = ph.op("scalar", sA, a1d,
                               lambda sc, bi=bi: sc.activation(gt[bi][:], ytile[bi][:], AF.Square,
                                                               accum_out=st7[bi][:, 1:2]))
                    yield
                    r4 = ph.op("vector", sV, [r3_],
                               lambda v, bi=bi: v.tensor_scalar_mul(st7[bi][:, 2:3], st7[bi][:, 0:1], 1.0 / D))
                    yield
                    r5 = ph.op("vector", sV, [a1, r4],
                               lambda v, bi=bi: v.tensor_scalar(st7[bi][:, 3:4], st7[bi][:, 1:2], 1.0 / D, EPS,
                                                                ALU.mult, ALU.add))
                    yield
                    r6 = ph.op("vector", sV, [r4],
                               lambda v, bi=bi: v.tensor_tensor(st7[bi][:, 4:5], st7[bi][:, 2:3],
                                                                st7[bi][:, 2:3], ALU.mult))
                    yield
                    r7 = ph.op("vector", sV, [r6, r5],
                               lambda v, bi=bi: v.tensor_tensor(st7[bi][:, 3:4], st7[bi][:, 3:4],
                                                                st7[bi][:, 4:5], ALU.subtract))
                    yield
                    a2 = ph.op("scalar", sA, [r7],
                               lambda sc, bi=bi: sc.activation(st7[bi][:, 5:6], st7[bi][:, 3:4], AF.Sqrt))
                    yield
                    r8 = ph.op("vector", sV, [a2],
                               lambda v, bi=bi: v.reciprocal(st7[bi][:, 6:7], st7[bi][:, 5:6]))
                    yield
                    r9 = ph.op("vector", sV, [r8],
                               lambda v, bi=bi: v.tensor_scalar(ytile[bi][:], ytile[bi][:], st7[bi][:, 2:3],
                                                                st7[bi][:, 6:7], ALU.subtract, ALU.mult))
                    yield
                    r10 = ph.op("vector", sV, [r9],
                                lambda v, bi=bi: v.tensor_tensor(ytile[bi][:], ytile[bi][:], gbc[:], ALU.mult))
                    yield
                    r11 = ph.op("vector", sV, [r10],
                                lambda v, bi=bi: v.tensor_tensor(ytile[bi][:], ytile[bi][:], bbc[:], ALU.add))
                    yield
                    a3 = ph.op("scalar", sA, [r11],
                               lambda sc, bi=bi: sc.activation(gt[bi][:], ytile[bi][:], AF.Silu))
                    a3_hist[t_] = a3
                    yield
                    sld = [r1]
                    if t_ <= 1:
                        sld = sld + s_stores
                    sl = ph.op("sync", sns[bi], sld,
                               lambda sy, t_=t_, bi=bi: sy.dma_start(
                                   snt[bi][:], s_dram[t_ * 128:(t_ + 1) * 128, :]), 16)
                    x2d = []
                    if t_ >= 2:
                        x2d.append(r13_hist[t_ - 2])
                    xl2 = ph.op("gpsimd", x7s[bi], x2d,
                                lambda g, t_=t_, bi=bi: g.dma_start(
                                    xt7[bi][:], x[t_ * 128:(t_ + 1) * 128, :]), 16)
                    yield
                    odep = [a3, sl]
                    if t_ >= 2:
                        odep.append(out_hist[t_ - 2])
                    r12 = ph.op("vector", sV, odep,
                                lambda v, bi=bi: v.tensor_tensor(ot[bi][:], gt[bi][:], snt[bi][:], ALU.mult))
                    r12_hist[t_] = r12
                    yield
                    r13 = ph.op("vector", sV, [r12, xl2],
                                lambda v, bi=bi: v.tensor_tensor(ot[bi][:], ot[bi][:], xt7[bi][:], ALU.add))
                    r13_hist[t_] = r13
                    om = ph.op("sync", oss[bi], [r13],
                               lambda sy, t_=t_, bi=bi: sy.dma_start(
                                   out[t_ * 128:(t_ + 1) * 128, :], ot[bi][:]), 16)
                    out_hist[t_] = om

                for pr in range(16):
                    ga = tile_chain(2 * pr)
                    gb = tile_chain(2 * pr + 1)
                    alive = [ga, gb]
                    while alive:
                        nxt = []
                        for g_ in alive:
                            try:
                                next(g_)
                                nxt.append(g_)
                            except StopIteration:
                                pass
                        alive = nxt
                ph.run("p7")

        all_dma_sems = list(S.values())
        with nc.Block("fin") as blk:
            @blk.gpsimd
            def _(g):
                for s_ in all_dma_sems:
                    if s_.n > 0:
                        g.wait_ge(s_.h, s_.n)
    return nc


_NC_CACHE = {}


def kernel(**inputs):
    ins = {k: np.ascontiguousarray(np.asarray(v, dtype=np.float32)) for k, v in inputs.items()}
    if "main" not in _NC_CACHE:
        _NC_CACHE["main"] = build_nc(debug=False)
    nc = _NC_CACHE["main"]
    base = dict(
        uv_w=ins["L0_attn_uv_w"], o_w=ins["L0_attn_o_w"],
        rpe_ln_w=ins["L0_attn_rpe_ln_w"], rpe_ln_b=ins["L0_attn_rpe_ln_b"],
        rpe_fc_w=ins["L0_attn_rpe_fc_w"],
        rpe_oln_w=ins["L0_attn_rpe_out_ln_w"].reshape(1, 32),
        rpe_oln_b=ins["L0_attn_rpe_out_ln_b"].reshape(1, 32),
        rpe_pos_w=ins["L0_attn_rpe_pos_w"],
        rpe_out_w=ins["L0_attn_rpe_out_w"],
        W1=ins["L0_ffn_W1"], b1=ins["L0_ffn_b1"], W2=ins["L0_ffn_W2"], b2=ins["L0_ffn_b2"],
        Wr=ins["L0_ffn_Wr"], br=ins["L0_ffn_br"].reshape(1, NE),
        gamma=ins["L0_ffn_gamma"].reshape(1, D), beta=ins["L0_ffn_beta"].reshape(1, D),
    )
    xfull = ins["L0_attn_x"]
    in_maps = [dict(base, x=np.ascontiguousarray(xfull[i])) for i in range(8)]
    res = run_bass_kernel_spmd(nc, in_maps, core_ids=list(range(8)))
    return np.stack([res.results[i]["out"] for i in range(8)], axis=0)


# version 15
# speedup vs baseline: 1.0850x; 1.0850x over previous
import contextlib
import numpy as np
import ml_dtypes
import concourse.bass as bass
import concourse.mybir as mybir
from concourse.bass_utils import run_bass_kernel_spmd

F32 = mybir.dt.float32
BF16 = mybir.dt.bfloat16
AF = mybir.ActivationFunctionType
ALU = mybir.AluOpType
AX = mybir.AxisListType
BF = ml_dtypes.bfloat16

NT = 4096
D = 1024
FF = 4096
NE = 8
M = 8192
EPS = 1e-5
GAMMA = 0.99
ENGS = ("sync", "gpsimd", "tensor", "vector", "scalar")


def _consts():
    t = {}
    a = np.arange(64); b = np.arange(128); k2 = np.arange(128); k1 = np.arange(64)
    m2 = np.arange(64)
    tp = 2 * np.pi
    t["f1c"] = np.cos(tp * np.outer(b, k2) / 128).astype(BF)
    t["f1s"] = (-np.sin(tp * np.outer(b, k2) / 128)).astype(BF)
    ec = np.cos(tp * np.outer(a, k1) / 64).astype(np.float32)
    es = np.sin(tp * np.outer(a, k1) / 64).astype(np.float32)
    bd = lambda mtx: np.kron(np.eye(2, dtype=np.float32), mtx)
    t["bdec"] = bd(ec).astype(BF)
    t["bdes"] = bd(es).astype(BF)
    t["bdesn"] = bd(-es).astype(BF)
    t["c2c"] = np.cos(tp * np.outer(k2, m2) / 128).astype(BF)
    t["c2s"] = (-np.sin(tp * np.outer(k2, m2) / 128)).astype(BF)
    twc1 = np.cos(tp * np.outer(a, k2) / M).astype(np.float32)
    tws1 = (-np.sin(tp * np.outer(a, k2) / M)).astype(np.float32)
    t["twc"] = np.concatenate([twc1, twc1], 0).astype(BF)
    t["tws"] = np.concatenate([tws1, tws1], 0).astype(BF)
    tw2c = (np.cos(tp * np.outer(k2, a) / M) / M).astype(np.float32)
    tw2s = (np.sin(tp * np.outer(k2, a) / M) / M).astype(np.float32)
    t["tw2c"] = np.concatenate([tw2c, tw2c], 1).astype(BF)
    t["tw2s"] = np.concatenate([tw2s, tw2s], 1).astype(BF)
    t["identf"] = np.eye(128, dtype=np.float32)
    t["identb"] = np.eye(128, dtype=BF)
    mm_ = (np.arange(64)[None, :] * 128 + np.arange(128)[:, None]).astype(np.int64)
    pos = np.where(mm_ <= 4095, mm_, mm_ - M).astype(np.float32)
    pos[mm_ == 4096] = 0.0
    dec = (GAMMA ** np.abs(pos)).astype(np.float32)
    dec[mm_ == 4096] = 0.0
    t["posx"] = np.repeat(pos[:, :, None], 32, axis=2).reshape(128, 2048).astype(np.float32)
    t["decayx"] = np.repeat(dec[:, :, None], 32, axis=2).reshape(128, 2048).astype(np.float32)
    t["onesrow"] = np.ones((1, 128), np.float32)
    t["ones11"] = np.ones((1, 1), np.float32)
    t["onescol_b"] = np.ones((128, 1), BF)
    t["iota8"] = np.arange(8, dtype=np.float32).reshape(1, 8)
    gb_ = np.zeros((128, 48), np.float32)
    gs_ = np.zeros((128, 48), np.float32)
    p_ = np.arange(128, dtype=np.float32)
    for c in range(8):
        gb_[:, c] = p_ + c * 128
        gs_[:, c] = 1024.0
    for c in range(32):
        gb_[:, 8 + c] = p_ + c * 128
        gs_[:, 8 + c] = 4096.0
    gb_[0:32, 40] = np.arange(32)
    gs_[0:32, 40] = 32.0
    t["gidx_base"] = gb_
    t["gscale"] = gs_
    return t


class Sem:
    def __init__(self, stack, nc, name):
        self.h = stack.enter_context(nc.semaphore(name))
        self.n = 0
        self.name = name

    def bump(self, k=1):
        self.n += k
        return self.n

    def tok(self):
        return (self, self.n)


class Phase:

    def __init__(self, nc):
        self.nc = nc
        self.q = {e: [] for e in ENGS}

    def op(self, eng, sem, deps, fn, inc=1):
        waits = [(s.h, v) for t_ in deps if t_ is not None for (s, v) in [t_] if v > 0]
        tok = None
        h = None
        if sem is not None:
            tok = (sem, sem.bump(inc))
            h = sem.h

        def th(E, waits=waits, h=h, inc=inc, fn=fn):
            for hh, vv in waits:
                E.wait_ge(hh, vv)
            ins = fn(E)
            if h is not None and ins is not None:
                ins.then_inc(h, inc)
        self.q[eng].append(th)
        return tok

    def run(self, name):
        with self.nc.Block(name) as blk:
            for eng in ENGS:
                ops = self.q[eng]
                if ops:
                    getattr(blk, eng)(lambda E, ops=ops: [f(E) for f in ops])


def build_nc(debug=False):
    nc = bass.Bass("TRN2", target_bir_lowering=False)
    T = _consts()

    def din(name, shape):
        return nc.dram_tensor(name, shape, F32, kind="ExternalInput").ap()

    x = din("x", [NT, D])
    uv_w = din("uv_w", [D, 2 * D])
    o_w = din("o_w", [D, D])
    rpe_ln_w = din("rpe_ln_w", [3, 32])
    rpe_ln_b = din("rpe_ln_b", [3, 32])
    rpe_fc_w = din("rpe_fc_w", [3, 32, 32])
    rpe_oln_w = din("rpe_oln_w", [1, 32])
    rpe_oln_b = din("rpe_oln_b", [1, 32])
    rpe_pos_w = din("rpe_pos_w", [1, 32])
    rpe_out_w = din("rpe_out_w", [32, D])
    W1 = din("W1", [NE, D, FF])
    b1 = din("b1", [NE, FF])
    W2 = din("W2", [NE, FF, D])
    b2 = din("b2", [NE, D])
    Wr = din("Wr", [D, NE])
    br = din("br", [1, NE])
    gamma = din("gamma", [1, D])
    beta = din("beta", [1, D])
    out = nc.dram_tensor("out", [NT, D], F32, kind="ExternalOutput").ap()

    sk = "ExternalOutput" if debug else "Internal"
    xn_dram = nc.dram_tensor("xn_dram", [NT, D], F32, kind=sk).ap()
    u_dram = nc.dram_tensor("u_dram", [D, NT], F32, kind=sk).ap()
    v_dram = nc.dram_tensor("v_dram", [D, NT], BF16, kind=sk).ap()
    t_dram = nc.dram_tensor("t_dram", [D, M], BF16, kind=sk).ap()
    y_dram = nc.dram_tensor("y_dram", [D, NT], F32, kind=sk).ap()
    s_dram = nc.dram_tensor("s_dram", [NT, D], F32, kind=sk).ap()
    moe_dram = nc.dram_tensor("moe_dram", [NT, D], F32, kind=sk).ap()
    xnT_dram = nc.dram_tensor("xnT_dram", [128, 8 * NT], BF16, kind="Internal").ap()
    b2mix_dram = nc.dram_tensor("b2mix_dram", [1, D], F32, kind="Internal").ap()

    ci = {k: nc.inline_tensor(v, f"c_{k}").ap() for k, v in T.items()}

    st = contextlib.ExitStack()
    with st:
        def sb(stk, name, shape, dt):
            return stk.enter_context(nc.sbuf_tensor(name, shape, dt))

        xnT_stack = contextlib.ExitStack()
        xnT = xnT_stack.enter_context(nc.sbuf_tensor("xnT", [128, 8 * NT], BF16, side="right"))
        C = {}
        for k, shape, dt in [
            ("f1c", [128, 128], BF16), ("f1s", [128, 128], BF16),
            ("bdec", [128, 128], BF16), ("bdes", [128, 128], BF16),
            ("bdesn", [128, 128], BF16), ("c2c", [128, 64], BF16),
            ("c2s", [128, 64], BF16), ("twc", [128, 128], BF16),
            ("tws", [128, 128], BF16), ("tw2c", [128, 128], BF16),
            ("tw2s", [128, 128], BF16), ("identf", [128, 128], F32),
            ("identb", [128, 128], BF16), ("onesrow", [1, 128], F32),
            ("ones11", [1, 1], F32), ("onescol_b", [128, 1], BF16),
            ("iota8", [1, 8], F32), ("gidx_base", [128, 48], F32),
            ("gscale", [128, 48], F32),
        ]:
            C[k] = sb(st, "cc_" + k, shape, dt)
        ebc = sb(st, "ebc", [128, 2], F32)
        cwslbc = sb(st, "cwslbc", [128, 2], F32)

        sP = Sem(st, nc, "sP")
        sA = Sem(st, nc, "sA")
        sV = Sem(st, nc, "sV")
        sG = Sem(st, nc, "sG")
        S = {}

        def dsem(name):
            if name not in S:
                S[name] = Sem(st, nc, "d_" + name)
            return S[name]

        ph = Phase(nc)
        cst = dsem("cst")
        const_ld = None
        for k in C:
            const_ld = ph.op("sync", cst, [], lambda sy, k=k: sy.dma_start(C[k][:], ci[k]), 16)

        with contextlib.ExitStack() as pp:
            ps_xsum = pp.enter_context(nc.psum_tensor("ps_xsum", [1, 1024], F32))
            ps_tr = [pp.enter_context(nc.psum_tensor(f"ps_tr{i}", [128, 128], BF16))
                     for i in range(2)]
            with contextlib.ExitStack() as sb0:
                xt = [sb(sb0, f"xt{i}", [128, D], F32) for i in range(2)]
                xnb = [sb(sb0, f"xnb{i}", [128, D], BF16) for i in range(2)]
                xnf = [sb(sb0, f"xnf{i}", [128, D], F32) for i in range(2)]
                stt = [sb(sb0, f"stt{i}", [128, 8], F32) for i in range(2)]
                ph.run("p0a")
                ph = Phase(nc)
                xts = [dsem("xt0"), dsem("xt1")]
                xns = [dsem("xns0"), dsem("xns1")]
                tile_pe_done = {}
                tile_store = {}
                ev_hist = []
                r8_hist = {}

                def p0_chain(t_):
                    bi = t_ % 2
                    deps = [tile_pe_done.get(t_ - 2), r8_hist.get(t_ - 2)] if t_ >= 2 else []
                    ld = ph.op("sync", xts[bi], deps,
                               lambda sy, t_=t_, bi=bi: sy.dma_start(
                                   xt[bi][:], x[t_ * 128:(t_ + 1) * 128, :]), 16)
                    yield
                    r1 = ph.op("vector", sV, [ld],
                               lambda v, bi=bi: v.tensor_reduce(stt[bi][:, 0:1], xt[bi][:], AX.X, ALU.add))
                    sqdeps = [ld]
                    if t_ >= 2:
                        sqdeps.append(tile_store[t_ - 2])
                    a1 = ph.op("scalar", sA, sqdeps,
                               lambda sc, bi=bi: sc.activation(xnf[bi][:], xt[bi][:], AF.Square,
                                                               accum_out=stt[bi][:, 1:2]))
                    yield
                    r2 = ph.op("vector", sV, [r1],
                               lambda v, bi=bi: v.tensor_scalar_mul(stt[bi][:, 2:3], stt[bi][:, 0:1], 1.0 / D))
                    yield
                    r3 = ph.op("vector", sV, [a1, r2],
                               lambda v, bi=bi: v.tensor_scalar(stt[bi][:, 3:4], stt[bi][:, 1:2], 1.0 / D, EPS,
                                                                ALU.mult, ALU.add))
                    yield
                    r4 = ph.op("vector", sV, [r2],
                               lambda v, bi=bi: v.tensor_tensor(stt[bi][:, 4:5], stt[bi][:, 2:3],
                                                                stt[bi][:, 2:3], ALU.mult))
                    yield
                    r5 = ph.op("vector", sV, [r4, r3],
                               lambda v, bi=bi: v.tensor_tensor(stt[bi][:, 3:4], stt[bi][:, 3:4],
                                                                stt[bi][:, 4:5], ALU.subtract))
                    yield
                    a2 = ph.op("scalar", sA, [r5],
                               lambda sc, bi=bi: sc.activation(stt[bi][:, 5:6], stt[bi][:, 3:4], AF.Sqrt))
                    yield
                    r6 = ph.op("vector", sV, [a2],
                               lambda v, bi=bi: v.reciprocal(stt[bi][:, 6:7], stt[bi][:, 5:6]))
                    yield
                    r7 = ph.op("vector", sV, [r6],
                               lambda v, bi=bi: v.tensor_scalar(xnb[bi][:], xt[bi][:], stt[bi][:, 2:3],
                                                                stt[bi][:, 6:7], ALU.subtract, ALU.mult))
                    yield
                    r8 = ph.op("vector", sV, [r7],
                               lambda v, bi=bi: v.tensor_scalar(xnf[bi][:], xt[bi][:], stt[bi][:, 2:3],
                                                                stt[bi][:, 6:7], ALU.subtract, ALU.mult))
                    r8_hist[t_] = r8
                    tile_store[t_] = ph.op("scalar", xns[bi], [r8],
                                           lambda sc, t_=t_, bi=bi: sc.dma_start(
                                               xn_dram[t_ * 128:(t_ + 1) * 128, :], xnf[bi][:]), 16)
                    yield
                    for db in range(8):
                        pdeps = [r7]
                        if not ev_hist:
                            pdeps.append(const_ld)
                        if len(ev_hist) >= 2:
                            pdeps.append(ev_hist[-2])
                        pm = ph.op("tensor", sP, pdeps,
                                   lambda t, bi=bi, db=db: t.transpose(
                                       ps_tr[db % 2][:], xnb[bi][:, db * 128:(db + 1) * 128],
                                       C["identb"][:]))
                        ev_hist.append(ph.op("scalar", sA, [pm],
                                             lambda sc, t_=t_, db=db: sc.activation(
                                                 xnT[:, db * NT + t_ * 128: db * NT + (t_ + 1) * 128],
                                                 ps_tr[db % 2][:], AF.Copy)))
                    ph.op("tensor", sP, [r7, const_ld],
                          lambda t, t_=t_, bi=bi: t.matmul(
                              ps_xsum[0:1, 0:512], C["onescol_b"][:], xnb[bi][:, 0:512],
                              start=(t_ == 0), stop=(t_ == 31)))
                    m2 = ph.op("tensor", sP, [],
                               lambda t, t_=t_, bi=bi: t.matmul(
                                   ps_xsum[0:1, 512:1024], C["onescol_b"][:], xnb[bi][:, 512:1024],
                                   start=(t_ == 0), stop=(t_ == 31)))
                    tile_pe_done[t_] = m2

                prev = None
                for pr in range(16):
                    ga = p0_chain(2 * pr)
                    gb = p0_chain(2 * pr + 1)
                    alive = [ga, gb]
                    while alive:
                        nxt = []
                        for g_ in alive:
                            try:
                                next(g_)
                                nxt.append(g_)
                            except StopIteration:
                                pass
                        alive = nxt
                xsum_mm = tile_pe_done[31]
                xn_stores_l = [xns[0].tok(), xns[1].tok()]
                ph.run("p0b")

            with contextlib.ExitStack() as sb1:
                xsum_sb = sb(sb1, "xsum_sb", [1, 1024], F32)
                xmT = sb(sb1, "xmT", [128, 8], F32)
                wr_sb = sb(sb1, "wr_sb", [128, 64], F32)
                b2sb = sb(sb1, "b2sb", [8, 1024], F32)
                brr = sb(sb1, "brr", [1, 8], F32)
                rw = sb(sb1, "rw", [1, 8], F32)
                rt = sb(sb1, "rt", [1, 8], F32)
                eq1 = sb(sb1, "eq1", [1, 8], F32)
                eq2 = sb(sb1, "eq2", [1, 8], F32)
                s4 = sb(sb1, "s4", [1, 4], F32)
                cw8 = sb(sb1, "cw8", [1, 8], F32)
                cw8T = sb(sb1, "cw8T", [8, 1], F32)
                b2mix = sb(sb1, "b2mix", [1, D], F32)
                eidx = sb(sb1, "eidx", [1, 2], F32)
                cwrow = sb(sb1, "cwrow", [1, 2], F32)
                with contextlib.ExitStack() as pp1:
                    ps_m = pp1.enter_context(nc.psum_tensor("ps_m", [128, 512], F32))
                    ps_r = pp1.enter_context(nc.psum_tensor("ps_r", [1, 512], F32))
                    ps_c8 = pp1.enter_context(nc.psum_tensor("ps_c8", [8, 8], F32))
                    ph = Phase(nc)
                    p1s = dsem("p1s")
                    for apdst, apsrc in [(wr_sb[:], Wr.rearrange("(c p) e -> p c e", p=128)),
                                         (b2sb[:], b2), (brr[:], br)]:
                        p1_all = ph.op("sync", p1s, xn_stores_l,
                                       lambda sy, apdst=apdst, apsrc=apsrc: sy.dma_start(apdst, apsrc), 16)
                    e1 = ph.op("vector", sV, [xsum_mm] + xn_stores_l,
                               lambda v: v.tensor_copy(xsum_sb[:, 0:512], ps_xsum[0:1, 0:512]))
                    e2 = ph.op("vector", sV, [e1],
                               lambda v: v.tensor_copy(xsum_sb[:, 512:1024], ps_xsum[0:1, 512:1024]))
                    mt = None
                    for k in range(8):
                        mt = ph.op("tensor", sP, [e2, const_ld],
                                   lambda t, k=k: t.matmul(ps_m[:, k:k + 1],
                                                           xsum_sb[0:1, k * 128:(k + 1) * 128],
                                                           C["ones11"][:], start=True, stop=True))
                    v1 = ph.op("vector", sV, [mt],
                               lambda v: v.tensor_scalar_mul(xmT[:], ps_m[:, 0:8], 1.0 / NT))
                    rmm = None
                    for k in range(8):
                        rmm = ph.op("tensor", sP, [v1, p1_all],
                                    lambda t, k=k: t.matmul(ps_r[0:1, 0:8], xmT[:, k:k + 1],
                                                            wr_sb[:, k * 8:(k + 1) * 8],
                                                            start=(k == 0), stop=(k == 7)))
                    c1 = ph.op("vector", sV, [rmm],
                               lambda v: v.tensor_copy(rw[:], ps_r[0:1, 0:8]))
                    c2 = ph.op("vector", sV, [c1],
                               lambda v: v.tensor_tensor(rw[:], rw[:], brr[:], ALU.add))
                    c3 = ph.op("vector", sV, [c2],
                               lambda v: v.tensor_reduce(s4[:, 0:1], rw[:], AX.X, ALU.max))
                    c4 = ph.op("vector", sV, [c3],
                               lambda v: v.tensor_scalar(rt[:], rw[:], s4[:, 0:1], None, ALU.subtract))
                    c5 = ph.op("scalar", sA, [c4],
                               lambda sc: sc.activation(rt[:], rt[:], AF.Exp))
                    c6 = ph.op("vector", sV, [c5],
                               lambda v: v.tensor_reduce(s4[:, 1:2], rt[:], AX.X, ALU.add))
                    c7 = ph.op("vector", sV, [c6],
                               lambda v: v.reciprocal(s4[:, 2:3], s4[:, 1:2]))
                    c8 = ph.op("vector", sV, [c7],
                               lambda v: v.tensor_scalar(rw[:], rt[:], s4[:, 2:3], None, ALU.mult))
                    d1 = ph.op("vector", sV, [c8],
                               lambda v: v.tensor_reduce(s4[:, 0:1], rw[:], AX.X, ALU.max))
                    d2 = ph.op("vector", sV, [d1],
                               lambda v: v.tensor_scalar(eq1[:], rw[:], s4[:, 0:1], None, ALU.is_equal))
                    d3 = ph.op("vector", sV, [d2],
                               lambda v: v.tensor_scalar_mul(rt[:], eq1[:], -3.0))
                    d4 = ph.op("vector", sV, [d3],
                               lambda v: v.tensor_tensor(rt[:], rw[:], rt[:], ALU.add))
                    d5 = ph.op("vector", sV, [d4],
                               lambda v: v.tensor_reduce(s4[:, 1:2], rt[:], AX.X, ALU.max))
                    d6 = ph.op("vector", sV, [d5],
                               lambda v: v.tensor_scalar(eq2[:], rt[:], s4[:, 1:2], None, ALU.is_equal))
                    i1 = ph.op("vector", sV, [d6],
                               lambda v: v.tensor_tensor(rt[:], eq1[:], C["iota8"][:], ALU.mult))
                    i2 = ph.op("vector", sV, [i1],
                               lambda v: v.tensor_reduce(eidx[:, 0:1], rt[:], AX.X, ALU.add))
                    i3 = ph.op("vector", sV, [i2],
                               lambda v: v.tensor_tensor(rt[:], eq2[:], C["iota8"][:], ALU.mult))
                    i4 = ph.op("vector", sV, [i3],
                               lambda v: v.tensor_reduce(eidx[:, 1:2], rt[:], AX.X, ALU.add))
                    d7 = ph.op("vector", sV, [i4],
                               lambda v: v.tensor_tensor(s4[:, 2:3], s4[:, 1:2], s4[:, 0:1], ALU.subtract))
                    d8 = ph.op("scalar", sA, [d7],
                               lambda sc: sc.activation(s4[:, 2:3], s4[:, 2:3], AF.Exp))
                    d9 = ph.op("vector", sV, [d8],
                               lambda v: v.tensor_scalar(s4[:, 3:4], s4[:, 2:3], 1.0, None, ALU.add))
                    d10 = ph.op("vector", sV, [d9],
                                lambda v: v.reciprocal(s4[:, 3:4], s4[:, 3:4]))
                    d11 = ph.op("vector", sV, [d10],
                                lambda v: v.tensor_scalar(eq1[:], eq1[:], s4[:, 3:4], None, ALU.mult))
                    d12 = ph.op("vector", sV, [d11],
                                lambda v: v.tensor_scalar(s4[:, 2:3], s4[:, 3:4], -1.0, 1.0,
                                                          ALU.mult, ALU.add))
                    d13 = ph.op("vector", sV, [d12],
                                lambda v: v.tensor_scalar(eq2[:], eq2[:], s4[:, 2:3], None, ALU.mult))
                    d14 = ph.op("vector", sV, [d13],
                                lambda v: v.tensor_tensor(cw8[:], eq1[:], eq2[:], ALU.add))
                    d15 = ph.op("vector", sV, [d12],
                                lambda v: v.tensor_copy(cwrow[:, 0:1], s4[:, 3:4]))
                    d16 = ph.op("vector", sV, [d15],
                                lambda v: v.tensor_copy(cwrow[:, 1:2], s4[:, 2:3]))
                    g1 = ph.op("tensor", sP, [d16],
                               lambda t: t.matmul(ps_m[:, 0:2], C["onesrow"][:], cwrow[:],
                                                  start=True, stop=True))
                    g1b = ph.op("tensor", sP, [d16],
                                lambda t: t.matmul(ps_m[:, 8:10], C["onesrow"][:], eidx[:],
                                                   start=True, stop=True))
                    g2 = ph.op("tensor", sP, [d14],
                               lambda t: t.matmul(ps_c8[:, 0:1], cw8[:], C["ones11"][:],
                                                  start=True, stop=True))
                    g3 = ph.op("vector", sV, [g1b],
                               lambda v: v.tensor_copy(cwslbc[:], ps_m[:, 0:2]))
                    g3b = ph.op("vector", sV, [g3],
                                lambda v: v.tensor_copy(ebc[:], ps_m[:, 8:10]))
                    g4 = ph.op("vector", sV, [g3b, g2],
                               lambda v: v.tensor_copy(cw8T[:], ps_c8[:, 0:1]))
                    h1m = ph.op("tensor", sP, [g4],
                                lambda t: t.matmul(ps_r[0:1, 0:512], cw8T[:], b2sb[:, 0:512],
                                                   start=True, stop=True))
                    h1e = ph.op("vector", sV, [h1m],
                                lambda v: v.tensor_copy(b2mix[:, 0:512], ps_r[0:1, 0:512]))
                    h2m = ph.op("tensor", sP, [h1e],
                                lambda t: t.matmul(ps_r[0:1, 0:512], cw8T[:], b2sb[:, 512:1024],
                                                   start=True, stop=True))
                    h2e = ph.op("vector", sV, [h2m],
                                lambda v: v.tensor_copy(b2mix[:, 512:1024], ps_r[0:1, 0:512]))
                    b2s = dsem("b2s")
                    b2mix_store = ph.op("sync", b2s, [h2e],
                                        lambda sy: sy.dma_start(b2mix_dram, b2mix[:]), 16)
                    ph.run("p1")

        with contextlib.ExitStack() as pp:
            ps_uv = [pp.enter_context(nc.psum_tensor(f"ps_uv{i}", [128, 512], F32)) for i in range(2)]
            with contextlib.ExitStack() as sb2:
                wt = [sb(sb2, f"uvw{i}", [128, 8 * 128], BF16) for i in range(2)]
                ev = [sb(sb2, f"uvev{i}", [128, 512], F32) for i in range(2)]
                evb = [sb(sb2, f"uvevb{i}", [128, 512], BF16) for i in range(2)]
                ph = Phase(nc)
                wts = [dsem("wt0"), dsem("wt1")]
                evs_s = [dsem("evs0"), dsem("evs1")]
                evbs_s = [dsem("evbs0"), dsem("evbs1")]
                mm_last = {}
                ev_hist2 = {}
                st_hist = {}
                for fb in range(16):
                    wdeps = [mm_last.get(fb - 2)] if fb >= 2 else [b2mix_store]
                    wl = ph.op("gpsimd", wts[fb % 2], wdeps,
                               lambda g, fb=fb: g.dma_start(
                                   wt[fb % 2][:],
                                   uv_w[:, fb * 128:(fb + 1) * 128].rearrange("(c p) f -> p c f", p=128)), 16)
                    for tc in range(8):
                        idx = fb * 8 + tc
                        mdeps = [wl]
                        if idx >= 2:
                            mdeps.append(ev_hist2[idx - 2])
                        last = None
                        for dc in range(8):
                            last = ph.op("tensor", sP, mdeps if dc == 0 else [],
                                         lambda t, fb=fb, tc=tc, dc=dc: t.matmul(
                                             ps_uv[tc % 2][:], wt[fb % 2][:, dc * 128:(dc + 1) * 128],
                                             xnT[:, dc * NT + tc * 512: dc * NT + (tc + 1) * 512],
                                             start=(dc == 0), stop=(dc == 7)))
                        mm_last[fb] = last
                        edeps = [last]
                        if idx >= 2:
                            edeps.append(st_hist[idx - 2])
                        dst = ev if fb < 8 else evb
                        em = ph.op("scalar", sA, edeps,
                                   lambda sc, tc=tc, dst=dst: sc.activation(
                                       dst[tc % 2][:], ps_uv[tc % 2][:], AF.Silu))
                        if fb < 8:
                            stm = ph.op("sync", evs_s[tc % 2], [em],
                                        lambda sy, fb=fb, tc=tc: sy.dma_start(
                                            u_dram[fb * 128:(fb + 1) * 128, tc * 512:(tc + 1) * 512],
                                            ev[tc % 2][:]), 16)
                        else:
                            stm = ph.op("sync", evbs_s[tc % 2], [em],
                                        lambda sy, fb=fb, tc=tc: sy.dma_start(
                                            v_dram[(fb - 8) * 128:(fb - 7) * 128, tc * 512:(tc + 1) * 512],
                                            evb[tc % 2][:]), 16)
                        ev_hist2[idx] = em
                        st_hist[idx] = stm
                u_stores = [evs_s[0].tok(), evs_s[1].tok()]
                v_stores = [evbs_s[0].tok(), evbs_s[1].tok()]
                ph.run("p2")

        with contextlib.ExitStack() as pp:
            ps_t32 = [pp.enter_context(nc.psum_tensor(f"ps_t32_{i}", [32, 128], F32)) for i in range(2)]
            ps_h32 = [pp.enter_context(nc.psum_tensor(f"ps_h32_{i}", [128, 32], F32)) for i in range(2)]
            ps_tg = [pp.enter_context(nc.psum_tensor(f"ps_tg{i}", [128, 512], F32)) for i in range(2)]
            with contextlib.ExitStack() as sb3:
                posx = sb(sb3, "posx", [128, 2048], F32)
                decayx = sb(sb3, "decayx", [128, 2048], F32)
                h1 = sb(sb3, "h1", [128, 2048], F32)
                h2 = sb(sb3, "h2", [128, 2048], F32)
                hsq = sb(sb3, "hsq", [128, 2048], F32)
                st64 = sb(sb3, "st64", [128, 256], F32)
                hT = sb(sb3, "hT", [32, M], BF16)
                fcw = sb(sb3, "fcw", [32, 96], BF16)
                woutb = sb(sb3, "woutb", [32, D], BF16)
                lnrow = sb(sb3, "lnrow", [1, 9 * 32], F32)
                lnbc = sb(sb3, "lnbc", [128, 9 * 32], F32)
                tg = [sb(sb3, f"tg{i}", [128, 512], BF16) for i in range(2)]
                ph = Phase(nc)
                p3s = dsem("p3s")
                xds = dsem("xds")
                tgs = [dsem("tgs0"), dsem("tgs1")]
                xnT_dump = None
                for k in range(8):
                    xnT_dump = ph.op("sync", xds, [],
                                     lambda sy, k=k: sy.dma_start(
                                         xnT_dram[:, k * NT:(k + 1) * NT],
                                         xnT[:, k * NT:(k + 1) * NT]), 16)
                ph.op("sync", p3s, u_stores + v_stores,
                      lambda sy: sy.dma_start(posx[:], ci["posx"]), 16)
                ph.op("sync", p3s, [], lambda sy: sy.dma_start(decayx[:], ci["decayx"]), 16)
                p3g = dsem("p3g")
                for i in range(3):
                    ph.op("gpsimd", p3g, u_stores + v_stores if i == 0 else [],
                          lambda g, i=i: g.dma_start(fcw[:, i * 32:(i + 1) * 32],
                                                     rpe_fc_w[i]), 16)
                ph.op("gpsimd", p3g, [], lambda g: g.dma_start(woutb[:], rpe_out_w), 16)
                rows = [(0, rpe_ln_w, 0), (1, rpe_ln_w, 1), (2, rpe_ln_w, 2),
                        (3, rpe_ln_b, 0), (4, rpe_ln_b, 1), (5, rpe_ln_b, 2),
                        (6, rpe_oln_w, None), (7, rpe_oln_b, None), (8, rpe_pos_w, None)]
                p3_all = None
                for slot, src, ri in rows:
                    ap_src = src[ri:ri + 1, :] if ri is not None else src
                    p3_all = ph.op("sync", p3s, [], lambda sy, slot=slot, ap_src=ap_src: sy.dma_start(
                        lnrow[:, slot * 32:(slot + 1) * 32], ap_src), 16)
                bcd = None
                for slot in range(9):
                    bdeps = [p3_all]
                    if bcd is not None:
                        bdeps.append(bcd)
                    bm = ph.op("tensor", sP, bdeps,
                               lambda t, slot=slot: t.matmul(ps_tg[slot % 2][:, 0:32], C["onesrow"][:],
                                                             lnrow[0:1, slot * 32:(slot + 1) * 32],
                                                             start=True, stop=True))
                    bcd = ph.op("vector", sV, [bm],
                                lambda v, slot=slot: v.tensor_copy(lnbc[:, slot * 32:(slot + 1) * 32],
                                                                   ps_tg[slot % 2][:, 0:32]))

                def r3ap(t_ap):
                    return t_ap.rearrange("p (g j) -> p g j", j=32)

                def ln_silu(hin, wslot, bslot, dep0):
                    q1 = ph.op("vector", sV, [dep0, bcd],
                               lambda v: v.tensor_reduce(st64[:, 0:64], r3ap(hin.ap()), AX.X, ALU.add))
                    q2 = ph.op("vector", sV, [q1],
                               lambda v: v.tensor_scalar_mul(st64[:, 0:64], st64[:, 0:64], 1.0 / 32))
                    q3 = ph.op("vector", sV, [q2],
                               lambda v: v.tensor_tensor(
                                   r3ap(hin.ap()), r3ap(hin.ap()),
                                   st64[:, 0:64].unsqueeze(2).broadcast_to([128, 64, 32]), ALU.subtract))
                    q4 = ph.op("scalar", sA, [q3],
                               lambda sc: sc.activation(hsq[:], hin[:], AF.Square))
                    q5 = ph.op("vector", sV, [q4],
                               lambda v: v.tensor_reduce(st64[:, 64:128], r3ap(hsq.ap()), AX.X, ALU.add))
                    q6 = ph.op("vector", sV, [q5],
                               lambda v: v.tensor_scalar(st64[:, 64:128], st64[:, 64:128], 1.0 / 32,
                                                         EPS, ALU.mult, ALU.add))
                    q7 = ph.op("scalar", sA, [q6],
                               lambda sc: sc.activation(st64[:, 128:192], st64[:, 64:128], AF.Sqrt))
                    q8 = ph.op("vector", sV, [q7],
                               lambda v: v.reciprocal(st64[:, 192:256], st64[:, 128:192]))
                    q9 = ph.op("vector", sV, [q8],
                               lambda v: v.tensor_tensor(
                                   r3ap(hin.ap()), r3ap(hin.ap()),
                                   st64[:, 192:256].unsqueeze(2).broadcast_to([128, 64, 32]), ALU.mult))
                    q10 = ph.op("vector", sV, [q9],
                                lambda v: v.tensor_tensor(
                                    r3ap(hin.ap()), r3ap(hin.ap()),
                                    lnbc[:, wslot * 32:(wslot + 1) * 32].unsqueeze(1).broadcast_to([128, 64, 32]),
                                    ALU.mult))
                    q11 = ph.op("vector", sV, [q10],
                                lambda v: v.tensor_tensor(
                                    r3ap(hin.ap()), r3ap(hin.ap()),
                                    lnbc[:, bslot * 32:(bslot + 1) * 32].unsqueeze(1).broadcast_to([128, 64, 32]),
                                    ALU.add))
                    q12 = ph.op("scalar", sA, [q11],
                                lambda sc: sc.activation(hin[:], hin[:], AF.Silu))
                    return q12

                h0 = ph.op("vector", sV, [bcd, p3_all, p3g.tok()],
                           lambda v: v.tensor_tensor(
                               r3ap(h1.ap()), r3ap(posx.ap()),
                               lnbc[:, 8 * 32:9 * 32].unsqueeze(1).broadcast_to([128, 64, 32]), ALU.mult))
                dep = h0
                hin = h1
                for layer in range(3):
                    hout = h2 if hin is h1 else h1
                    sdone = ln_silu(hin, layer, 3 + layer, dep)
                    evl = {}
                    for pt in range(64):
                        pd = [sdone] if pt == 0 else []
                        if pt >= 2:
                            pd.append(evl[pt - 2])
                        pm = ph.op("tensor", sP, pd,
                                   lambda t, pt=pt, hin=hin: t.transpose(
                                       ps_t32[pt % 2][:], hin[:, pt * 32:(pt + 1) * 32], C["identf"][:]))
                        evl[pt] = ph.op("scalar", sA, [pm],
                                        lambda sc, pt=pt: sc.activation(
                                            hT[:, pt * 128:(pt + 1) * 128], ps_t32[pt % 2][:], AF.Copy))
                    ev2 = {}
                    for pt in range(64):
                        pd = [evl[63]] if pt == 0 else []
                        if pt >= 2:
                            pd.append(ev2[pt - 2])
                        pm = ph.op("tensor", sP, pd,
                                   lambda t, pt=pt, layer=layer: t.matmul(
                                       ps_h32[pt % 2][:], hT[:, pt * 128:(pt + 1) * 128],
                                       fcw[:, layer * 32:(layer + 1) * 32], start=True, stop=True))
                        ev2[pt] = ph.op("scalar", sA, [pm],
                                        lambda sc, pt=pt, hout=hout: sc.activation(
                                            hout[:, pt * 32:(pt + 1) * 32], ps_h32[pt % 2][:], AF.Copy))
                    dep = ev2[63]
                    hin = hout
                sdone = ln_silu(hin, 6, 7, dep)
                pg = ph.op("vector", sV, [sdone],
                           lambda v, hin=hin: v.tensor_tensor(hin[:], hin[:], decayx[:], ALU.mult))
                evl = {}
                for pt in range(64):
                    pd = [pg] if pt == 0 else []
                    if pt >= 2:
                        pd.append(evl[pt - 2])
                    pm = ph.op("tensor", sP, pd,
                               lambda t, pt=pt, hin=hin: t.transpose(
                                   ps_t32[pt % 2][:], hin[:, pt * 32:(pt + 1) * 32], C["identf"][:]))
                    evl[pt] = ph.op("scalar", sA, [pm],
                                    lambda sc, pt=pt: sc.activation(
                                        hT[:, pt * 128:(pt + 1) * 128], ps_t32[pt % 2][:], AF.Copy))
                tev = {}
                tstv = {}
                for cb in range(8):
                    for mc2 in range(16):
                        i = cb * 16 + mc2
                        pd = [evl[63]] if i == 0 else []
                        if i >= 2:
                            pd.append(tev[i - 2])
                        pm = ph.op("tensor", sP, pd,
                                   lambda t, cb=cb, mc2=mc2, i=i: t.matmul(
                                       ps_tg[i % 2][:], woutb[:, cb * 128:(cb + 1) * 128],
                                       hT[:, mc2 * 512:(mc2 + 1) * 512], start=True, stop=True))
                        ed = [pm]
                        if i >= 2:
                            ed.append(tstv[i - 2])
                        em = ph.op("scalar", sA, ed,
                                   lambda sc, i=i: sc.activation(tg[i % 2][:], ps_tg[i % 2][:], AF.Copy))
                        sm = ph.op("sync", tgs[i % 2], [em],
                                   lambda sy, cb=cb, mc2=mc2, i=i: sy.dma_start(
                                       t_dram[cb * 128:(cb + 1) * 128, mc2 * 512:(mc2 + 1) * 512],
                                       tg[i % 2][:]), 16)
                        tev[i] = em
                        tstv[i] = sm
                t_stores = [tgs[0].tok(), tgs[1].tok()]
                xd_total = xds.tok()
                ph.run("p3")

        xnT_stack.close()
        with contextlib.ExitStack() as pp:
            psE4 = [pp.enter_context(nc.psum_tensor(f"psE{i}", [128, 512], F32)) for i in range(4)]
            psY4 = pp.enter_context(nc.psum_tensor("psY4", [64, 2048], F32))
            with contextlib.ExitStack() as sb4:
                vsl = [sb(sb4, f"vsl{i}", [64, 4096], BF16) for i in range(2)]
                tsl = [sb(sb4, f"tsl{i}", [128, 4096], BF16) for i in range(2)]
                bA = [[sb(sb4, f"bA{p}_{i}", [128, 4096], BF16) for i in range(2)] for p in range(2)]
                bB = [[sb(sb4, f"bB{p}_{i}", [128, 4096], BF16) for i in range(2)] for p in range(2)]
                bC = [[sb(sb4, f"bC{p}_{i}", [128, 4096], BF16) for i in range(2)] for p in range(2)]
                ysb = [sb(sb4, f"ysb{p}", [64, 4096], F32) for p in range(2)]
                ph = Phase(nc)
                vls = [dsem("vl0"), dsem("vl1")]
                tls = [dsem("tl0"), dsem("tl1")]
                yss = dsem("ys")
                st_ = {}
                ydma_hist = {}
                g3 = lambda ap_: ap_.rearrange("p (g j) -> p g j", j=128)
                twcB = C["twc"].ap().unsqueeze(1).broadcast_to([128, 32, 128])
                twsB = C["tws"].ap().unsqueeze(1).broadcast_to([128, 32, 128])
                t2cB = C["tw2c"].ap().unsqueeze(1).broadcast_to([128, 32, 128])
                t2sB = C["tw2s"].ap().unsqueeze(1).broadcast_to([128, 32, 128])

                def step1(slab, nb, ldtok, dst, psfree, bufree):
                    evs = []
                    last = None
                    for gb in range(8):
                        pe0 = psE4[2 * (gb % 2)]
                        pe1 = psE4[2 * (gb % 2) + 1]
                        for g4 in range(4):
                            g = gb * 4 + g4
                            dep = []
                            if gb == 0 and g4 == 0:
                                dep = [ldtok] + psfree
                            if g4 == 0 and gb >= 2:
                                dep.append(evs[gb - 2])
                            lhs = slab.ap().rearrange("b (d a) -> b d a", a=64)[:, 2 * g:2 * g + 2, :]
                            ph.op("tensor", sP, dep,
                                  lambda t, lhs=lhs, g4=g4, nb=nb, pe0=pe0: t.matmul(
                                      pe0[:, g4 * 128:(g4 + 1) * 128], lhs,
                                      C["f1c"][0:nb, :], start=True, stop=True))
                            last = ph.op("tensor", sP, [],
                                         lambda t, lhs=lhs, g4=g4, nb=nb, pe1=pe1: t.matmul(
                                             pe1[:, g4 * 128:(g4 + 1) * 128], lhs,
                                             C["f1s"][0:nb, :], start=True, stop=True))
                        ed0 = [last] + (bufree if gb == 0 else [])
                        ph.op("scalar", sA, ed0,
                              lambda sc, gb=gb, dst=dst, pe0=pe0: sc.activation(
                                  dst[0][:, gb * 512:(gb + 1) * 512], pe0[:], AF.Copy))
                        e2_ = ph.op("scalar", sA, [],
                                    lambda sc, gb=gb, dst=dst, pe1=pe1: sc.activation(
                                        dst[1][:, gb * 512:(gb + 1) * 512], pe1[:], AF.Copy))
                        evs.append(e2_)
                    return last, evs[-1]

                def cmul6(deps, ar, ai, br_c, bs_c, outr, outi, tmp, eng="vector", sem=None):
                    sm = sem if sem is not None else sV
                    w1 = ph.op(eng, sm, deps,
                               lambda v: v.tensor_tensor(g3(outr.ap()), g3(ar.ap()), br_c, ALU.mult))
                    w2 = ph.op(eng, sm, [w1],
                               lambda v: v.tensor_tensor(g3(outi.ap()), g3(ai.ap()), bs_c, ALU.mult))
                    w3 = ph.op(eng, sm, [w2],
                               lambda v: v.tensor_tensor(outr[:], outr[:], outi[:], ALU.subtract))
                    w4 = ph.op(eng, sm, [w3],
                               lambda v: v.tensor_tensor(g3(outi.ap()), g3(ar.ap()), bs_c, ALU.mult))
                    w5 = ph.op(eng, sm, [w4],
                               lambda v: v.tensor_tensor(g3(ar.ap()), g3(ai.ap()), br_c, ALU.mult))
                    w6 = ph.op(eng, sm, [w5],
                               lambda v: v.tensor_tensor(outi[:], outi[:], ar[:], ALU.add))
                    return w6, w5

                def stepC(src, dst, dep0):
                    evs = []
                    for gb in range(8):
                        pe0 = psE4[2 * (gb % 2)]
                        pe1 = psE4[2 * (gb % 2) + 1]
                        last = None
                        for g4 in range(4):
                            g = gb * 4 + g4
                            dep = []
                            if g4 == 0:
                                dep = list(dep0) if gb == 0 else [evs[gb - 2]] if gb >= 2 else []
                            sl_re = src[0][:, g * 128:(g + 1) * 128]
                            sl_im = src[1][:, g * 128:(g + 1) * 128]
                            ph.op("tensor", sP, dep,
                                  lambda t, sl_re=sl_re, g4=g4, pe0=pe0: t.matmul(
                                      pe0[:, g4 * 128:(g4 + 1) * 128], C["bdec"][:], sl_re,
                                      start=True, stop=False))
                            ph.op("tensor", sP, [],
                                  lambda t, sl_im=sl_im, g4=g4, pe0=pe0: t.matmul(
                                      pe0[:, g4 * 128:(g4 + 1) * 128], C["bdes"][:], sl_im,
                                      start=False, stop=True))
                            ph.op("tensor", sP, [],
                                  lambda t, sl_im=sl_im, g4=g4, pe1=pe1: t.matmul(
                                      pe1[:, g4 * 128:(g4 + 1) * 128], C["bdec"][:], sl_im,
                                      start=True, stop=False))
                            last = ph.op("tensor", sP, [],
                                         lambda t, sl_re=sl_re, g4=g4, pe1=pe1: t.matmul(
                                             pe1[:, g4 * 128:(g4 + 1) * 128], C["bdesn"][:], sl_re,
                                             start=False, stop=True))
                        ph.op("scalar", sA, [last],
                              lambda sc, gb=gb, dst=dst, pe0=pe0: sc.activation(
                                  dst[0][:, gb * 512:(gb + 1) * 512], pe0[:], AF.Copy))
                        e2_ = ph.op("scalar", sA, [],
                                    lambda sc, gb=gb, dst=dst, pe1=pe1: sc.activation(
                                        dst[1][:, gb * 512:(gb + 1) * 512], pe1[:], AF.Copy))
                        evs.append(e2_)
                    return last, evs[-1]

                def stepA(src, dst, dep0):
                    evs = []
                    for gb in range(8):
                        pe0 = psE4[2 * (gb % 2)]
                        pe1 = psE4[2 * (gb % 2) + 1]
                        last = None
                        for g4 in range(4):
                            g = gb * 4 + g4
                            dep = []
                            if g4 == 0:
                                dep = list(dep0) if gb == 0 else [evs[gb - 2]] if gb >= 2 else []
                            yre = src[0][:, g * 128:(g + 1) * 128]
                            yim = src[1][:, g * 128:(g + 1) * 128]
                            ph.op("tensor", sP, dep,
                                  lambda t, yre=yre, g4=g4, pe0=pe0: t.matmul(
                                      pe0[:, g4 * 128:(g4 + 1) * 128], yre, C["bdec"][:],
                                      start=True, stop=False))
                            ph.op("tensor", sP, [],
                                  lambda t, yim=yim, g4=g4, pe0=pe0: t.matmul(
                                      pe0[:, g4 * 128:(g4 + 1) * 128], yim, C["bdesn"][:],
                                      start=False, stop=True))
                            ph.op("tensor", sP, [],
                                  lambda t, yre=yre, g4=g4, pe1=pe1: t.matmul(
                                      pe1[:, g4 * 128:(g4 + 1) * 128], yre, C["bdes"][:],
                                      start=True, stop=False))
                            last = ph.op("tensor", sP, [],
                                         lambda t, yim=yim, g4=g4, pe1=pe1: t.matmul(
                                             pe1[:, g4 * 128:(g4 + 1) * 128], yim, C["bdec"][:],
                                             start=False, stop=True))
                        ph.op("scalar", sA, [last],
                              lambda sc, gb=gb, dst=dst, pe0=pe0: sc.activation(
                                  dst[0][:, gb * 512:(gb + 1) * 512], pe0[:], AF.Copy))
                        e2_ = ph.op("scalar", sA, [],
                                    lambda sc, gb=gb, dst=dst, pe1=pe1: sc.activation(
                                        dst[1][:, gb * 512:(gb + 1) * 512], pe1[:], AF.Copy))
                        evs.append(e2_)
                    return last, evs[-1]

                def emit_A_B(cp):
                    pr = cp % 2
                    stc = st_[cp]
                    la, ea = stepA(bB[pr], bA[pr], [stc["p6"], stc["psfree_for_A"]])
                    stc["stA_last"] = la
                    stc["stA_ev"] = ea
                    return la, ea

                def emit_B(cp):
                    pr = cp % 2
                    stc = st_[cp]
                    for hb in range(2):
                        lastY = None
                        dep0 = [stc["iv6"]]
                        pv = ydma_hist.get((cp, hb - 1), ydma_hist.get((cp - 1, 1)))
                        if pv:
                            dep0.append(pv)
                        for g16 in range(16):
                            g = hb * 16 + g16
                            ph.op("tensor", sP, dep0 if g16 == 0 else [],
                                  lambda t, g=g, g16=g16, pr=pr: t.matmul(
                                      psY4[:, g16 * 128:(g16 + 1) * 128], C["c2c"][:],
                                      bC[pr][0][:, g * 128:(g + 1) * 128], start=True, stop=False))
                            lastY = ph.op("tensor", sP, [],
                                          lambda t, g=g, g16=g16, pr=pr: t.matmul(
                                              psY4[:, g16 * 128:(g16 + 1) * 128], C["c2s"][:],
                                              bC[pr][1][:, g * 128:(g + 1) * 128], start=False, stop=True))
                        yed = [lastY]
                        pv2 = ydma_hist.get((cp - 2, 1))
                        if pv2:
                            yed.append(pv2)
                        ye = ph.op("scalar", sA, yed,
                                   lambda sc, hb=hb, pr=pr: sc.activation(
                                       ysb[pr][:, hb * 2048:(hb + 1) * 2048], psY4[:], AF.Copy))
                        ydma = ph.op("gpsimd", yss, [ye],
                                     lambda g_, cp=cp, hb=hb, pr=pr: g_.dma_start(
                                         bass.AP(y_dram.tensor, (cp * 64 + hb * 32) * NT,
                                                 [[64, 64], [2 * NT, 16], [NT, 2], [1, 64]]),
                                         bass.AP(ysb[pr], hb * 2048,
                                                 [[4096, 64], [128, 16], [64, 2], [1, 64]])), 16)
                        ydma_hist[(cp, hb)] = ydma

                for ct in range(16):
                    par = ct % 2
                    st_[ct] = {}
                    vdep = [st_[ct - 2]["s1v_last"]] if ct >= 2 else list(v_stores) + [xd_total]
                    vld = ph.op("gpsimd", vls[par], vdep,
                                lambda g, ct=ct, par=par: g.dma_start(
                                    vsl[par].ap().rearrange("b (d a) -> b d a", a=64),
                                    bass.AP(v_dram.tensor, ct * 64 * NT,
                                            [[64, 64], [NT, 64], [1, 64]])), 16)
                    tdep = [st_[ct - 2]["s1t_last"]] if ct >= 2 else list(t_stores) + [xd_total]
                    tld = ph.op("sync", tls[par], tdep,
                                lambda sy, ct=ct, par=par: sy.dma_start(
                                    tsl[par].ap().rearrange("b (d a) -> b d a", a=64),
                                    bass.AP(t_dram.tensor, ct * 64 * M,
                                            [[64, 128], [M, 64], [1, 64]])), 16)
                    psfree = [st_[ct - 1]["sCT_ev"]] if ct >= 1 else [const_ld]
                    bufree = [st_[ct - 2]["iv5"]] if ct >= 2 else []
                    l1, e1 = step1(vsl[par], 64, vld, bA[par], psfree, bufree)
                    st_[ct]["s1v_last"] = l1
                    twdeps = [e1]
                    if ct >= 2:
                        twdeps.append(st_[ct - 2]["stA_last"])
                    w6, _ = cmul6(twdeps, bA[par][0], bA[par][1], twcB, twsB,
                                  bB[par][0], bB[par][1], None)
                    lcv, xev = stepC(bB[par], bA[par], [w6])
                    st_[ct]["sCv_last"] = lcv
                    if ct >= 1:
                        st_[ct - 1]["psfree_for_A"] = xev
                        emit_A_B(ct - 1)
                        pr = (ct - 1) % 2
                        iv6, iv5 = cmul6([st_[ct - 1]["stA_ev"]],
                                         bA[pr][0], bA[pr][1], t2cB, t2sB,
                                         bC[pr][0], bC[pr][1], None,
                                         eng="gpsimd", sem=sG)
                        st_[ct - 1]["iv6"] = iv6
                        st_[ct - 1]["iv5"] = iv5
                    psfreeT = [st_[ct - 1]["stA_ev"]] if ct >= 1 else [xev]
                    l2, e2 = step1(tsl[par], 128, tld, bC[par], psfreeT, [])
                    st_[ct]["s1t_last"] = l2
                    w6t, _ = cmul6([e2], bC[par][0], bC[par][1], twcB, twsB,
                                   bB[par][0], bB[par][1], None)
                    lct, aev = stepC(bB[par], bC[par], [w6t])
                    st_[ct]["sCT_last"] = lct
                    st_[ct]["sCT_ev"] = aev
                    if ct >= 1:
                        emit_B(ct - 1)
                    pwdeps = [xev, aev]
                    if ct >= 1:
                        pwdeps.append(st_[ct - 1]["stA_last"])
                    p1_ = ph.op("vector", sV, pwdeps,
                                lambda v, par=par: v.tensor_tensor(bB[par][0][:], bA[par][0][:],
                                                                   bC[par][0][:], ALU.mult))
                    p2_ = ph.op("vector", sV, [p1_],
                                lambda v, par=par: v.tensor_tensor(bB[par][1][:], bA[par][1][:],
                                                                   bC[par][1][:], ALU.mult))
                    p3_ = ph.op("vector", sV, [p2_],
                                lambda v, par=par: v.tensor_tensor(bB[par][0][:], bB[par][0][:],
                                                                   bB[par][1][:], ALU.subtract))
                    p4_ = ph.op("vector", sV, [p3_],
                                lambda v, par=par: v.tensor_tensor(bB[par][1][:], bA[par][0][:],
                                                                   bC[par][1][:], ALU.mult))
                    p5_ = ph.op("vector", sV, [p4_],
                                lambda v, par=par: v.tensor_tensor(bA[par][0][:], bA[par][1][:],
                                                                   bC[par][0][:], ALU.mult))
                    p6_ = ph.op("vector", sV, [p5_],
                                lambda v, par=par: v.tensor_tensor(bB[par][1][:], bB[par][1][:],
                                                                   bA[par][0][:], ALU.add))
                    st_[ct]["p6"] = p6_
                st_[15]["psfree_for_A"] = st_[15]["sCT_ev"]
                emit_A_B(15)
                iv6, iv5 = cmul6([st_[15]["stA_ev"]], bA[1][0], bA[1][1], t2cB, t2sB,
                                 bC[1][0], bC[1][1], None, eng="gpsimd", sem=sG)
                st_[15]["iv6"] = iv6
                st_[15]["iv5"] = iv5
                emit_B(15)
                y_stores = yss.tok()
                ph.run("p4")

        with contextlib.ExitStack() as pp:
            ps_s5 = [pp.enter_context(nc.psum_tensor(f"ps_s5{i}", [128, 512], F32)) for i in range(2)]
            ps_t5 = [pp.enter_context(nc.psum_tensor(f"ps_t5{i}", [128, 128], F32)) for i in range(2)]
            with contextlib.ExitStack() as sb5:
                o_sb = sb(sb5, "o_sb", [128, 8 * D], BF16)
                ysl = [sb(sb5, f"ysl{i}", [128, 512], F32) for i in range(2)]
                usl = [sb(sb5, f"usl{i}", [128, 512], F32) for i in range(2)]
                prod = sb(sb5, "prod", [128, 8 * 512], BF16)
                sev = [sb(sb5, f"sev{i}", [128, 512], F32) for i in range(2)]
                stt5 = [sb(sb5, f"stt5{i}", [128, 128], F32) for i in range(2)]
                ph = Phase(nc)
                ows = dsem("ow")
                yus = [dsem("yu0"), dsem("yu1")]
                sss = [dsem("ss0"), dsem("ss1")]
                owl = ph.op("gpsimd", ows, [],
                            lambda g: g.dma_start(o_sb[:], o_w.rearrange("(cc p) d -> p cc d", p=128)), 16)
                sd_list = []
                em_by_slot = {}
                smm_last = None
                prev_pr = None
                for mc in range(8):
                    pr = {}
                    for cb in range(8):
                        ldd = []
                        if mc == 0 and cb == 0:
                            ldd = [y_stores] + u_stores
                        if cb >= 2:
                            ldd.append(pr[cb - 2])
                        elif mc >= 1:
                            ldd.append(prev_pr[6 + cb])
                        ph.op("sync", yus[cb % 2], ldd,
                              lambda sy, mc=mc, cb=cb: sy.dma_start(
                                  ysl[cb % 2][:],
                                  y_dram[cb * 128:(cb + 1) * 128, mc * 512:(mc + 1) * 512]), 16)
                        ul = ph.op("sync", yus[cb % 2], [],
                                   lambda sy, mc=mc, cb=cb: sy.dma_start(
                                       usl[cb % 2][:],
                                       u_dram[cb * 128:(cb + 1) * 128, mc * 512:(mc + 1) * 512]), 16)
                        pdep = [ul]
                        if mc >= 1 and cb == 0:
                            pdep.append(smm_last)
                        pr[cb] = ph.op("vector", sV, pdep,
                                       lambda v, cb=cb: v.tensor_tensor(
                                           prod[:, cb * 512:(cb + 1) * 512], ysl[cb % 2][:],
                                           usl[cb % 2][:], ALU.mult))
                    prev_pr = pr
                    for db in range(8):
                        last = None
                        mdep = [pr[7], owl]
                        if em_by_slot.get(db % 2):
                            mdep.append(em_by_slot[db % 2])
                        for cb in range(8):
                            last = ph.op("tensor", sP, mdep if cb == 0 else [],
                                         lambda t, db=db, cb=cb: t.matmul(
                                             ps_s5[db % 2][:],
                                             o_sb[:, cb * D + db * 128: cb * D + (db + 1) * 128],
                                             prod[:, cb * 512:(cb + 1) * 512],
                                             start=(cb == 0), stop=(cb == 7)))
                        smm_last = last
                        em = ph.op("scalar", sA, [last],
                                   lambda sc, db=db: sc.activation(sev[db % 2][:], ps_s5[db % 2][:], AF.Copy))
                        st_ev = {}
                        for sub in range(4):
                            td = [em] if sub == 0 else []
                            if sub >= 2:
                                td.append(st_ev[sub - 2])
                            pm = ph.op("tensor", sP, td,
                                       lambda t, db=db, sub=sub: t.transpose(
                                           ps_t5[sub % 2][:], sev[db % 2][:, sub * 128:(sub + 1) * 128],
                                           C["identf"][:]))
                            ed = [pm]
                            if len(sd_list) >= 2:
                                ed.append(sd_list[-2])
                            e2 = ph.op("scalar", sA, ed,
                                       lambda sc, sub=sub: sc.activation(stt5[sub % 2][:],
                                                                         ps_t5[sub % 2][:], AF.Copy))
                            st_ev[sub] = e2
                            sd = ph.op("sync", sss[sub % 2], [e2],
                                       lambda sy, mc=mc, db=db, sub=sub: sy.dma_start(
                                           s_dram[mc * 512 + sub * 128: mc * 512 + (sub + 1) * 128,
                                                  db * 128:(db + 1) * 128], stt5[sub % 2][:]), 16)
                            sd_list.append(sd)
                        em_by_slot[db % 2] = em
                s_stores = [sss[0].tok(), sss[1].tok()]
                ph.run("p5")

        with contextlib.ExitStack() as pp:
            ps_a6 = [pp.enter_context(nc.psum_tensor(f"ps_a6{i}", [128, 256], F32)) for i in range(2)]
            ps_b6 = [pp.enter_context(nc.psum_tensor(f"ps_b6{i}", [128, 512], F32)) for i in range(4)]
            ps_b1t = pp.enter_context(nc.psum_tensor("ps_b1t", [128, 32], F32))
            with contextlib.ExitStack() as sb6:
                w1sb = sb(sb6, "w1sb", [128, 8 * FF], BF16)
                w2sb = sb(sb6, "w2sb", [128, 32 * 1024], BF16)
                stage = [sb(sb6, f"stage{i}", [128, FF], F32) for i in range(2)]
                xtb = [sb(sb6, f"xtb{i}", [128, 2048], BF16) for i in range(2)]
                hcur = [sb(sb6, f"hcur{i}", [128, 256], BF16) for i in range(2)]
                b1T = sb(sb6, "b1T", [32, 128], F32)
                b1sb = sb(sb6, "b1sb", [128, 32], F32)
                mev = sb(sb6, "mev", [128, 2048], F32)
                gixf = sb(sb6, "gixf", [128, 48], F32)
                gixi = sb(sb6, "gixi", [128, 2 * 48], mybir.dt.int32)
                ph = Phase(nc)
                gst = [dsem("gst0"), dsem("gst1")]
                b1s = dsem("b1s")
                xtbs = [dsem("xtb0"), dsem("xtb1")]
                mas = dsem("mas")
                gix_done = None
                for j in range(2):
                    q1 = ph.op("vector", sV, [gix_done] if gix_done else [const_ld] + list(s_stores),
                               lambda v, j=j: v.tensor_scalar(
                                   gixf[:], C["gscale"][:], ebc[:, j:j + 1], None, ALU.mult))
                    q2 = ph.op("vector", sV, [q1],
                               lambda v: v.tensor_tensor(gixf[:], gixf[:], C["gidx_base"][:], ALU.add))
                    gix_done = ph.op("vector", sV, [q2],
                                     lambda v, j=j: v.tensor_copy(gixi[:, j * 48:(j + 1) * 48], gixf[:]))
                W1v = W1.rearrange("e d f -> (e d) f")
                W2v = W2.rearrange("e f d -> (e f) d")
                b1v = b1.rearrange("e (fc p) -> (e fc) p", p=128)
                prev_expert_pe = None
                prev_bev = None
                moe_hist = {}
                mev_done = None
                for e in range(2):
                    cast_last = None
                    gmark = {}
                    for c in range(40):
                        si = c % 2
                        gdeps = [gix_done] if c == 0 else []
                        if prev_expert_pe is not None and c <= 1:
                            gdeps.append(prev_expert_pe)
                        if c >= 2:
                            gdeps.append(gmark[c - 2])
                        if c < 8:
                            gl = ph.op("gpsimd", gst[si], gdeps,
                                       lambda g, e=e, c=c, si=si: g.indirect_dma_start(
                                           stage[si][:], None, W1v,
                                           bass.IndirectOffsetOnAxis(gixi[:, e * 48 + c: e * 48 + c + 1], 0)), 16)
                            cm = ph.op("vector" if c % 4 < 2 else "scalar",
                                       sV if c % 4 < 2 else sA, [gl],
                                       (lambda v, c=c, si=si: v.tensor_copy(
                                           w1sb[:, c * FF:(c + 1) * FF], stage[si][:]))
                                       if c % 4 < 2 else
                                       (lambda sc, c=c, si=si: sc.activation(
                                           w1sb[:, c * FF:(c + 1) * FF], stage[si][:], AF.Copy)))
                        else:
                            fc = c - 8
                            gl = ph.op("gpsimd", gst[si], gdeps,
                                       lambda g, e=e, fc=fc, si=si: g.indirect_dma_start(
                                           stage[si][:, 0:1024], None, W2v,
                                           bass.IndirectOffsetOnAxis(gixi[:, e * 48 + 8 + fc: e * 48 + 9 + fc], 0)), 16)
                            cm = ph.op("vector" if c % 4 < 2 else "scalar",
                                       sV if c % 4 < 2 else sA, [gl],
                                       (lambda v, fc=fc, si=si: v.tensor_copy(
                                           w2sb[:, fc * 1024:(fc + 1) * 1024], stage[si][:, 0:1024]))
                                       if c % 4 < 2 else
                                       (lambda sc, fc=fc, si=si: sc.activation(
                                           w2sb[:, fc * 1024:(fc + 1) * 1024], stage[si][:, 0:1024], AF.Copy)))
                        gmark[c] = cm
                        if c % 4 < 2:
                            cast_dve_last = cm
                        else:
                            cast_act_last = cm
                    wl1 = cast_dve_last
                    wl2 = cast_act_last
                    bl = ph.op("gpsimd", b1s, [gix_done] + ([prev_bev] if prev_bev else []),
                               lambda g, e=e: g.indirect_dma_start(
                                   b1T[:], None, b1v,
                                   bass.IndirectOffsetOnAxis(gixi[0:32, e * 48 + 40: e * 48 + 41], 0)), 16)
                    bdeps = [bl]
                    if prev_bev is not None:
                        bdeps.append(prev_bev)
                    btr = ph.op("tensor", sP, bdeps,
                                lambda t: t.transpose(ps_b1t[:], b1T[:], C["identf"][0:32, 0:32]))
                    bev = ph.op("vector", sV, [btr],
                                lambda v: v.tensor_copy(b1sb[:], ps_b1t[:]))
                    prev_bev = bev
                    last_b_mm = None
                    lastA_by_tb = {}
                    for TB in range(16):
                        xdeps = [xd_total] + list(s_stores) if (e == 0 and TB == 0) else []
                        if TB >= 2:
                            xdeps.append(lastA_by_tb[TB - 2])
                        elif e > 0:
                            xdeps.append(prev_expert_pe)
                        xld = ph.op("sync", xtbs[TB % 2], xdeps,
                                    lambda sy, TB=TB: sy.dma_start(
                                        xtb[TB % 2].ap().rearrange("p (dc t) -> p dc t", t=256),
                                        xnT_dram.rearrange("p (dc m) -> p dc m", m=NT)[:, :,
                                                                                       TB * 256:(TB + 1) * 256]), 16)
                        gel = {}

                        def emit_B(fc, TB=TB):
                            bdeps2 = [gel[fc]]
                            if fc == 0:
                                bdeps2.append(wl2)
                                if mev_done is not None:
                                    bdeps2.append(mev_done)
                            lastB = None
                            for tc2 in range(2):
                                for dh in range(2):
                                    j = tc2 * 2 + dh
                                    lastB = ph.op("tensor", sP, bdeps2 if j == 0 else [],
                                                  lambda t, fc=fc, tc2=tc2, dh=dh, j=j: t.matmul(
                                                      ps_b6[j][:],
                                                      hcur[fc % 2][:, tc2 * 128:(tc2 + 1) * 128],
                                                      w2sb[:, fc * 1024 + dh * 512: fc * 1024 + (dh + 1) * 512],
                                                      start=(fc == 0), stop=(fc == 31)))
                            return lastB

                        for fc in range(32):
                            adeps = []
                            if fc == 0:
                                adeps = [wl1, wl2, bev, xld]
                            if fc >= 2:
                                adeps.append(gel[fc - 2])
                            lastA = None
                            for dc in range(8):
                                lastA = ph.op("tensor", sP, adeps if dc == 0 else [],
                                              lambda t, TB=TB, fc=fc, dc=dc: t.matmul(
                                                  ps_a6[fc % 2][:],
                                                  w1sb[:, dc * FF + fc * 128: dc * FF + (fc + 1) * 128],
                                                  xtb[TB % 2][:, dc * 256:(dc + 1) * 256],
                                                  start=(dc == 0), stop=(dc == 7)))
                            lastA_by_tb[TB] = lastA
                            if fc >= 1:
                                emit_B(fc - 1)
                            gel[fc] = ph.op("scalar", sA, [lastA],
                                            lambda sc, fc=fc: sc.activation(
                                                hcur[fc % 2][:], ps_a6[fc % 2][:], AF.Gelu,
                                                bias=b1sb[:, fc:fc + 1]))
                        last_b_mm = emit_B(31)
                        for j in range(4):
                            mdep = [last_b_mm]
                            pv = moe_hist.get((e, TB - 1), moe_hist.get((e - 1, 15)))
                            if pv:
                                mdep.append(pv)
                            mev_done = ph.op("vector", sV, mdep,
                                             lambda v, j=j, e=e: v.tensor_scalar(
                                                 mev[:, j * 512:(j + 1) * 512], ps_b6[j][:],
                                                 cwslbc[:, e:e + 1], None, ALU.mult))
                        md = ph.op("gpsimd", mas, [mev_done],
                                   lambda g, e=e, TB=TB: g.dma_start(
                                       bass.AP(moe_dram.tensor, TB * 256 * D,
                                               [[D, 128], [128 * D, 2], [512, 2], [1, 512]]),
                                       mev.ap().rearrange("p (tc dh c) -> p tc dh c", dh=2, c=512),
                                       accum_op=(ALU.add if e > 0 else ALU.bypass)), 16)
                        moe_hist[(e, TB)] = md
                    prev_expert_pe = last_b_mm
                moe_done = mas.tok()
                ph.run("p6")

        with contextlib.ExitStack() as pp:
            ps_bc7 = pp.enter_context(nc.psum_tensor("ps_bc7", [128, 512], F32))
            with contextlib.ExitStack() as sb7:
                gbc = sb(sb7, "gbc", [128, D], F32)
                bbc = sb(sb7, "bbc", [128, D], F32)
                b2bc = sb(sb7, "b2bc", [128, D], F32)
                ytile = [sb(sb7, f"ytile{i}", [128, D], F32) for i in range(2)]
                snt = [sb(sb7, f"snt{i}", [128, D], F32) for i in range(2)]
                xt7 = [sb(sb7, f"xt7{i}", [128, D], F32) for i in range(2)]
                gt = [sb(sb7, f"gt{i}", [128, D], F32) for i in range(2)]
                ot = [sb(sb7, f"ot{i}", [128, D], F32) for i in range(2)]
                st7 = [sb(sb7, f"st7{i}", [128, 8], F32) for i in range(2)]
                grow = sb(sb7, "grow", [1, D], F32)
                brow = sb(sb7, "brow", [1, D], F32)
                b2r = sb(sb7, "b2r", [1, D], F32)
                ph = Phase(nc)
                p7s = dsem("p7s")
                yts = [dsem("yt70"), dsem("yt71")]
                sns = [dsem("sn0"), dsem("sn1")]
                x7s = [dsem("x70"), dsem("x71")]
                oss = [dsem("os0"), dsem("os1")]
                ph.op("sync", p7s, [], lambda sy: sy.dma_start(grow[:], gamma), 16)
                ph.op("sync", p7s, [], lambda sy: sy.dma_start(brow[:], beta), 16)
                rows_ld = ph.op("sync", p7s, [b2mix_store],
                                lambda sy: sy.dma_start(b2r[:], b2mix_dram), 16)
                bcv = None
                for row, dstt in [(grow, gbc), (brow, bbc), (b2r, b2bc)]:
                    for half in range(2):
                        bdeps = [rows_ld]
                        if bcv is not None:
                            bdeps.append(bcv)
                        bm = ph.op("tensor", sP, bdeps,
                                   lambda t, row=row, half=half: t.matmul(
                                       ps_bc7[:], C["onesrow"][:], row[0:1, half * 512:(half + 1) * 512],
                                       start=True, stop=True))
                        bcv = ph.op("vector", sV, [bm],
                                    lambda v, dstt=dstt, half=half: v.tensor_copy(
                                        dstt[:, half * 512:(half + 1) * 512], ps_bc7[:]))
                out_hist = {}
                r12_hist = {}
                r13_hist = {}
                a3_hist = {}

                def tile_chain(t_):
                    bi = t_ % 2
                    ldd = []
                    if t_ <= 1:
                        ldd.append(moe_done)
                    if t_ >= 2:
                        ldd.append(a3_hist[t_ - 2])
                    mld = ph.op("sync", yts[bi], ldd,
                                lambda sy, t_=t_, bi=bi: sy.dma_start(
                                    ytile[bi][:], moe_dram[t_ * 128:(t_ + 1) * 128, :]), 16)
                    yield
                    xld = []
                    if t_ <= 1:
                        xld = xld + xn_stores_l
                    if t_ >= 2:
                        xld.append(r12_hist[t_ - 2])
                    xl = ph.op("sync", sns[bi], xld,
                               lambda sy, t_=t_, bi=bi: sy.dma_start(
                                   snt[bi][:], xn_dram[t_ * 128:(t_ + 1) * 128, :]), 16)
                    yield
                    r1 = ph.op("vector", sV, [mld, xl, bcv],
                               lambda v, bi=bi: v.tensor_tensor(ytile[bi][:], ytile[bi][:], snt[bi][:], ALU.add))
                    yield
                    r2 = ph.op("vector", sV, [r1],
                               lambda v, bi=bi: v.tensor_tensor(ytile[bi][:], ytile[bi][:], b2bc[:], ALU.add))
                    yield
                    r3_ = ph.op("vector", sV, [r2],
                                lambda v, bi=bi: v.tensor_reduce(st7[bi][:, 0:1], ytile[bi][:], AX.X, ALU.add))
                    a1d = [r2]
                    if t_ >= 2:
                        a1d.append(r12_hist[t_ - 2])
                    a1 = ph.op("scalar", sA, a1d,
                               lambda sc, bi=bi: sc.activation(gt[bi][:], ytile[bi][:], AF.Square,
                                                               accum_out=st7[bi][:, 1:2]))
                    yield
                    r4 = ph.op("vector", sV, [r3_],
                               lambda v, bi=bi: v.tensor_scalar_mul(st7[bi][:, 2:3], st7[bi][:, 0:1], 1.0 / D))
                    yield
                    r5 = ph.op("vector", sV, [a1, r4],
                               lambda v, bi=bi: v.tensor_scalar(st7[bi][:, 3:4], st7[bi][:, 1:2], 1.0 / D, EPS,
                                                                ALU.mult, ALU.add))
                    yield
                    r6 = ph.op("vector", sV, [r4],
                               lambda v, bi=bi: v.tensor_tensor(st7[bi][:, 4:5], st7[bi][:, 2:3],
                                                                st7[bi][:, 2:3], ALU.mult))
                    yield
                    r7 = ph.op("vector", sV, [r6, r5],
                               lambda v, bi=bi: v.tensor_tensor(st7[bi][:, 3:4], st7[bi][:, 3:4],
                                                                st7[bi][:, 4:5], ALU.subtract))
                    yield
                    a2 = ph.op("scalar", sA, [r7],
                               lambda sc, bi=bi: sc.activation(st7[bi][:, 5:6], st7[bi][:, 3:4], AF.Sqrt))
                    yield
                    r8 = ph.op("vector", sV, [a2],
                               lambda v, bi=bi: v.reciprocal(st7[bi][:, 6:7], st7[bi][:, 5:6]))
                    yield
                    r9 = ph.op("vector", sV, [r8],
                               lambda v, bi=bi: v.tensor_scalar(ytile[bi][:], ytile[bi][:], st7[bi][:, 2:3],
                                                                st7[bi][:, 6:7], ALU.subtract, ALU.mult))
                    yield
                    r10 = ph.op("vector", sV, [r9],
                                lambda v, bi=bi: v.tensor_tensor(ytile[bi][:], ytile[bi][:], gbc[:], ALU.mult))
                    yield
                    r11 = ph.op("vector", sV, [r10],
                                lambda v, bi=bi: v.tensor_tensor(ytile[bi][:], ytile[bi][:], bbc[:], ALU.add))
                    yield
                    a3 = ph.op("scalar", sA, [r11],
                               lambda sc, bi=bi: sc.activation(gt[bi][:], ytile[bi][:], AF.Silu))
                    a3_hist[t_] = a3
                    yield
                    sld = [r1]
                    if t_ <= 1:
                        sld = sld + s_stores
                    sl = ph.op("sync", sns[bi], sld,
                               lambda sy, t_=t_, bi=bi: sy.dma_start(
                                   snt[bi][:], s_dram[t_ * 128:(t_ + 1) * 128, :]), 16)
                    x2d = []
                    if t_ >= 2:
                        x2d.append(r13_hist[t_ - 2])
                    xl2 = ph.op("gpsimd", x7s[bi], x2d,
                                lambda g, t_=t_, bi=bi: g.dma_start(
                                    xt7[bi][:], x[t_ * 128:(t_ + 1) * 128, :]), 16)
                    yield
                    odep = [a3, sl]
                    if t_ >= 2:
                        odep.append(out_hist[t_ - 2])
                    r12 = ph.op("vector", sV, odep,
                                lambda v, bi=bi: v.tensor_tensor(ot[bi][:], gt[bi][:], snt[bi][:], ALU.mult))
                    r12_hist[t_] = r12
                    yield
                    r13 = ph.op("vector", sV, [r12, xl2],
                                lambda v, bi=bi: v.tensor_tensor(ot[bi][:], ot[bi][:], xt7[bi][:], ALU.add))
                    r13_hist[t_] = r13
                    om = ph.op("sync", oss[bi], [r13],
                               lambda sy, t_=t_, bi=bi: sy.dma_start(
                                   out[t_ * 128:(t_ + 1) * 128, :], ot[bi][:]), 16)
                    out_hist[t_] = om

                for pr in range(16):
                    ga = tile_chain(2 * pr)
                    gb = tile_chain(2 * pr + 1)
                    alive = [ga, gb]
                    while alive:
                        nxt = []
                        for g_ in alive:
                            try:
                                next(g_)
                                nxt.append(g_)
                            except StopIteration:
                                pass
                        alive = nxt
                ph.run("p7")

        all_dma_sems = list(S.values())
        with nc.Block("fin") as blk:
            @blk.gpsimd
            def _(g):
                for s_ in all_dma_sems:
                    if s_.n > 0:
                        g.wait_ge(s_.h, s_.n)
    return nc


_NC_CACHE = {}


def kernel(**inputs):
    ins = {k: np.ascontiguousarray(np.asarray(v, dtype=np.float32)) for k, v in inputs.items()}
    if "main" not in _NC_CACHE:
        _NC_CACHE["main"] = build_nc(debug=False)
    nc = _NC_CACHE["main"]
    base = dict(
        uv_w=ins["L0_attn_uv_w"], o_w=ins["L0_attn_o_w"],
        rpe_ln_w=ins["L0_attn_rpe_ln_w"], rpe_ln_b=ins["L0_attn_rpe_ln_b"],
        rpe_fc_w=ins["L0_attn_rpe_fc_w"],
        rpe_oln_w=ins["L0_attn_rpe_out_ln_w"].reshape(1, 32),
        rpe_oln_b=ins["L0_attn_rpe_out_ln_b"].reshape(1, 32),
        rpe_pos_w=ins["L0_attn_rpe_pos_w"],
        rpe_out_w=ins["L0_attn_rpe_out_w"],
        W1=ins["L0_ffn_W1"], b1=ins["L0_ffn_b1"], W2=ins["L0_ffn_W2"], b2=ins["L0_ffn_b2"],
        Wr=ins["L0_ffn_Wr"], br=ins["L0_ffn_br"].reshape(1, NE),
        gamma=ins["L0_ffn_gamma"].reshape(1, D), beta=ins["L0_ffn_beta"].reshape(1, D),
    )
    xfull = ins["L0_attn_x"]
    in_maps = [dict(base, x=np.ascontiguousarray(xfull[i])) for i in range(8)]
    res = run_bass_kernel_spmd(nc, in_maps, core_ids=list(range(8)))
    return np.stack([res.results[i]["out"] for i in range(8)], axis=0)


# version 16
# speedup vs baseline: 1.1864x; 1.0935x over previous
import contextlib
import numpy as np
import ml_dtypes
import concourse.bass as bass
import concourse.mybir as mybir
from concourse.bass_utils import run_bass_kernel_spmd

F32 = mybir.dt.float32
BF16 = mybir.dt.bfloat16
AF = mybir.ActivationFunctionType
ALU = mybir.AluOpType
AX = mybir.AxisListType
BF = ml_dtypes.bfloat16

NT = 4096
D = 1024
FF = 4096
NE = 8
M = 8192
EPS = 1e-5
GAMMA = 0.99
ENGS = ("sync", "gpsimd", "tensor", "vector", "scalar")


def _consts():
    t = {}
    a = np.arange(64); b = np.arange(128); k2 = np.arange(128); k1 = np.arange(64)
    m2 = np.arange(64)
    tp = 2 * np.pi
    t["f1c"] = np.cos(tp * np.outer(b, k2) / 128).astype(BF)
    t["f1s"] = (-np.sin(tp * np.outer(b, k2) / 128)).astype(BF)
    ec = np.cos(tp * np.outer(a, k1) / 64).astype(np.float32)
    es = np.sin(tp * np.outer(a, k1) / 64).astype(np.float32)
    bd = lambda mtx: np.kron(np.eye(2, dtype=np.float32), mtx)
    t["bdec"] = bd(ec).astype(BF)
    t["bdes"] = bd(es).astype(BF)
    t["bdesn"] = bd(-es).astype(BF)
    t["c2c"] = np.cos(tp * np.outer(k2, m2) / 128).astype(BF)
    t["c2s"] = (-np.sin(tp * np.outer(k2, m2) / 128)).astype(BF)
    twc1 = np.cos(tp * np.outer(a, k2) / M).astype(np.float32)
    tws1 = (-np.sin(tp * np.outer(a, k2) / M)).astype(np.float32)
    t["twc"] = np.concatenate([twc1, twc1], 0).astype(BF)
    t["tws"] = np.concatenate([tws1, tws1], 0).astype(BF)
    tw2c = (np.cos(tp * np.outer(k2, a) / M) / M).astype(np.float32)
    tw2s = (np.sin(tp * np.outer(k2, a) / M) / M).astype(np.float32)
    t["tw2c"] = np.concatenate([tw2c, tw2c], 1).astype(BF)
    t["tw2s"] = np.concatenate([tw2s, tw2s], 1).astype(BF)
    t["identf"] = np.eye(128, dtype=np.float32)
    t["identb"] = np.eye(128, dtype=BF)
    mm_ = (np.arange(64)[None, :] * 128 + np.arange(128)[:, None]).astype(np.int64)
    pos = np.where(mm_ <= 4095, mm_, mm_ - M).astype(np.float32)
    pos[mm_ == 4096] = 0.0
    dec = (GAMMA ** np.abs(pos)).astype(np.float32)
    dec[mm_ == 4096] = 0.0
    t["posx"] = np.repeat(pos[:, :, None], 32, axis=2).reshape(128, 2048).astype(np.float32)
    t["decayx"] = np.repeat(dec[:, :, None], 32, axis=2).reshape(128, 2048).astype(np.float32)
    t["onesrow"] = np.ones((1, 128), np.float32)
    t["ones11"] = np.ones((1, 1), np.float32)
    t["onescol_b"] = np.ones((128, 1), BF)
    t["iota8"] = np.arange(8, dtype=np.float32).reshape(1, 8)
    gb_ = np.zeros((128, 48), np.float32)
    gs_ = np.zeros((128, 48), np.float32)
    p_ = np.arange(128, dtype=np.float32)
    for c in range(8):
        gb_[:, c] = p_ + c * 128
        gs_[:, c] = 1024.0
    for c in range(32):
        gb_[:, 8 + c] = p_ + c * 128
        gs_[:, 8 + c] = 4096.0
    gb_[0:32, 40] = np.arange(32)
    gs_[0:32, 40] = 32.0
    t["gidx_base"] = gb_
    t["gscale"] = gs_
    return t


class Sem:
    def __init__(self, stack, nc, name):
        self.h = stack.enter_context(nc.semaphore(name))
        self.n = 0
        self.name = name

    def bump(self, k=1):
        self.n += k
        return self.n

    def tok(self):
        return (self, self.n)


class Phase:

    def __init__(self, nc):
        self.nc = nc
        self.q = {e: [] for e in ENGS}

    def op(self, eng, sem, deps, fn, inc=1):
        waits = [(s.h, v) for t_ in deps if t_ is not None for (s, v) in [t_] if v > 0]
        tok = None
        h = None
        if sem is not None:
            tok = (sem, sem.bump(inc))
            h = sem.h

        def th(E, waits=waits, h=h, inc=inc, fn=fn):
            for hh, vv in waits:
                E.wait_ge(hh, vv)
            ins = fn(E)
            if h is not None and ins is not None:
                ins.then_inc(h, inc)
        self.q[eng].append(th)
        return tok

    def run(self, name):
        with self.nc.Block(name) as blk:
            for eng in ENGS:
                ops = self.q[eng]
                if ops:
                    getattr(blk, eng)(lambda E, ops=ops: [f(E) for f in ops])


def build_nc(debug=False):
    nc = bass.Bass("TRN2", target_bir_lowering=False)
    T = _consts()

    def din(name, shape):
        return nc.dram_tensor(name, shape, F32, kind="ExternalInput").ap()

    x = din("x", [NT, D])
    uv_w = din("uv_w", [D, 2 * D])
    o_w = din("o_w", [D, D])
    rpe_ln_w = din("rpe_ln_w", [3, 32])
    rpe_ln_b = din("rpe_ln_b", [3, 32])
    rpe_fc_w = din("rpe_fc_w", [3, 32, 32])
    rpe_oln_w = din("rpe_oln_w", [1, 32])
    rpe_oln_b = din("rpe_oln_b", [1, 32])
    rpe_pos_w = din("rpe_pos_w", [1, 32])
    rpe_out_w = din("rpe_out_w", [32, D])
    W1 = din("W1", [NE, D, FF])
    b1 = din("b1", [NE, FF])
    W2 = din("W2", [NE, FF, D])
    b2 = din("b2", [NE, D])
    Wr = din("Wr", [D, NE])
    br = din("br", [1, NE])
    gamma = din("gamma", [1, D])
    beta = din("beta", [1, D])
    out = nc.dram_tensor("out", [NT, D], F32, kind="ExternalOutput").ap()

    sk = "ExternalOutput" if debug else "Internal"
    xn_dram = nc.dram_tensor("xn_dram", [NT, D], F32, kind=sk).ap()
    u_dram = nc.dram_tensor("u_dram", [D, NT], F32, kind=sk).ap()
    v_dram = nc.dram_tensor("v_dram", [D, NT], BF16, kind=sk).ap()
    t_dram = nc.dram_tensor("t_dram", [D, M], BF16, kind=sk).ap()
    y_dram = nc.dram_tensor("y_dram", [D, NT], F32, kind=sk).ap()
    s_dram = nc.dram_tensor("s_dram", [NT, D], F32, kind=sk).ap()
    moe_dram = nc.dram_tensor("moe_dram", [NT, D], F32, kind=sk).ap()
    xnT_dram = nc.dram_tensor("xnT_dram", [128, 8 * NT], BF16, kind="Internal").ap()
    b2mix_dram = nc.dram_tensor("b2mix_dram", [1, D], F32, kind="Internal").ap()

    ci = {k: nc.inline_tensor(v, f"c_{k}").ap() for k, v in T.items()}

    st = contextlib.ExitStack()
    with st:
        def sb(stk, name, shape, dt):
            return stk.enter_context(nc.sbuf_tensor(name, shape, dt))

        xnT_stack = contextlib.ExitStack()
        xnT = xnT_stack.enter_context(nc.sbuf_tensor("xnT", [128, 8 * NT], BF16, side="right"))
        C = {}
        for k, shape, dt in [
            ("f1c", [128, 128], BF16), ("f1s", [128, 128], BF16),
            ("bdec", [128, 128], BF16), ("bdes", [128, 128], BF16),
            ("bdesn", [128, 128], BF16), ("c2c", [128, 64], BF16),
            ("c2s", [128, 64], BF16), ("twc", [128, 128], BF16),
            ("tws", [128, 128], BF16), ("tw2c", [128, 128], BF16),
            ("tw2s", [128, 128], BF16), ("identf", [128, 128], F32),
            ("identb", [128, 128], BF16), ("onesrow", [1, 128], F32),
            ("ones11", [1, 1], F32), ("onescol_b", [128, 1], BF16),
            ("iota8", [1, 8], F32), ("gidx_base", [128, 48], F32),
            ("gscale", [128, 48], F32),
        ]:
            C[k] = sb(st, "cc_" + k, shape, dt)
        ebc = sb(st, "ebc", [128, 2], F32)
        cwslbc = sb(st, "cwslbc", [128, 2], F32)

        sP = Sem(st, nc, "sP")
        sA = Sem(st, nc, "sA")
        sV = Sem(st, nc, "sV")
        S = {}

        def dsem(name):
            if name not in S:
                S[name] = Sem(st, nc, "d_" + name)
            return S[name]

        ph = Phase(nc)
        cst = dsem("cst")
        const_ld = None
        for k in C:
            const_ld = ph.op("sync", cst, [], lambda sy, k=k: sy.dma_start(C[k][:], ci[k]), 16)

        with contextlib.ExitStack() as pp:
            ps_xsum = pp.enter_context(nc.psum_tensor("ps_xsum", [1, 1024], F32))
            ps_tr = [pp.enter_context(nc.psum_tensor(f"ps_tr{i}", [128, 128], BF16))
                     for i in range(2)]
            with contextlib.ExitStack() as sb0:
                xt = [sb(sb0, f"xt{i}", [128, D], F32) for i in range(2)]
                xnb = [sb(sb0, f"xnb{i}", [128, D], BF16) for i in range(2)]
                xnf = [sb(sb0, f"xnf{i}", [128, D], F32) for i in range(2)]
                stt = [sb(sb0, f"stt{i}", [128, 8], F32) for i in range(2)]
                ph.run("p0a")
                ph = Phase(nc)
                xts = [dsem("xt0"), dsem("xt1")]
                xns = [dsem("xns0"), dsem("xns1")]
                tile_pe_done = {}
                tile_store = {}
                ev_hist = []
                r8_hist = {}

                def p0_chain(t_):
                    bi = t_ % 2
                    deps = [tile_pe_done.get(t_ - 2), r8_hist.get(t_ - 2)] if t_ >= 2 else []
                    ld = ph.op("sync", xts[bi], deps,
                               lambda sy, t_=t_, bi=bi: sy.dma_start(
                                   xt[bi][:], x[t_ * 128:(t_ + 1) * 128, :]), 16)
                    yield
                    r1 = ph.op("vector", sV, [ld],
                               lambda v, bi=bi: v.tensor_reduce(stt[bi][:, 0:1], xt[bi][:], AX.X, ALU.add))
                    sqdeps = [ld]
                    if t_ >= 2:
                        sqdeps.append(tile_store[t_ - 2])
                    a1 = ph.op("scalar", sA, sqdeps,
                               lambda sc, bi=bi: sc.activation(xnf[bi][:], xt[bi][:], AF.Square,
                                                               accum_out=stt[bi][:, 1:2]))
                    yield
                    r2 = ph.op("vector", sV, [r1],
                               lambda v, bi=bi: v.tensor_scalar_mul(stt[bi][:, 2:3], stt[bi][:, 0:1], 1.0 / D))
                    yield
                    r3 = ph.op("vector", sV, [a1, r2],
                               lambda v, bi=bi: v.tensor_scalar(stt[bi][:, 3:4], stt[bi][:, 1:2], 1.0 / D, EPS,
                                                                ALU.mult, ALU.add))
                    yield
                    r4 = ph.op("vector", sV, [r2],
                               lambda v, bi=bi: v.tensor_tensor(stt[bi][:, 4:5], stt[bi][:, 2:3],
                                                                stt[bi][:, 2:3], ALU.mult))
                    yield
                    r5 = ph.op("vector", sV, [r4, r3],
                               lambda v, bi=bi: v.tensor_tensor(stt[bi][:, 3:4], stt[bi][:, 3:4],
                                                                stt[bi][:, 4:5], ALU.subtract))
                    yield
                    a2 = ph.op("scalar", sA, [r5],
                               lambda sc, bi=bi: sc.activation(stt[bi][:, 5:6], stt[bi][:, 3:4], AF.Sqrt))
                    yield
                    r6 = ph.op("vector", sV, [a2],
                               lambda v, bi=bi: v.reciprocal(stt[bi][:, 6:7], stt[bi][:, 5:6]))
                    yield
                    r7 = ph.op("vector", sV, [r6],
                               lambda v, bi=bi: v.tensor_scalar(xnb[bi][:], xt[bi][:], stt[bi][:, 2:3],
                                                                stt[bi][:, 6:7], ALU.subtract, ALU.mult))
                    yield
                    r8 = ph.op("vector", sV, [r7],
                               lambda v, bi=bi: v.tensor_scalar(xnf[bi][:], xt[bi][:], stt[bi][:, 2:3],
                                                                stt[bi][:, 6:7], ALU.subtract, ALU.mult))
                    r8_hist[t_] = r8
                    tile_store[t_] = ph.op("scalar", xns[bi], [r8],
                                           lambda sc, t_=t_, bi=bi: sc.dma_start(
                                               xn_dram[t_ * 128:(t_ + 1) * 128, :], xnf[bi][:]), 16)
                    yield
                    for db in range(8):
                        pdeps = [r7]
                        if not ev_hist:
                            pdeps.append(const_ld)
                        if len(ev_hist) >= 2:
                            pdeps.append(ev_hist[-2])
                        pm = ph.op("tensor", sP, pdeps,
                                   lambda t, bi=bi, db=db: t.transpose(
                                       ps_tr[db % 2][:], xnb[bi][:, db * 128:(db + 1) * 128],
                                       C["identb"][:]))
                        ev_hist.append(ph.op("scalar", sA, [pm],
                                             lambda sc, t_=t_, db=db: sc.activation(
                                                 xnT[:, db * NT + t_ * 128: db * NT + (t_ + 1) * 128],
                                                 ps_tr[db % 2][:], AF.Copy)))
                    ph.op("tensor", sP, [r7, const_ld],
                          lambda t, t_=t_, bi=bi: t.matmul(
                              ps_xsum[0:1, 0:512], C["onescol_b"][:], xnb[bi][:, 0:512],
                              start=(t_ == 0), stop=(t_ == 31)))
                    m2 = ph.op("tensor", sP, [],
                               lambda t, t_=t_, bi=bi: t.matmul(
                                   ps_xsum[0:1, 512:1024], C["onescol_b"][:], xnb[bi][:, 512:1024],
                                   start=(t_ == 0), stop=(t_ == 31)))
                    tile_pe_done[t_] = m2

                prev = None
                for pr in range(16):
                    ga = p0_chain(2 * pr)
                    gb = p0_chain(2 * pr + 1)
                    alive = [ga, gb]
                    while alive:
                        nxt = []
                        for g_ in alive:
                            try:
                                next(g_)
                                nxt.append(g_)
                            except StopIteration:
                                pass
                        alive = nxt
                xsum_mm = tile_pe_done[31]
                xn_stores_l = [xns[0].tok(), xns[1].tok()]
                ph.run("p0b")

            with contextlib.ExitStack() as sb1:
                xsum_sb = sb(sb1, "xsum_sb", [1, 1024], F32)
                xmT = sb(sb1, "xmT", [128, 8], F32)
                wr_sb = sb(sb1, "wr_sb", [128, 64], F32)
                b2sb = sb(sb1, "b2sb", [8, 1024], F32)
                brr = sb(sb1, "brr", [1, 8], F32)
                rw = sb(sb1, "rw", [1, 8], F32)
                rt = sb(sb1, "rt", [1, 8], F32)
                eq1 = sb(sb1, "eq1", [1, 8], F32)
                eq2 = sb(sb1, "eq2", [1, 8], F32)
                s4 = sb(sb1, "s4", [1, 4], F32)
                cw8 = sb(sb1, "cw8", [1, 8], F32)
                cw8T = sb(sb1, "cw8T", [8, 1], F32)
                b2mix = sb(sb1, "b2mix", [1, D], F32)
                eidx = sb(sb1, "eidx", [1, 2], F32)
                cwrow = sb(sb1, "cwrow", [1, 2], F32)
                with contextlib.ExitStack() as pp1:
                    ps_m = pp1.enter_context(nc.psum_tensor("ps_m", [128, 512], F32))
                    ps_r = pp1.enter_context(nc.psum_tensor("ps_r", [1, 512], F32))
                    ps_c8 = pp1.enter_context(nc.psum_tensor("ps_c8", [8, 8], F32))
                    ph = Phase(nc)
                    p1s = dsem("p1s")
                    for apdst, apsrc in [(wr_sb[:], Wr.rearrange("(c p) e -> p c e", p=128)),
                                         (b2sb[:], b2), (brr[:], br)]:
                        p1_all = ph.op("sync", p1s, xn_stores_l,
                                       lambda sy, apdst=apdst, apsrc=apsrc: sy.dma_start(apdst, apsrc), 16)
                    e1 = ph.op("vector", sV, [xsum_mm] + xn_stores_l,
                               lambda v: v.tensor_copy(xsum_sb[:, 0:512], ps_xsum[0:1, 0:512]))
                    e2 = ph.op("vector", sV, [e1],
                               lambda v: v.tensor_copy(xsum_sb[:, 512:1024], ps_xsum[0:1, 512:1024]))
                    mt = None
                    for k in range(8):
                        mt = ph.op("tensor", sP, [e2, const_ld],
                                   lambda t, k=k: t.matmul(ps_m[:, k:k + 1],
                                                           xsum_sb[0:1, k * 128:(k + 1) * 128],
                                                           C["ones11"][:], start=True, stop=True))
                    v1 = ph.op("vector", sV, [mt],
                               lambda v: v.tensor_scalar_mul(xmT[:], ps_m[:, 0:8], 1.0 / NT))
                    rmm = None
                    for k in range(8):
                        rmm = ph.op("tensor", sP, [v1, p1_all],
                                    lambda t, k=k: t.matmul(ps_r[0:1, 0:8], xmT[:, k:k + 1],
                                                            wr_sb[:, k * 8:(k + 1) * 8],
                                                            start=(k == 0), stop=(k == 7)))
                    c1 = ph.op("vector", sV, [rmm],
                               lambda v: v.tensor_copy(rw[:], ps_r[0:1, 0:8]))
                    c2 = ph.op("vector", sV, [c1],
                               lambda v: v.tensor_tensor(rw[:], rw[:], brr[:], ALU.add))
                    c3 = ph.op("vector", sV, [c2],
                               lambda v: v.tensor_reduce(s4[:, 0:1], rw[:], AX.X, ALU.max))
                    c4 = ph.op("vector", sV, [c3],
                               lambda v: v.tensor_scalar(rt[:], rw[:], s4[:, 0:1], None, ALU.subtract))
                    c5 = ph.op("scalar", sA, [c4],
                               lambda sc: sc.activation(rt[:], rt[:], AF.Exp))
                    c6 = ph.op("vector", sV, [c5],
                               lambda v: v.tensor_reduce(s4[:, 1:2], rt[:], AX.X, ALU.add))
                    c7 = ph.op("vector", sV, [c6],
                               lambda v: v.reciprocal(s4[:, 2:3], s4[:, 1:2]))
                    c8 = ph.op("vector", sV, [c7],
                               lambda v: v.tensor_scalar(rw[:], rt[:], s4[:, 2:3], None, ALU.mult))
                    d1 = ph.op("vector", sV, [c8],
                               lambda v: v.tensor_reduce(s4[:, 0:1], rw[:], AX.X, ALU.max))
                    d2 = ph.op("vector", sV, [d1],
                               lambda v: v.tensor_scalar(eq1[:], rw[:], s4[:, 0:1], None, ALU.is_equal))
                    d3 = ph.op("vector", sV, [d2],
                               lambda v: v.tensor_scalar_mul(rt[:], eq1[:], -3.0))
                    d4 = ph.op("vector", sV, [d3],
                               lambda v: v.tensor_tensor(rt[:], rw[:], rt[:], ALU.add))
                    d5 = ph.op("vector", sV, [d4],
                               lambda v: v.tensor_reduce(s4[:, 1:2], rt[:], AX.X, ALU.max))
                    d6 = ph.op("vector", sV, [d5],
                               lambda v: v.tensor_scalar(eq2[:], rt[:], s4[:, 1:2], None, ALU.is_equal))
                    i1 = ph.op("vector", sV, [d6],
                               lambda v: v.tensor_tensor(rt[:], eq1[:], C["iota8"][:], ALU.mult))
                    i2 = ph.op("vector", sV, [i1],
                               lambda v: v.tensor_reduce(eidx[:, 0:1], rt[:], AX.X, ALU.add))
                    i3 = ph.op("vector", sV, [i2],
                               lambda v: v.tensor_tensor(rt[:], eq2[:], C["iota8"][:], ALU.mult))
                    i4 = ph.op("vector", sV, [i3],
                               lambda v: v.tensor_reduce(eidx[:, 1:2], rt[:], AX.X, ALU.add))
                    d7 = ph.op("vector", sV, [i4],
                               lambda v: v.tensor_tensor(s4[:, 2:3], s4[:, 1:2], s4[:, 0:1], ALU.subtract))
                    d8 = ph.op("scalar", sA, [d7],
                               lambda sc: sc.activation(s4[:, 2:3], s4[:, 2:3], AF.Exp))
                    d9 = ph.op("vector", sV, [d8],
                               lambda v: v.tensor_scalar(s4[:, 3:4], s4[:, 2:3], 1.0, None, ALU.add))
                    d10 = ph.op("vector", sV, [d9],
                                lambda v: v.reciprocal(s4[:, 3:4], s4[:, 3:4]))
                    d11 = ph.op("vector", sV, [d10],
                                lambda v: v.tensor_scalar(eq1[:], eq1[:], s4[:, 3:4], None, ALU.mult))
                    d12 = ph.op("vector", sV, [d11],
                                lambda v: v.tensor_scalar(s4[:, 2:3], s4[:, 3:4], -1.0, 1.0,
                                                          ALU.mult, ALU.add))
                    d13 = ph.op("vector", sV, [d12],
                                lambda v: v.tensor_scalar(eq2[:], eq2[:], s4[:, 2:3], None, ALU.mult))
                    d14 = ph.op("vector", sV, [d13],
                                lambda v: v.tensor_tensor(cw8[:], eq1[:], eq2[:], ALU.add))
                    d15 = ph.op("vector", sV, [d12],
                                lambda v: v.tensor_copy(cwrow[:, 0:1], s4[:, 3:4]))
                    d16 = ph.op("vector", sV, [d15],
                                lambda v: v.tensor_copy(cwrow[:, 1:2], s4[:, 2:3]))
                    g1 = ph.op("tensor", sP, [d16],
                               lambda t: t.matmul(ps_m[:, 0:2], C["onesrow"][:], cwrow[:],
                                                  start=True, stop=True))
                    g1b = ph.op("tensor", sP, [d16],
                                lambda t: t.matmul(ps_m[:, 8:10], C["onesrow"][:], eidx[:],
                                                   start=True, stop=True))
                    g2 = ph.op("tensor", sP, [d14],
                               lambda t: t.matmul(ps_c8[:, 0:1], cw8[:], C["ones11"][:],
                                                  start=True, stop=True))
                    g3 = ph.op("vector", sV, [g1b],
                               lambda v: v.tensor_copy(cwslbc[:], ps_m[:, 0:2]))
                    g3b = ph.op("vector", sV, [g3],
                                lambda v: v.tensor_copy(ebc[:], ps_m[:, 8:10]))
                    g4 = ph.op("vector", sV, [g3b, g2],
                               lambda v: v.tensor_copy(cw8T[:], ps_c8[:, 0:1]))
                    h1m = ph.op("tensor", sP, [g4],
                                lambda t: t.matmul(ps_r[0:1, 0:512], cw8T[:], b2sb[:, 0:512],
                                                   start=True, stop=True))
                    h1e = ph.op("vector", sV, [h1m],
                                lambda v: v.tensor_copy(b2mix[:, 0:512], ps_r[0:1, 0:512]))
                    h2m = ph.op("tensor", sP, [h1e],
                                lambda t: t.matmul(ps_r[0:1, 0:512], cw8T[:], b2sb[:, 512:1024],
                                                   start=True, stop=True))
                    h2e = ph.op("vector", sV, [h2m],
                                lambda v: v.tensor_copy(b2mix[:, 512:1024], ps_r[0:1, 0:512]))
                    b2s = dsem("b2s")
                    b2mix_store = ph.op("sync", b2s, [h2e],
                                        lambda sy: sy.dma_start(b2mix_dram, b2mix[:]), 16)
                    ph.run("p1")

        with contextlib.ExitStack() as pp:
            ps_uv = [pp.enter_context(nc.psum_tensor(f"ps_uv{i}", [128, 512], F32)) for i in range(2)]
            with contextlib.ExitStack() as sb2:
                wt = [sb(sb2, f"uvw{i}", [128, 8 * 128], BF16) for i in range(2)]
                ev = [sb(sb2, f"uvev{i}", [128, 512], F32) for i in range(2)]
                evb = [sb(sb2, f"uvevb{i}", [128, 512], BF16) for i in range(2)]
                ph = Phase(nc)
                wts = [dsem("wt0"), dsem("wt1")]
                evs_s = [dsem("evs0"), dsem("evs1")]
                evbs_s = [dsem("evbs0"), dsem("evbs1")]
                mm_last = {}
                ev_hist2 = {}
                st_hist = {}
                for fb in range(16):
                    wdeps = [mm_last.get(fb - 2)] if fb >= 2 else [b2mix_store]
                    wl = ph.op("gpsimd", wts[fb % 2], wdeps,
                               lambda g, fb=fb: g.dma_start(
                                   wt[fb % 2][:],
                                   uv_w[:, fb * 128:(fb + 1) * 128].rearrange("(c p) f -> p c f", p=128)), 16)
                    for tc in range(8):
                        idx = fb * 8 + tc
                        mdeps = [wl]
                        if idx >= 2:
                            mdeps.append(ev_hist2[idx - 2])
                        last = None
                        for dc in range(8):
                            last = ph.op("tensor", sP, mdeps if dc == 0 else [],
                                         lambda t, fb=fb, tc=tc, dc=dc: t.matmul(
                                             ps_uv[tc % 2][:], wt[fb % 2][:, dc * 128:(dc + 1) * 128],
                                             xnT[:, dc * NT + tc * 512: dc * NT + (tc + 1) * 512],
                                             start=(dc == 0), stop=(dc == 7)))
                        mm_last[fb] = last
                        edeps = [last]
                        if idx >= 2:
                            edeps.append(st_hist[idx - 2])
                        dst = ev if fb < 8 else evb
                        em = ph.op("scalar", sA, edeps,
                                   lambda sc, tc=tc, dst=dst: sc.activation(
                                       dst[tc % 2][:], ps_uv[tc % 2][:], AF.Silu))
                        if fb < 8:
                            stm = ph.op("sync", evs_s[tc % 2], [em],
                                        lambda sy, fb=fb, tc=tc: sy.dma_start(
                                            u_dram[fb * 128:(fb + 1) * 128, tc * 512:(tc + 1) * 512],
                                            ev[tc % 2][:]), 16)
                        else:
                            stm = ph.op("sync", evbs_s[tc % 2], [em],
                                        lambda sy, fb=fb, tc=tc: sy.dma_start(
                                            v_dram[(fb - 8) * 128:(fb - 7) * 128, tc * 512:(tc + 1) * 512],
                                            evb[tc % 2][:]), 16)
                        ev_hist2[idx] = em
                        st_hist[idx] = stm
                u_stores = [evs_s[0].tok(), evs_s[1].tok()]
                v_stores = [evbs_s[0].tok(), evbs_s[1].tok()]
                ph.run("p2")

        with contextlib.ExitStack() as pp:
            ps_t32 = [pp.enter_context(nc.psum_tensor(f"ps_t32_{i}", [32, 128], F32)) for i in range(2)]
            ps_h32 = [pp.enter_context(nc.psum_tensor(f"ps_h32_{i}", [128, 32], F32)) for i in range(2)]
            ps_tg = [pp.enter_context(nc.psum_tensor(f"ps_tg{i}", [128, 512], F32)) for i in range(2)]
            with contextlib.ExitStack() as sb3:
                posx = sb(sb3, "posx", [128, 2048], F32)
                decayx = sb(sb3, "decayx", [128, 2048], F32)
                h1 = sb(sb3, "h1", [128, 2048], F32)
                h2 = sb(sb3, "h2", [128, 2048], F32)
                hsq = sb(sb3, "hsq", [128, 2048], F32)
                st64 = sb(sb3, "st64", [128, 256], F32)
                hT = sb(sb3, "hT", [32, M], BF16)
                fcw = sb(sb3, "fcw", [32, 96], BF16)
                woutb = sb(sb3, "woutb", [32, D], BF16)
                lnrow = sb(sb3, "lnrow", [1, 9 * 32], F32)
                lnbc = sb(sb3, "lnbc", [128, 9 * 32], F32)
                tg = [sb(sb3, f"tg{i}", [128, 512], BF16) for i in range(2)]
                ph = Phase(nc)
                p3s = dsem("p3s")
                xds = dsem("xds")
                tgs = [dsem("tgs0"), dsem("tgs1")]
                xnT_dump = None
                for k in range(8):
                    xnT_dump = ph.op("sync", xds, [],
                                     lambda sy, k=k: sy.dma_start(
                                         xnT_dram[:, k * NT:(k + 1) * NT],
                                         xnT[:, k * NT:(k + 1) * NT]), 16)
                ph.op("sync", p3s, u_stores + v_stores,
                      lambda sy: sy.dma_start(posx[:], ci["posx"]), 16)
                ph.op("sync", p3s, [], lambda sy: sy.dma_start(decayx[:], ci["decayx"]), 16)
                p3g = dsem("p3g")
                for i in range(3):
                    ph.op("gpsimd", p3g, u_stores + v_stores if i == 0 else [],
                          lambda g, i=i: g.dma_start(fcw[:, i * 32:(i + 1) * 32],
                                                     rpe_fc_w[i]), 16)
                ph.op("gpsimd", p3g, [], lambda g: g.dma_start(woutb[:], rpe_out_w), 16)
                rows = [(0, rpe_ln_w, 0), (1, rpe_ln_w, 1), (2, rpe_ln_w, 2),
                        (3, rpe_ln_b, 0), (4, rpe_ln_b, 1), (5, rpe_ln_b, 2),
                        (6, rpe_oln_w, None), (7, rpe_oln_b, None), (8, rpe_pos_w, None)]
                p3_all = None
                for slot, src, ri in rows:
                    ap_src = src[ri:ri + 1, :] if ri is not None else src
                    p3_all = ph.op("sync", p3s, [], lambda sy, slot=slot, ap_src=ap_src: sy.dma_start(
                        lnrow[:, slot * 32:(slot + 1) * 32], ap_src), 16)
                bcd = None
                for slot in range(9):
                    bdeps = [p3_all]
                    if bcd is not None:
                        bdeps.append(bcd)
                    bm = ph.op("tensor", sP, bdeps,
                               lambda t, slot=slot: t.matmul(ps_tg[slot % 2][:, 0:32], C["onesrow"][:],
                                                             lnrow[0:1, slot * 32:(slot + 1) * 32],
                                                             start=True, stop=True))
                    bcd = ph.op("vector", sV, [bm],
                                lambda v, slot=slot: v.tensor_copy(lnbc[:, slot * 32:(slot + 1) * 32],
                                                                   ps_tg[slot % 2][:, 0:32]))

                def r3ap(t_ap):
                    return t_ap.rearrange("p (g j) -> p g j", j=32)

                def ln_silu(hin, wslot, bslot, dep0):
                    q1 = ph.op("vector", sV, [dep0, bcd],
                               lambda v: v.tensor_reduce(st64[:, 0:64], r3ap(hin.ap()), AX.X, ALU.add))
                    q2 = ph.op("vector", sV, [q1],
                               lambda v: v.tensor_scalar_mul(st64[:, 0:64], st64[:, 0:64], 1.0 / 32))
                    q3 = ph.op("vector", sV, [q2],
                               lambda v: v.tensor_tensor(
                                   r3ap(hin.ap()), r3ap(hin.ap()),
                                   st64[:, 0:64].unsqueeze(2).broadcast_to([128, 64, 32]), ALU.subtract))
                    q4 = ph.op("scalar", sA, [q3],
                               lambda sc: sc.activation(hsq[:], hin[:], AF.Square))
                    q5 = ph.op("vector", sV, [q4],
                               lambda v: v.tensor_reduce(st64[:, 64:128], r3ap(hsq.ap()), AX.X, ALU.add))
                    q6 = ph.op("vector", sV, [q5],
                               lambda v: v.tensor_scalar(st64[:, 64:128], st64[:, 64:128], 1.0 / 32,
                                                         EPS, ALU.mult, ALU.add))
                    q7 = ph.op("scalar", sA, [q6],
                               lambda sc: sc.activation(st64[:, 128:192], st64[:, 64:128], AF.Sqrt))
                    q8 = ph.op("vector", sV, [q7],
                               lambda v: v.reciprocal(st64[:, 192:256], st64[:, 128:192]))
                    q9 = ph.op("vector", sV, [q8],
                               lambda v: v.tensor_tensor(
                                   r3ap(hin.ap()), r3ap(hin.ap()),
                                   st64[:, 192:256].unsqueeze(2).broadcast_to([128, 64, 32]), ALU.mult))
                    q10 = ph.op("vector", sV, [q9],
                                lambda v: v.tensor_tensor(
                                    r3ap(hin.ap()), r3ap(hin.ap()),
                                    lnbc[:, wslot * 32:(wslot + 1) * 32].unsqueeze(1).broadcast_to([128, 64, 32]),
                                    ALU.mult))
                    q11 = ph.op("vector", sV, [q10],
                                lambda v: v.tensor_tensor(
                                    r3ap(hin.ap()), r3ap(hin.ap()),
                                    lnbc[:, bslot * 32:(bslot + 1) * 32].unsqueeze(1).broadcast_to([128, 64, 32]),
                                    ALU.add))
                    q12 = ph.op("scalar", sA, [q11],
                                lambda sc: sc.activation(hin[:], hin[:], AF.Silu))
                    return q12

                h0 = ph.op("vector", sV, [bcd, p3_all, p3g.tok()],
                           lambda v: v.tensor_tensor(
                               r3ap(h1.ap()), r3ap(posx.ap()),
                               lnbc[:, 8 * 32:9 * 32].unsqueeze(1).broadcast_to([128, 64, 32]), ALU.mult))
                dep = h0
                hin = h1
                for layer in range(3):
                    hout = h2 if hin is h1 else h1
                    sdone = ln_silu(hin, layer, 3 + layer, dep)
                    evl = {}
                    for pt in range(64):
                        pd = [sdone] if pt == 0 else []
                        if pt >= 2:
                            pd.append(evl[pt - 2])
                        pm = ph.op("tensor", sP, pd,
                                   lambda t, pt=pt, hin=hin: t.transpose(
                                       ps_t32[pt % 2][:], hin[:, pt * 32:(pt + 1) * 32], C["identf"][:]))
                        evl[pt] = ph.op("scalar", sA, [pm],
                                        lambda sc, pt=pt: sc.activation(
                                            hT[:, pt * 128:(pt + 1) * 128], ps_t32[pt % 2][:], AF.Copy))
                    ev2 = {}
                    for pt in range(64):
                        pd = [evl[63]] if pt == 0 else []
                        if pt >= 2:
                            pd.append(ev2[pt - 2])
                        pm = ph.op("tensor", sP, pd,
                                   lambda t, pt=pt, layer=layer: t.matmul(
                                       ps_h32[pt % 2][:], hT[:, pt * 128:(pt + 1) * 128],
                                       fcw[:, layer * 32:(layer + 1) * 32], start=True, stop=True))
                        ev2[pt] = ph.op("scalar", sA, [pm],
                                        lambda sc, pt=pt, hout=hout: sc.activation(
                                            hout[:, pt * 32:(pt + 1) * 32], ps_h32[pt % 2][:], AF.Copy))
                    dep = ev2[63]
                    hin = hout
                sdone = ln_silu(hin, 6, 7, dep)
                pg = ph.op("vector", sV, [sdone],
                           lambda v, hin=hin: v.tensor_tensor(hin[:], hin[:], decayx[:], ALU.mult))
                evl = {}
                for pt in range(64):
                    pd = [pg] if pt == 0 else []
                    if pt >= 2:
                        pd.append(evl[pt - 2])
                    pm = ph.op("tensor", sP, pd,
                               lambda t, pt=pt, hin=hin: t.transpose(
                                   ps_t32[pt % 2][:], hin[:, pt * 32:(pt + 1) * 32], C["identf"][:]))
                    evl[pt] = ph.op("scalar", sA, [pm],
                                    lambda sc, pt=pt: sc.activation(
                                        hT[:, pt * 128:(pt + 1) * 128], ps_t32[pt % 2][:], AF.Copy))
                tev = {}
                tstv = {}
                for cb in range(8):
                    for mc2 in range(16):
                        i = cb * 16 + mc2
                        pd = [evl[63]] if i == 0 else []
                        if i >= 2:
                            pd.append(tev[i - 2])
                        pm = ph.op("tensor", sP, pd,
                                   lambda t, cb=cb, mc2=mc2, i=i: t.matmul(
                                       ps_tg[i % 2][:], woutb[:, cb * 128:(cb + 1) * 128],
                                       hT[:, mc2 * 512:(mc2 + 1) * 512], start=True, stop=True))
                        ed = [pm]
                        if i >= 2:
                            ed.append(tstv[i - 2])
                        em = ph.op("scalar", sA, ed,
                                   lambda sc, i=i: sc.activation(tg[i % 2][:], ps_tg[i % 2][:], AF.Copy))
                        sm = ph.op("sync", tgs[i % 2], [em],
                                   lambda sy, cb=cb, mc2=mc2, i=i: sy.dma_start(
                                       t_dram[cb * 128:(cb + 1) * 128, mc2 * 512:(mc2 + 1) * 512],
                                       tg[i % 2][:]), 16)
                        tev[i] = em
                        tstv[i] = sm
                t_stores = [tgs[0].tok(), tgs[1].tok()]
                xd_total = xds.tok()
                ph.run("p3")

        xnT_stack.close()
        with contextlib.ExitStack() as pp:
            psE4 = [pp.enter_context(nc.psum_tensor(f"psE{i}", [128, 512], F32)) for i in range(4)]
            psY4 = pp.enter_context(nc.psum_tensor("psY4", [64, 2048], F32))
            with contextlib.ExitStack() as sb4:
                vsl = [sb(sb4, f"vsl{i}", [64, 4096], BF16) for i in range(2)]
                tsl = [sb(sb4, f"tsl{i}", [128, 4096], BF16) for i in range(2)]
                bA = [[sb(sb4, f"bA{p}_{i}", [128, 4096], BF16) for i in range(2)] for p in range(2)]
                bB = [[sb(sb4, f"bB{p}_{i}", [128, 4096], BF16) for i in range(2)] for p in range(2)]
                bC = [[sb(sb4, f"bC{p}_{i}", [128, 4096], BF16) for i in range(2)] for p in range(2)]
                ysb = [sb(sb4, f"ysb{p}", [64, 4096], F32) for p in range(2)]
                ph = Phase(nc)
                vls = [dsem("vl0"), dsem("vl1")]
                tls = [dsem("tl0"), dsem("tl1")]
                yss = dsem("ys")
                st_ = {}
                ydma_hist = {}
                g3 = lambda ap_: ap_.rearrange("p (g j) -> p g j", j=128)
                twcB = C["twc"].ap().unsqueeze(1).broadcast_to([128, 32, 128])
                twsB = C["tws"].ap().unsqueeze(1).broadcast_to([128, 32, 128])
                t2cB = C["tw2c"].ap().unsqueeze(1).broadcast_to([128, 32, 128])
                t2sB = C["tw2s"].ap().unsqueeze(1).broadcast_to([128, 32, 128])

                def step1(slab, nb, ldtok, dst, psfree, bufree):
                    evs = []
                    last = None
                    for gb in range(8):
                        pe0 = psE4[2 * (gb % 2)]
                        pe1 = psE4[2 * (gb % 2) + 1]
                        for g4 in range(4):
                            g = gb * 4 + g4
                            dep = []
                            if gb == 0 and g4 == 0:
                                dep = [ldtok] + psfree
                            if g4 == 0 and gb >= 2:
                                dep.append(evs[gb - 2])
                            lhs = slab.ap().rearrange("b (d a) -> b d a", a=64)[:, 2 * g:2 * g + 2, :]
                            ph.op("tensor", sP, dep,
                                  lambda t, lhs=lhs, g4=g4, nb=nb, pe0=pe0: t.matmul(
                                      pe0[:, g4 * 128:(g4 + 1) * 128], lhs,
                                      C["f1c"][0:nb, :], start=True, stop=True))
                            last = ph.op("tensor", sP, [],
                                         lambda t, lhs=lhs, g4=g4, nb=nb, pe1=pe1: t.matmul(
                                             pe1[:, g4 * 128:(g4 + 1) * 128], lhs,
                                             C["f1s"][0:nb, :], start=True, stop=True))
                        ed0 = [last] + (bufree if gb == 0 else [])
                        ph.op("scalar", sA, ed0,
                              lambda sc, gb=gb, dst=dst, pe0=pe0: sc.activation(
                                  dst[0][:, gb * 512:(gb + 1) * 512], pe0[:], AF.Copy))
                        e2_ = ph.op("scalar", sA, [],
                                    lambda sc, gb=gb, dst=dst, pe1=pe1: sc.activation(
                                        dst[1][:, gb * 512:(gb + 1) * 512], pe1[:], AF.Copy))
                        evs.append(e2_)
                    return last, evs[-1]

                def cmul6(deps, ar, ai, br_c, bs_c, outr, outi, tmp):
                    w1 = ph.op("vector", sV, deps,
                               lambda v: v.tensor_tensor(g3(outr.ap()), g3(ar.ap()), br_c, ALU.mult))
                    w2 = ph.op("vector", sV, [w1],
                               lambda v: v.tensor_tensor(g3(outi.ap()), g3(ai.ap()), bs_c, ALU.mult))
                    w3 = ph.op("vector", sV, [w2],
                               lambda v: v.tensor_tensor(outr[:], outr[:], outi[:], ALU.subtract))
                    w4 = ph.op("vector", sV, [w3],
                               lambda v: v.tensor_tensor(g3(outi.ap()), g3(ar.ap()), bs_c, ALU.mult))
                    w5 = ph.op("vector", sV, [w4],
                               lambda v: v.tensor_tensor(g3(ar.ap()), g3(ai.ap()), br_c, ALU.mult))
                    w6 = ph.op("vector", sV, [w5],
                               lambda v: v.tensor_tensor(outi[:], outi[:], ar[:], ALU.add))
                    return w6, w5

                def stepC(src, dst, dep0):
                    evs = []
                    for gb in range(8):
                        pe0 = psE4[2 * (gb % 2)]
                        pe1 = psE4[2 * (gb % 2) + 1]
                        last = None
                        for g4 in range(4):
                            g = gb * 4 + g4
                            dep = []
                            if g4 == 0:
                                dep = list(dep0) if gb == 0 else [evs[gb - 2]] if gb >= 2 else []
                            sl_re = src[0][:, g * 128:(g + 1) * 128]
                            sl_im = src[1][:, g * 128:(g + 1) * 128]
                            ph.op("tensor", sP, dep,
                                  lambda t, sl_re=sl_re, g4=g4, pe0=pe0: t.matmul(
                                      pe0[:, g4 * 128:(g4 + 1) * 128], C["bdec"][:], sl_re,
                                      start=True, stop=False))
                            ph.op("tensor", sP, [],
                                  lambda t, sl_im=sl_im, g4=g4, pe0=pe0: t.matmul(
                                      pe0[:, g4 * 128:(g4 + 1) * 128], C["bdes"][:], sl_im,
                                      start=False, stop=True))
                            ph.op("tensor", sP, [],
                                  lambda t, sl_im=sl_im, g4=g4, pe1=pe1: t.matmul(
                                      pe1[:, g4 * 128:(g4 + 1) * 128], C["bdec"][:], sl_im,
                                      start=True, stop=False))
                            last = ph.op("tensor", sP, [],
                                         lambda t, sl_re=sl_re, g4=g4, pe1=pe1: t.matmul(
                                             pe1[:, g4 * 128:(g4 + 1) * 128], C["bdesn"][:], sl_re,
                                             start=False, stop=True))
                        ph.op("scalar", sA, [last],
                              lambda sc, gb=gb, dst=dst, pe0=pe0: sc.activation(
                                  dst[0][:, gb * 512:(gb + 1) * 512], pe0[:], AF.Copy))
                        e2_ = ph.op("scalar", sA, [],
                                    lambda sc, gb=gb, dst=dst, pe1=pe1: sc.activation(
                                        dst[1][:, gb * 512:(gb + 1) * 512], pe1[:], AF.Copy))
                        evs.append(e2_)
                    return last, evs[-1]

                def stepA(src, dst, dep0):
                    evs = []
                    for gb in range(8):
                        pe0 = psE4[2 * (gb % 2)]
                        pe1 = psE4[2 * (gb % 2) + 1]
                        last = None
                        for g4 in range(4):
                            g = gb * 4 + g4
                            dep = []
                            if g4 == 0:
                                dep = list(dep0) if gb == 0 else [evs[gb - 2]] if gb >= 2 else []
                            yre = src[0][:, g * 128:(g + 1) * 128]
                            yim = src[1][:, g * 128:(g + 1) * 128]
                            ph.op("tensor", sP, dep,
                                  lambda t, yre=yre, g4=g4, pe0=pe0: t.matmul(
                                      pe0[:, g4 * 128:(g4 + 1) * 128], yre, C["bdec"][:],
                                      start=True, stop=False))
                            ph.op("tensor", sP, [],
                                  lambda t, yim=yim, g4=g4, pe0=pe0: t.matmul(
                                      pe0[:, g4 * 128:(g4 + 1) * 128], yim, C["bdesn"][:],
                                      start=False, stop=True))
                            ph.op("tensor", sP, [],
                                  lambda t, yre=yre, g4=g4, pe1=pe1: t.matmul(
                                      pe1[:, g4 * 128:(g4 + 1) * 128], yre, C["bdes"][:],
                                      start=True, stop=False))
                            last = ph.op("tensor", sP, [],
                                         lambda t, yim=yim, g4=g4, pe1=pe1: t.matmul(
                                             pe1[:, g4 * 128:(g4 + 1) * 128], yim, C["bdec"][:],
                                             start=False, stop=True))
                        ph.op("scalar", sA, [last],
                              lambda sc, gb=gb, dst=dst, pe0=pe0: sc.activation(
                                  dst[0][:, gb * 512:(gb + 1) * 512], pe0[:], AF.Copy))
                        e2_ = ph.op("scalar", sA, [],
                                    lambda sc, gb=gb, dst=dst, pe1=pe1: sc.activation(
                                        dst[1][:, gb * 512:(gb + 1) * 512], pe1[:], AF.Copy))
                        evs.append(e2_)
                    return last, evs[-1]

                def emit_A_B(cp):
                    pr = cp % 2
                    stc = st_[cp]
                    la, ea = stepA(bB[pr], bA[pr], [stc["p6"], stc["psfree_for_A"]])
                    stc["stA_last"] = la
                    stc["stA_ev"] = ea
                    return la, ea

                def emit_B(cp):
                    pr = cp % 2
                    stc = st_[cp]
                    for hb in range(2):
                        lastY = None
                        dep0 = [stc["iv6"]]
                        pv = ydma_hist.get((cp, hb - 1), ydma_hist.get((cp - 1, 1)))
                        if pv:
                            dep0.append(pv)
                        for g16 in range(16):
                            g = hb * 16 + g16
                            ph.op("tensor", sP, dep0 if g16 == 0 else [],
                                  lambda t, g=g, g16=g16, pr=pr: t.matmul(
                                      psY4[:, g16 * 128:(g16 + 1) * 128], C["c2c"][:],
                                      bC[pr][0][:, g * 128:(g + 1) * 128], start=True, stop=False))
                            lastY = ph.op("tensor", sP, [],
                                          lambda t, g=g, g16=g16, pr=pr: t.matmul(
                                              psY4[:, g16 * 128:(g16 + 1) * 128], C["c2s"][:],
                                              bC[pr][1][:, g * 128:(g + 1) * 128], start=False, stop=True))
                        yed = [lastY]
                        pv2 = ydma_hist.get((cp - 2, 1))
                        if pv2:
                            yed.append(pv2)
                        ye = ph.op("scalar", sA, yed,
                                   lambda sc, hb=hb, pr=pr: sc.activation(
                                       ysb[pr][:, hb * 2048:(hb + 1) * 2048], psY4[:], AF.Copy))
                        ydma = ph.op("gpsimd", yss, [ye],
                                     lambda g_, cp=cp, hb=hb, pr=pr: g_.dma_start(
                                         bass.AP(y_dram.tensor, (cp * 64 + hb * 32) * NT,
                                                 [[64, 64], [2 * NT, 16], [NT, 2], [1, 64]]),
                                         bass.AP(ysb[pr], hb * 2048,
                                                 [[4096, 64], [128, 16], [64, 2], [1, 64]])), 16)
                        ydma_hist[(cp, hb)] = ydma

                for ct in range(16):
                    par = ct % 2
                    st_[ct] = {}
                    vdep = [st_[ct - 2]["s1v_last"]] if ct >= 2 else list(v_stores) + [xd_total]
                    vld = ph.op("gpsimd", vls[par], vdep,
                                lambda g, ct=ct, par=par: g.dma_start(
                                    vsl[par].ap().rearrange("b (d a) -> b d a", a=64),
                                    bass.AP(v_dram.tensor, ct * 64 * NT,
                                            [[64, 64], [NT, 64], [1, 64]])), 16)
                    tdep = [st_[ct - 2]["s1t_last"]] if ct >= 2 else list(t_stores) + [xd_total]
                    tld = ph.op("sync", tls[par], tdep,
                                lambda sy, ct=ct, par=par: sy.dma_start(
                                    tsl[par].ap().rearrange("b (d a) -> b d a", a=64),
                                    bass.AP(t_dram.tensor, ct * 64 * M,
                                            [[64, 128], [M, 64], [1, 64]])), 16)
                    psfree = [st_[ct - 1]["sCT_ev"]] if ct >= 1 else [const_ld]
                    bufree = [st_[ct - 2]["iv5"]] if ct >= 2 else []
                    l1, e1 = step1(vsl[par], 64, vld, bA[par], psfree, bufree)
                    st_[ct]["s1v_last"] = l1
                    twdeps = [e1]
                    if ct >= 2:
                        twdeps.append(st_[ct - 2]["stA_last"])
                    w6, _ = cmul6(twdeps, bA[par][0], bA[par][1], twcB, twsB,
                                  bB[par][0], bB[par][1], None)
                    lcv, xev = stepC(bB[par], bA[par], [w6])
                    st_[ct]["sCv_last"] = lcv
                    if ct >= 1:
                        st_[ct - 1]["psfree_for_A"] = xev
                        emit_A_B(ct - 1)
                        pr = (ct - 1) % 2
                        iv6, iv5 = cmul6([st_[ct - 1]["stA_ev"]],
                                         bA[pr][0], bA[pr][1], t2cB, t2sB,
                                         bC[pr][0], bC[pr][1], None)
                        st_[ct - 1]["iv6"] = iv6
                        st_[ct - 1]["iv5"] = iv5
                    psfreeT = [st_[ct - 1]["stA_ev"]] if ct >= 1 else [xev]
                    l2, e2 = step1(tsl[par], 128, tld, bC[par], psfreeT, [])
                    st_[ct]["s1t_last"] = l2
                    w6t, _ = cmul6([e2], bC[par][0], bC[par][1], twcB, twsB,
                                   bB[par][0], bB[par][1], None)
                    lct, aev = stepC(bB[par], bC[par], [w6t])
                    st_[ct]["sCT_last"] = lct
                    st_[ct]["sCT_ev"] = aev
                    if ct >= 1:
                        emit_B(ct - 1)
                    pwdeps = [xev, aev]
                    if ct >= 1:
                        pwdeps.append(st_[ct - 1]["stA_last"])
                    p1_ = ph.op("vector", sV, pwdeps,
                                lambda v, par=par: v.tensor_tensor(bB[par][0][:], bA[par][0][:],
                                                                   bC[par][0][:], ALU.mult))
                    p2_ = ph.op("vector", sV, [p1_],
                                lambda v, par=par: v.tensor_tensor(bB[par][1][:], bA[par][1][:],
                                                                   bC[par][1][:], ALU.mult))
                    p3_ = ph.op("vector", sV, [p2_],
                                lambda v, par=par: v.tensor_tensor(bB[par][0][:], bB[par][0][:],
                                                                   bB[par][1][:], ALU.subtract))
                    p4_ = ph.op("vector", sV, [p3_],
                                lambda v, par=par: v.tensor_tensor(bB[par][1][:], bA[par][0][:],
                                                                   bC[par][1][:], ALU.mult))
                    p5_ = ph.op("vector", sV, [p4_],
                                lambda v, par=par: v.tensor_tensor(bA[par][0][:], bA[par][1][:],
                                                                   bC[par][0][:], ALU.mult))
                    p6_ = ph.op("vector", sV, [p5_],
                                lambda v, par=par: v.tensor_tensor(bB[par][1][:], bB[par][1][:],
                                                                   bA[par][0][:], ALU.add))
                    st_[ct]["p6"] = p6_
                st_[15]["psfree_for_A"] = st_[15]["sCT_ev"]
                emit_A_B(15)
                iv6, iv5 = cmul6([st_[15]["stA_ev"]], bA[1][0], bA[1][1], t2cB, t2sB,
                                 bC[1][0], bC[1][1], None)
                st_[15]["iv6"] = iv6
                st_[15]["iv5"] = iv5
                emit_B(15)
                y_stores = yss.tok()
                ph.run("p4")

        with contextlib.ExitStack() as pp:
            ps_s5 = [pp.enter_context(nc.psum_tensor(f"ps_s5{i}", [128, 512], F32)) for i in range(2)]
            ps_t5 = [pp.enter_context(nc.psum_tensor(f"ps_t5{i}", [128, 128], F32)) for i in range(2)]
            with contextlib.ExitStack() as sb5:
                o_sb = sb(sb5, "o_sb", [128, 8 * D], BF16)
                ysl = [sb(sb5, f"ysl{i}", [128, 512], F32) for i in range(2)]
                usl = [sb(sb5, f"usl{i}", [128, 512], F32) for i in range(2)]
                prod = sb(sb5, "prod", [128, 8 * 512], BF16)
                sev = [sb(sb5, f"sev{i}", [128, 512], F32) for i in range(2)]
                stt5 = [sb(sb5, f"stt5{i}", [128, 128], F32) for i in range(2)]
                ph = Phase(nc)
                ows = dsem("ow")
                yus = [dsem("yu0"), dsem("yu1")]
                sss = [dsem("ss0"), dsem("ss1")]
                owl = ph.op("gpsimd", ows, [],
                            lambda g: g.dma_start(o_sb[:], o_w.rearrange("(cc p) d -> p cc d", p=128)), 16)
                sd_list = []
                em_by_slot = {}
                smm_last = None
                prev_pr = None
                for mc in range(8):
                    pr = {}
                    for cb in range(8):
                        ldd = []
                        if mc == 0 and cb == 0:
                            ldd = [y_stores] + u_stores
                        if cb >= 2:
                            ldd.append(pr[cb - 2])
                        elif mc >= 1:
                            ldd.append(prev_pr[6 + cb])
                        ph.op("sync", yus[cb % 2], ldd,
                              lambda sy, mc=mc, cb=cb: sy.dma_start(
                                  ysl[cb % 2][:],
                                  y_dram[cb * 128:(cb + 1) * 128, mc * 512:(mc + 1) * 512]), 16)
                        ul = ph.op("sync", yus[cb % 2], [],
                                   lambda sy, mc=mc, cb=cb: sy.dma_start(
                                       usl[cb % 2][:],
                                       u_dram[cb * 128:(cb + 1) * 128, mc * 512:(mc + 1) * 512]), 16)
                        pdep = [ul]
                        if mc >= 1 and cb == 0:
                            pdep.append(smm_last)
                        pr[cb] = ph.op("vector", sV, pdep,
                                       lambda v, cb=cb: v.tensor_tensor(
                                           prod[:, cb * 512:(cb + 1) * 512], ysl[cb % 2][:],
                                           usl[cb % 2][:], ALU.mult))
                    prev_pr = pr
                    for db in range(8):
                        last = None
                        mdep = [pr[7], owl]
                        if em_by_slot.get(db % 2):
                            mdep.append(em_by_slot[db % 2])
                        for cb in range(8):
                            last = ph.op("tensor", sP, mdep if cb == 0 else [],
                                         lambda t, db=db, cb=cb: t.matmul(
                                             ps_s5[db % 2][:],
                                             o_sb[:, cb * D + db * 128: cb * D + (db + 1) * 128],
                                             prod[:, cb * 512:(cb + 1) * 512],
                                             start=(cb == 0), stop=(cb == 7)))
                        smm_last = last
                        em = ph.op("scalar", sA, [last],
                                   lambda sc, db=db: sc.activation(sev[db % 2][:], ps_s5[db % 2][:], AF.Copy))
                        st_ev = {}
                        for sub in range(4):
                            td = [em] if sub == 0 else []
                            if sub >= 2:
                                td.append(st_ev[sub - 2])
                            pm = ph.op("tensor", sP, td,
                                       lambda t, db=db, sub=sub: t.transpose(
                                           ps_t5[sub % 2][:], sev[db % 2][:, sub * 128:(sub + 1) * 128],
                                           C["identf"][:]))
                            ed = [pm]
                            if len(sd_list) >= 2:
                                ed.append(sd_list[-2])
                            e2 = ph.op("scalar", sA, ed,
                                       lambda sc, sub=sub: sc.activation(stt5[sub % 2][:],
                                                                         ps_t5[sub % 2][:], AF.Copy))
                            st_ev[sub] = e2
                            sd = ph.op("sync", sss[sub % 2], [e2],
                                       lambda sy, mc=mc, db=db, sub=sub: sy.dma_start(
                                           s_dram[mc * 512 + sub * 128: mc * 512 + (sub + 1) * 128,
                                                  db * 128:(db + 1) * 128], stt5[sub % 2][:]), 16)
                            sd_list.append(sd)
                        em_by_slot[db % 2] = em
                s_stores = [sss[0].tok(), sss[1].tok()]
                ph.run("p5")

        with contextlib.ExitStack() as pp:
            ps_a6 = [pp.enter_context(nc.psum_tensor(f"ps_a6{i}", [128, 256], F32)) for i in range(2)]
            ps_b6 = [pp.enter_context(nc.psum_tensor(f"ps_b6{i}", [128, 512], F32)) for i in range(4)]
            ps_b1t = pp.enter_context(nc.psum_tensor("ps_b1t", [128, 32], F32))
            with contextlib.ExitStack() as sb6:
                w1sb = sb(sb6, "w1sb", [128, 8 * FF], BF16)
                w2sb = sb(sb6, "w2sb", [128, 32 * 1024], BF16)
                stage = [sb(sb6, f"stage{i}", [128, FF], F32) for i in range(2)]
                xtb = [sb(sb6, f"xtb{i}", [128, 2048], BF16) for i in range(2)]
                hcur = [sb(sb6, f"hcur{i}", [128, 256], BF16) for i in range(2)]
                b1T = sb(sb6, "b1T", [32, 128], F32)
                b1sb = sb(sb6, "b1sb", [128, 32], F32)
                mev = sb(sb6, "mev", [128, 2048], F32)
                gixf = sb(sb6, "gixf", [128, 48], F32)
                gixi = sb(sb6, "gixi", [128, 2 * 48], mybir.dt.int32)
                ph = Phase(nc)
                gst = [dsem("gst0"), dsem("gst1")]
                b1s = dsem("b1s")
                xtbs = [dsem("xtb0"), dsem("xtb1")]
                mas = dsem("mas")
                gix_done = None
                for j in range(2):
                    q1 = ph.op("vector", sV, [gix_done] if gix_done else [const_ld] + list(s_stores),
                               lambda v, j=j: v.tensor_scalar(
                                   gixf[:], C["gscale"][:], ebc[:, j:j + 1], None, ALU.mult))
                    q2 = ph.op("vector", sV, [q1],
                               lambda v: v.tensor_tensor(gixf[:], gixf[:], C["gidx_base"][:], ALU.add))
                    gix_done = ph.op("vector", sV, [q2],
                                     lambda v, j=j: v.tensor_copy(gixi[:, j * 48:(j + 1) * 48], gixf[:]))
                W1v = W1.rearrange("e d f -> (e d) f")
                W2v = W2.rearrange("e f d -> (e f) d")
                b1v = b1.rearrange("e (fc p) -> (e fc) p", p=128)
                prev_expert_pe = None
                prev_bev = None
                moe_hist = {}
                mev_done = None
                for e in range(2):
                    cast_last = None
                    gmark = {}
                    for c in range(40):
                        si = c % 2
                        gdeps = [gix_done] if c == 0 else []
                        if prev_expert_pe is not None and c <= 1:
                            gdeps.append(prev_expert_pe)
                        if c >= 2:
                            gdeps.append(gmark[c - 2])
                        if c < 8:
                            gl = ph.op("gpsimd", gst[si], gdeps,
                                       lambda g, e=e, c=c, si=si: g.indirect_dma_start(
                                           stage[si][:], None, W1v,
                                           bass.IndirectOffsetOnAxis(gixi[:, e * 48 + c: e * 48 + c + 1], 0)), 16)
                            cm = ph.op("vector" if c % 4 < 2 else "scalar",
                                       sV if c % 4 < 2 else sA, [gl],
                                       (lambda v, c=c, si=si: v.tensor_copy(
                                           w1sb[:, c * FF:(c + 1) * FF], stage[si][:]))
                                       if c % 4 < 2 else
                                       (lambda sc, c=c, si=si: sc.activation(
                                           w1sb[:, c * FF:(c + 1) * FF], stage[si][:], AF.Copy)))
                        else:
                            fc = c - 8
                            gl = ph.op("gpsimd", gst[si], gdeps,
                                       lambda g, e=e, fc=fc, si=si: g.indirect_dma_start(
                                           stage[si][:, 0:1024], None, W2v,
                                           bass.IndirectOffsetOnAxis(gixi[:, e * 48 + 8 + fc: e * 48 + 9 + fc], 0)), 16)
                            cm = ph.op("vector" if c % 4 < 2 else "scalar",
                                       sV if c % 4 < 2 else sA, [gl],
                                       (lambda v, fc=fc, si=si: v.tensor_copy(
                                           w2sb[:, fc * 1024:(fc + 1) * 1024], stage[si][:, 0:1024]))
                                       if c % 4 < 2 else
                                       (lambda sc, fc=fc, si=si: sc.activation(
                                           w2sb[:, fc * 1024:(fc + 1) * 1024], stage[si][:, 0:1024], AF.Copy)))
                        gmark[c] = cm
                        if c % 4 < 2:
                            cast_dve_last = cm
                        else:
                            cast_act_last = cm
                    wl1 = cast_dve_last
                    wl2 = cast_act_last
                    bl = ph.op("gpsimd", b1s, [gix_done] + ([prev_bev] if prev_bev else []),
                               lambda g, e=e: g.indirect_dma_start(
                                   b1T[:], None, b1v,
                                   bass.IndirectOffsetOnAxis(gixi[0:32, e * 48 + 40: e * 48 + 41], 0)), 16)
                    bdeps = [bl]
                    if prev_bev is not None:
                        bdeps.append(prev_bev)
                    btr = ph.op("tensor", sP, bdeps,
                                lambda t: t.transpose(ps_b1t[:], b1T[:], C["identf"][0:32, 0:32]))
                    bev = ph.op("vector", sV, [btr],
                                lambda v: v.tensor_copy(b1sb[:], ps_b1t[:]))
                    prev_bev = bev
                    last_b_mm = None
                    lastA_by_tb = {}
                    for TB in range(16):
                        xdeps = [xd_total] + list(s_stores) if (e == 0 and TB == 0) else []
                        if TB >= 2:
                            xdeps.append(lastA_by_tb[TB - 2])
                        elif e > 0:
                            xdeps.append(prev_expert_pe)
                        xld = ph.op("sync", xtbs[TB % 2], xdeps,
                                    lambda sy, TB=TB: sy.dma_start(
                                        xtb[TB % 2].ap().rearrange("p (dc t) -> p dc t", t=256),
                                        xnT_dram.rearrange("p (dc m) -> p dc m", m=NT)[:, :,
                                                                                       TB * 256:(TB + 1) * 256]), 16)
                        gel = {}

                        def emit_B(fc, TB=TB):
                            bdeps2 = [gel[fc]]
                            if fc == 0:
                                bdeps2.append(wl2)
                                if mev_done is not None:
                                    bdeps2.append(mev_done)
                            lastB = None
                            for tc2 in range(2):
                                for dh in range(2):
                                    j = tc2 * 2 + dh
                                    lastB = ph.op("tensor", sP, bdeps2 if j == 0 else [],
                                                  lambda t, fc=fc, tc2=tc2, dh=dh, j=j: t.matmul(
                                                      ps_b6[j][:],
                                                      hcur[fc % 2][:, tc2 * 128:(tc2 + 1) * 128],
                                                      w2sb[:, fc * 1024 + dh * 512: fc * 1024 + (dh + 1) * 512],
                                                      start=(fc == 0), stop=(fc == 31)))
                            return lastB

                        for fc in range(32):
                            adeps = []
                            if fc == 0:
                                adeps = [wl1, wl2, bev, xld]
                            if fc >= 2:
                                adeps.append(gel[fc - 2])
                            lastA = None
                            for dc in range(8):
                                lastA = ph.op("tensor", sP, adeps if dc == 0 else [],
                                              lambda t, TB=TB, fc=fc, dc=dc: t.matmul(
                                                  ps_a6[fc % 2][:],
                                                  w1sb[:, dc * FF + fc * 128: dc * FF + (fc + 1) * 128],
                                                  xtb[TB % 2][:, dc * 256:(dc + 1) * 256],
                                                  start=(dc == 0), stop=(dc == 7)))
                            lastA_by_tb[TB] = lastA
                            if fc >= 1:
                                emit_B(fc - 1)
                            gel[fc] = ph.op("scalar", sA, [lastA],
                                            lambda sc, fc=fc: sc.activation(
                                                hcur[fc % 2][:], ps_a6[fc % 2][:], AF.Gelu,
                                                bias=b1sb[:, fc:fc + 1]))
                        last_b_mm = emit_B(31)
                        for j in range(4):
                            mdep = [last_b_mm]
                            pv = moe_hist.get((e, TB - 1), moe_hist.get((e - 1, 15)))
                            if pv:
                                mdep.append(pv)
                            mev_done = ph.op("vector", sV, mdep,
                                             lambda v, j=j, e=e: v.tensor_scalar(
                                                 mev[:, j * 512:(j + 1) * 512], ps_b6[j][:],
                                                 cwslbc[:, e:e + 1], None, ALU.mult))
                        md = ph.op("gpsimd", mas, [mev_done],
                                   lambda g, e=e, TB=TB: g.dma_start(
                                       bass.AP(moe_dram.tensor, TB * 256 * D,
                                               [[D, 128], [128 * D, 2], [512, 2], [1, 512]]),
                                       mev.ap().rearrange("p (tc dh c) -> p tc dh c", dh=2, c=512),
                                       accum_op=(ALU.add if e > 0 else ALU.bypass)), 16)
                        moe_hist[(e, TB)] = md
                    prev_expert_pe = last_b_mm
                moe_done = mas.tok()
                ph.run("p6")

        with contextlib.ExitStack() as pp:
            ps_bc7 = pp.enter_context(nc.psum_tensor("ps_bc7", [128, 512], F32))
            with contextlib.ExitStack() as sb7:
                gbc = sb(sb7, "gbc", [128, D], F32)
                bbc = sb(sb7, "bbc", [128, D], F32)
                b2bc = sb(sb7, "b2bc", [128, D], F32)
                ytile = [sb(sb7, f"ytile{i}", [128, D], F32) for i in range(4)]
                snt = [sb(sb7, f"snt{i}", [128, D], F32) for i in range(4)]
                xt7 = [sb(sb7, f"xt7{i}", [128, D], F32) for i in range(4)]
                gt = [sb(sb7, f"gt{i}", [128, D], F32) for i in range(4)]
                ot = [sb(sb7, f"ot{i}", [128, D], F32) for i in range(4)]
                st7 = [sb(sb7, f"st7{i}", [128, 8], F32) for i in range(4)]
                grow = sb(sb7, "grow", [1, D], F32)
                brow = sb(sb7, "brow", [1, D], F32)
                b2r = sb(sb7, "b2r", [1, D], F32)
                ph = Phase(nc)
                p7s = dsem("p7s")
                yts = [dsem(f"yt7{i}") for i in range(4)]
                sns = [dsem(f"sn{i}") for i in range(4)]
                x7s = [dsem(f"x7{i}") for i in range(4)]
                oss = [dsem(f"os{i}") for i in range(4)]
                ph.op("sync", p7s, [], lambda sy: sy.dma_start(grow[:], gamma), 16)
                ph.op("sync", p7s, [], lambda sy: sy.dma_start(brow[:], beta), 16)
                rows_ld = ph.op("sync", p7s, [b2mix_store],
                                lambda sy: sy.dma_start(b2r[:], b2mix_dram), 16)
                bcv = None
                for row, dstt in [(grow, gbc), (brow, bbc), (b2r, b2bc)]:
                    for half in range(2):
                        bdeps = [rows_ld]
                        if bcv is not None:
                            bdeps.append(bcv)
                        bm = ph.op("tensor", sP, bdeps,
                                   lambda t, row=row, half=half: t.matmul(
                                       ps_bc7[:], C["onesrow"][:], row[0:1, half * 512:(half + 1) * 512],
                                       start=True, stop=True))
                        bcv = ph.op("vector", sV, [bm],
                                    lambda v, dstt=dstt, half=half: v.tensor_copy(
                                        dstt[:, half * 512:(half + 1) * 512], ps_bc7[:]))
                out_hist = {}
                r12_hist = {}
                r13_hist = {}
                a3_hist = {}

                def tile_chain(t_):
                    bi = t_ % 4
                    ldd = []
                    if t_ <= 3:
                        ldd.append(moe_done)
                    if t_ >= 4:
                        ldd.append(a3_hist[t_ - 4])
                    mld = ph.op("sync", yts[bi], ldd,
                                lambda sy, t_=t_, bi=bi: sy.dma_start(
                                    ytile[bi][:], moe_dram[t_ * 128:(t_ + 1) * 128, :]), 16)
                    yield
                    xld = []
                    if t_ <= 3:
                        xld = xld + xn_stores_l
                    if t_ >= 4:
                        xld.append(r12_hist[t_ - 4])
                    xl = ph.op("sync", sns[bi], xld,
                               lambda sy, t_=t_, bi=bi: sy.dma_start(
                                   snt[bi][:], xn_dram[t_ * 128:(t_ + 1) * 128, :]), 16)
                    yield
                    r1 = ph.op("vector", sV, [mld, xl, bcv],
                               lambda v, bi=bi: v.tensor_tensor(ytile[bi][:], ytile[bi][:], snt[bi][:], ALU.add))
                    yield
                    r2 = ph.op("vector", sV, [r1],
                               lambda v, bi=bi: v.tensor_tensor(ytile[bi][:], ytile[bi][:], b2bc[:], ALU.add))
                    yield
                    r3_ = ph.op("vector", sV, [r2],
                                lambda v, bi=bi: v.tensor_reduce(st7[bi][:, 0:1], ytile[bi][:], AX.X, ALU.add))
                    a1d = [r2]
                    if t_ >= 4:
                        a1d.append(r12_hist[t_ - 4])
                    a1 = ph.op("scalar", sA, a1d,
                               lambda sc, bi=bi: sc.activation(gt[bi][:], ytile[bi][:], AF.Square,
                                                               accum_out=st7[bi][:, 1:2]))
                    yield
                    r4 = ph.op("vector", sV, [r3_],
                               lambda v, bi=bi: v.tensor_scalar_mul(st7[bi][:, 2:3], st7[bi][:, 0:1], 1.0 / D))
                    yield
                    r5 = ph.op("vector", sV, [a1, r4],
                               lambda v, bi=bi: v.tensor_scalar(st7[bi][:, 3:4], st7[bi][:, 1:2], 1.0 / D, EPS,
                                                                ALU.mult, ALU.add))
                    yield
                    r6 = ph.op("vector", sV, [r4],
                               lambda v, bi=bi: v.tensor_tensor(st7[bi][:, 4:5], st7[bi][:, 2:3],
                                                                st7[bi][:, 2:3], ALU.mult))
                    yield
                    r7 = ph.op("vector", sV, [r6, r5],
                               lambda v, bi=bi: v.tensor_tensor(st7[bi][:, 3:4], st7[bi][:, 3:4],
                                                                st7[bi][:, 4:5], ALU.subtract))
                    yield
                    a2 = ph.op("scalar", sA, [r7],
                               lambda sc, bi=bi: sc.activation(st7[bi][:, 5:6], st7[bi][:, 3:4], AF.Sqrt))
                    yield
                    r8 = ph.op("vector", sV, [a2],
                               lambda v, bi=bi: v.reciprocal(st7[bi][:, 6:7], st7[bi][:, 5:6]))
                    yield
                    r9 = ph.op("vector", sV, [r8],
                               lambda v, bi=bi: v.tensor_scalar(ytile[bi][:], ytile[bi][:], st7[bi][:, 2:3],
                                                                st7[bi][:, 6:7], ALU.subtract, ALU.mult))
                    yield
                    r10 = ph.op("vector", sV, [r9],
                                lambda v, bi=bi: v.tensor_tensor(ytile[bi][:], ytile[bi][:], gbc[:], ALU.mult))
                    yield
                    r11 = ph.op("vector", sV, [r10],
                                lambda v, bi=bi: v.tensor_tensor(ytile[bi][:], ytile[bi][:], bbc[:], ALU.add))
                    yield
                    a3 = ph.op("scalar", sA, [r11],
                               lambda sc, bi=bi: sc.activation(gt[bi][:], ytile[bi][:], AF.Silu))
                    a3_hist[t_] = a3
                    yield
                    sld = [r1]
                    if t_ <= 3:
                        sld = sld + s_stores
                    sl = ph.op("sync", sns[bi], sld,
                               lambda sy, t_=t_, bi=bi: sy.dma_start(
                                   snt[bi][:], s_dram[t_ * 128:(t_ + 1) * 128, :]), 16)
                    x2d = []
                    if t_ >= 4:
                        x2d.append(r13_hist[t_ - 4])
                    xl2 = ph.op("gpsimd", x7s[bi], x2d,
                                lambda g, t_=t_, bi=bi: g.dma_start(
                                    xt7[bi][:], x[t_ * 128:(t_ + 1) * 128, :]), 16)
                    yield
                    odep = [a3, sl]
                    if t_ >= 4:
                        odep.append(out_hist[t_ - 4])
                    r12 = ph.op("vector", sV, odep,
                                lambda v, bi=bi: v.tensor_tensor(ot[bi][:], gt[bi][:], snt[bi][:], ALU.mult))
                    r12_hist[t_] = r12
                    yield
                    r13 = ph.op("vector", sV, [r12, xl2],
                                lambda v, bi=bi: v.tensor_tensor(ot[bi][:], ot[bi][:], xt7[bi][:], ALU.add))
                    r13_hist[t_] = r13
                    om = ph.op("sync", oss[bi], [r13],
                               lambda sy, t_=t_, bi=bi: sy.dma_start(
                                   out[t_ * 128:(t_ + 1) * 128, :], ot[bi][:]), 16)
                    out_hist[t_] = om

                for pr in range(8):
                    alive = [tile_chain(4 * pr + j) for j in range(4)]
                    while alive:
                        nxt = []
                        for g_ in alive:
                            try:
                                next(g_)
                                nxt.append(g_)
                            except StopIteration:
                                pass
                        alive = nxt
                ph.run("p7")

        all_dma_sems = list(S.values())
        with nc.Block("fin") as blk:
            @blk.gpsimd
            def _(g):
                for s_ in all_dma_sems:
                    if s_.n > 0:
                        g.wait_ge(s_.h, s_.n)
    return nc


_NC_CACHE = {}


def kernel(**inputs):
    ins = {k: np.ascontiguousarray(np.asarray(v, dtype=np.float32)) for k, v in inputs.items()}
    if "main" not in _NC_CACHE:
        _NC_CACHE["main"] = build_nc(debug=False)
    nc = _NC_CACHE["main"]
    base = dict(
        uv_w=ins["L0_attn_uv_w"], o_w=ins["L0_attn_o_w"],
        rpe_ln_w=ins["L0_attn_rpe_ln_w"], rpe_ln_b=ins["L0_attn_rpe_ln_b"],
        rpe_fc_w=ins["L0_attn_rpe_fc_w"],
        rpe_oln_w=ins["L0_attn_rpe_out_ln_w"].reshape(1, 32),
        rpe_oln_b=ins["L0_attn_rpe_out_ln_b"].reshape(1, 32),
        rpe_pos_w=ins["L0_attn_rpe_pos_w"],
        rpe_out_w=ins["L0_attn_rpe_out_w"],
        W1=ins["L0_ffn_W1"], b1=ins["L0_ffn_b1"], W2=ins["L0_ffn_W2"], b2=ins["L0_ffn_b2"],
        Wr=ins["L0_ffn_Wr"], br=ins["L0_ffn_br"].reshape(1, NE),
        gamma=ins["L0_ffn_gamma"].reshape(1, D), beta=ins["L0_ffn_beta"].reshape(1, D),
    )
    xfull = ins["L0_attn_x"]
    in_maps = [dict(base, x=np.ascontiguousarray(xfull[i])) for i in range(8)]
    res = run_bass_kernel_spmd(nc, in_maps, core_ids=list(range(8)))
    return np.stack([res.results[i]["out"] for i in range(8)], axis=0)


# version 17 (final)
# speedup vs baseline: 1.2015x; 1.0127x over previous
import contextlib
import numpy as np
import ml_dtypes
import concourse.bass as bass
import concourse.mybir as mybir
from concourse.bass_utils import run_bass_kernel_spmd

F32 = mybir.dt.float32
BF16 = mybir.dt.bfloat16
AF = mybir.ActivationFunctionType
ALU = mybir.AluOpType
AX = mybir.AxisListType
BF = ml_dtypes.bfloat16

NT = 4096
D = 1024
FF = 4096
NE = 8
M = 8192
EPS = 1e-5
GAMMA = 0.99
ENGS = ("sync", "gpsimd", "tensor", "vector", "scalar")


def _consts():
    t = {}
    a = np.arange(64); b = np.arange(128); k2 = np.arange(128); k1 = np.arange(64)
    m2 = np.arange(64)
    tp = 2 * np.pi
    t["f1c"] = np.cos(tp * np.outer(b, k2) / 128).astype(BF)
    t["f1s"] = (-np.sin(tp * np.outer(b, k2) / 128)).astype(BF)
    ec = np.cos(tp * np.outer(a, k1) / 64).astype(np.float32)
    es = np.sin(tp * np.outer(a, k1) / 64).astype(np.float32)
    bd = lambda mtx: np.kron(np.eye(2, dtype=np.float32), mtx)
    t["bdec"] = bd(ec).astype(BF)
    t["bdes"] = bd(es).astype(BF)
    t["bdesn"] = bd(-es).astype(BF)
    t["c2c"] = np.cos(tp * np.outer(k2, m2) / 128).astype(BF)
    t["c2s"] = (-np.sin(tp * np.outer(k2, m2) / 128)).astype(BF)
    twc1 = np.cos(tp * np.outer(a, k2) / M).astype(np.float32)
    tws1 = (-np.sin(tp * np.outer(a, k2) / M)).astype(np.float32)
    t["twc"] = np.concatenate([twc1, twc1], 0).astype(BF)
    t["tws"] = np.concatenate([tws1, tws1], 0).astype(BF)
    tw2c = (np.cos(tp * np.outer(k2, a) / M) / M).astype(np.float32)
    tw2s = (np.sin(tp * np.outer(k2, a) / M) / M).astype(np.float32)
    t["tw2c"] = np.concatenate([tw2c, tw2c], 1).astype(BF)
    t["tw2s"] = np.concatenate([tw2s, tw2s], 1).astype(BF)
    t["identf"] = np.eye(128, dtype=np.float32)
    t["identb"] = np.eye(128, dtype=BF)
    mm_ = (np.arange(64)[None, :] * 128 + np.arange(128)[:, None]).astype(np.int64)
    pos = np.where(mm_ <= 4095, mm_, mm_ - M).astype(np.float32)
    pos[mm_ == 4096] = 0.0
    dec = (GAMMA ** np.abs(pos)).astype(np.float32)
    dec[mm_ == 4096] = 0.0
    t["posx"] = np.repeat(pos[:, :, None], 32, axis=2).reshape(128, 2048).astype(np.float32)
    t["decayx"] = np.repeat(dec[:, :, None], 32, axis=2).reshape(128, 2048).astype(np.float32)
    t["onesrow"] = np.ones((1, 128), np.float32)
    t["ones11"] = np.ones((1, 1), np.float32)
    t["onescol_b"] = np.ones((128, 1), BF)
    t["iota8"] = np.arange(8, dtype=np.float32).reshape(1, 8)
    gb_ = np.zeros((128, 48), np.float32)
    gs_ = np.zeros((128, 48), np.float32)
    p_ = np.arange(128, dtype=np.float32)
    for c in range(8):
        gb_[:, c] = p_ + c * 128
        gs_[:, c] = 1024.0
    for c in range(32):
        gb_[:, 8 + c] = p_ + c * 128
        gs_[:, 8 + c] = 4096.0
    gb_[0:32, 40] = np.arange(32)
    gs_[0:32, 40] = 32.0
    t["gidx_base"] = gb_
    t["gscale"] = gs_
    return t


class Sem:
    def __init__(self, stack, nc, name):
        self.h = stack.enter_context(nc.semaphore(name))
        self.n = 0
        self.name = name

    def bump(self, k=1):
        self.n += k
        return self.n

    def tok(self):
        return (self, self.n)


class Phase:

    def __init__(self, nc):
        self.nc = nc
        self.q = {e: [] for e in ENGS}

    def op(self, eng, sem, deps, fn, inc=1):
        waits = [(s.h, v) for t_ in deps if t_ is not None for (s, v) in [t_] if v > 0]
        tok = None
        h = None
        if sem is not None:
            tok = (sem, sem.bump(inc))
            h = sem.h

        def th(E, waits=waits, h=h, inc=inc, fn=fn):
            for hh, vv in waits:
                E.wait_ge(hh, vv)
            ins = fn(E)
            if h is not None and ins is not None:
                ins.then_inc(h, inc)
        self.q[eng].append(th)
        return tok

    def run(self, name):
        with self.nc.Block(name) as blk:
            for eng in ENGS:
                ops = self.q[eng]
                if ops:
                    getattr(blk, eng)(lambda E, ops=ops: [f(E) for f in ops])


def build_nc(debug=False):
    nc = bass.Bass("TRN2", target_bir_lowering=False)
    T = _consts()

    def din(name, shape):
        return nc.dram_tensor(name, shape, F32, kind="ExternalInput").ap()

    x = din("x", [NT, D])
    uv_w = din("uv_w", [D, 2 * D])
    o_w = din("o_w", [D, D])
    rpe_ln_w = din("rpe_ln_w", [3, 32])
    rpe_ln_b = din("rpe_ln_b", [3, 32])
    rpe_fc_w = din("rpe_fc_w", [3, 32, 32])
    rpe_oln_w = din("rpe_oln_w", [1, 32])
    rpe_oln_b = din("rpe_oln_b", [1, 32])
    rpe_pos_w = din("rpe_pos_w", [1, 32])
    rpe_out_w = din("rpe_out_w", [32, D])
    W1 = din("W1", [NE, D, FF])
    b1 = din("b1", [NE, FF])
    W2 = din("W2", [NE, FF, D])
    b2 = din("b2", [NE, D])
    Wr = din("Wr", [D, NE])
    br = din("br", [1, NE])
    gamma = din("gamma", [1, D])
    beta = din("beta", [1, D])
    out = nc.dram_tensor("out", [NT, D], F32, kind="ExternalOutput").ap()

    sk = "ExternalOutput" if debug else "Internal"
    xn_dram = nc.dram_tensor("xn_dram", [NT, D], F32, kind=sk).ap()
    u_dram = nc.dram_tensor("u_dram", [D, NT], F32, kind=sk).ap()
    v_dram = nc.dram_tensor("v_dram", [D, NT], BF16, kind=sk).ap()
    t_dram = nc.dram_tensor("t_dram", [D, M], BF16, kind=sk).ap()
    y_dram = nc.dram_tensor("y_dram", [D, NT], F32, kind=sk).ap()
    s_dram = nc.dram_tensor("s_dram", [NT, D], F32, kind=sk).ap()
    moe_dram = nc.dram_tensor("moe_dram", [NT, D], F32, kind=sk).ap()
    xnT_dram = nc.dram_tensor("xnT_dram", [128, 8 * NT], BF16, kind="Internal").ap()
    b2mix_dram = nc.dram_tensor("b2mix_dram", [1, D], F32, kind="Internal").ap()

    ci = {k: nc.inline_tensor(v, f"c_{k}").ap() for k, v in T.items()}

    st = contextlib.ExitStack()
    with st:
        def sb(stk, name, shape, dt):
            return stk.enter_context(nc.sbuf_tensor(name, shape, dt))

        xnT_stack = contextlib.ExitStack()
        xnT = xnT_stack.enter_context(nc.sbuf_tensor("xnT", [128, 8 * NT], BF16, side="right"))
        C = {}
        for k, shape, dt in [
            ("f1c", [128, 128], BF16), ("f1s", [128, 128], BF16),
            ("bdec", [128, 128], BF16), ("bdes", [128, 128], BF16),
            ("bdesn", [128, 128], BF16), ("c2c", [128, 64], BF16),
            ("c2s", [128, 64], BF16), ("twc", [128, 128], BF16),
            ("tws", [128, 128], BF16), ("tw2c", [128, 128], BF16),
            ("tw2s", [128, 128], BF16), ("identf", [128, 128], F32),
            ("identb", [128, 128], BF16), ("onesrow", [1, 128], F32),
            ("ones11", [1, 1], F32), ("onescol_b", [128, 1], BF16),
            ("iota8", [1, 8], F32), ("gidx_base", [128, 48], F32),
            ("gscale", [128, 48], F32),
        ]:
            C[k] = sb(st, "cc_" + k, shape, dt)
        ebc = sb(st, "ebc", [128, 2], F32)
        cwslbc = sb(st, "cwslbc", [128, 2], F32)

        sP = Sem(st, nc, "sP")
        sA = Sem(st, nc, "sA")
        sV = Sem(st, nc, "sV")
        S = {}

        def dsem(name):
            if name not in S:
                S[name] = Sem(st, nc, "d_" + name)
            return S[name]

        ph = Phase(nc)
        cst = dsem("cst")
        const_ld = None
        for k in C:
            const_ld = ph.op("sync", cst, [], lambda sy, k=k: sy.dma_start(C[k][:], ci[k]), 16)

        with contextlib.ExitStack() as pp:
            ps_xsum = pp.enter_context(nc.psum_tensor("ps_xsum", [1, 1024], F32))
            ps_tr = [pp.enter_context(nc.psum_tensor(f"ps_tr{i}", [128, 128], BF16))
                     for i in range(2)]
            with contextlib.ExitStack() as sb0:
                xt = [sb(sb0, f"xt{i}", [128, D], F32) for i in range(2)]
                xnb = [sb(sb0, f"xnb{i}", [128, D], BF16) for i in range(2)]
                xnf = [sb(sb0, f"xnf{i}", [128, D], F32) for i in range(2)]
                stt = [sb(sb0, f"stt{i}", [128, 8], F32) for i in range(2)]
                ph.run("p0a")
                ph = Phase(nc)
                xts = [dsem("xt0"), dsem("xt1")]
                xns = [dsem("xns0"), dsem("xns1")]
                tile_pe_done = {}
                tile_store = {}
                ev_hist = []
                r8_hist = {}

                def p0_chain(t_):
                    bi = t_ % 2
                    deps = [tile_pe_done.get(t_ - 2), r8_hist.get(t_ - 2)] if t_ >= 2 else []
                    ld = ph.op("sync", xts[bi], deps,
                               lambda sy, t_=t_, bi=bi: sy.dma_start(
                                   xt[bi][:], x[t_ * 128:(t_ + 1) * 128, :]), 16)
                    yield
                    r1 = ph.op("vector", sV, [ld],
                               lambda v, bi=bi: v.tensor_reduce(stt[bi][:, 0:1], xt[bi][:], AX.X, ALU.add))
                    sqdeps = [ld]
                    if t_ >= 2:
                        sqdeps.append(tile_store[t_ - 2])
                    a1 = ph.op("scalar", sA, sqdeps,
                               lambda sc, bi=bi: sc.activation(xnf[bi][:], xt[bi][:], AF.Square,
                                                               accum_out=stt[bi][:, 1:2]))
                    yield
                    r2 = ph.op("vector", sV, [r1],
                               lambda v, bi=bi: v.tensor_scalar_mul(stt[bi][:, 2:3], stt[bi][:, 0:1], 1.0 / D))
                    yield
                    r3 = ph.op("vector", sV, [a1, r2],
                               lambda v, bi=bi: v.tensor_scalar(stt[bi][:, 3:4], stt[bi][:, 1:2], 1.0 / D, EPS,
                                                                ALU.mult, ALU.add))
                    yield
                    r4 = ph.op("vector", sV, [r2],
                               lambda v, bi=bi: v.tensor_tensor(stt[bi][:, 4:5], stt[bi][:, 2:3],
                                                                stt[bi][:, 2:3], ALU.mult))
                    yield
                    r5 = ph.op("vector", sV, [r4, r3],
                               lambda v, bi=bi: v.tensor_tensor(stt[bi][:, 3:4], stt[bi][:, 3:4],
                                                                stt[bi][:, 4:5], ALU.subtract))
                    yield
                    a2 = ph.op("scalar", sA, [r5],
                               lambda sc, bi=bi: sc.activation(stt[bi][:, 5:6], stt[bi][:, 3:4], AF.Sqrt))
                    yield
                    r6 = ph.op("vector", sV, [a2],
                               lambda v, bi=bi: v.reciprocal(stt[bi][:, 6:7], stt[bi][:, 5:6]))
                    yield
                    r7 = ph.op("vector", sV, [r6],
                               lambda v, bi=bi: v.tensor_scalar(xnb[bi][:], xt[bi][:], stt[bi][:, 2:3],
                                                                stt[bi][:, 6:7], ALU.subtract, ALU.mult))
                    yield
                    r8 = ph.op("vector", sV, [r7],
                               lambda v, bi=bi: v.tensor_scalar(xnf[bi][:], xt[bi][:], stt[bi][:, 2:3],
                                                                stt[bi][:, 6:7], ALU.subtract, ALU.mult))
                    r8_hist[t_] = r8
                    tile_store[t_] = ph.op("scalar", xns[bi], [r8],
                                           lambda sc, t_=t_, bi=bi: sc.dma_start(
                                               xn_dram[t_ * 128:(t_ + 1) * 128, :], xnf[bi][:]), 16)
                    yield
                    for db in range(8):
                        pdeps = [r7]
                        if not ev_hist:
                            pdeps.append(const_ld)
                        if len(ev_hist) >= 2:
                            pdeps.append(ev_hist[-2])
                        pm = ph.op("tensor", sP, pdeps,
                                   lambda t, bi=bi, db=db: t.transpose(
                                       ps_tr[db % 2][:], xnb[bi][:, db * 128:(db + 1) * 128],
                                       C["identb"][:]))
                        ev_hist.append(ph.op("scalar", sA, [pm],
                                             lambda sc, t_=t_, db=db: sc.activation(
                                                 xnT[:, db * NT + t_ * 128: db * NT + (t_ + 1) * 128],
                                                 ps_tr[db % 2][:], AF.Copy)))
                    ph.op("tensor", sP, [r7, const_ld],
                          lambda t, t_=t_, bi=bi: t.matmul(
                              ps_xsum[0:1, 0:512], C["onescol_b"][:], xnb[bi][:, 0:512],
                              start=(t_ == 0), stop=(t_ == 31)))
                    m2 = ph.op("tensor", sP, [],
                               lambda t, t_=t_, bi=bi: t.matmul(
                                   ps_xsum[0:1, 512:1024], C["onescol_b"][:], xnb[bi][:, 512:1024],
                                   start=(t_ == 0), stop=(t_ == 31)))
                    tile_pe_done[t_] = m2

                prev = None
                for pr in range(16):
                    ga = p0_chain(2 * pr)
                    gb = p0_chain(2 * pr + 1)
                    alive = [ga, gb]
                    while alive:
                        nxt = []
                        for g_ in alive:
                            try:
                                next(g_)
                                nxt.append(g_)
                            except StopIteration:
                                pass
                        alive = nxt
                xsum_mm = tile_pe_done[31]
                xn_stores_l = [xns[0].tok(), xns[1].tok()]
                ph.run("p0b")

            with contextlib.ExitStack() as sb1:
                xsum_sb = sb(sb1, "xsum_sb", [1, 1024], F32)
                xmT = sb(sb1, "xmT", [128, 8], F32)
                wr_sb = sb(sb1, "wr_sb", [128, 64], F32)
                b2sb = sb(sb1, "b2sb", [8, 1024], F32)
                brr = sb(sb1, "brr", [1, 8], F32)
                rw = sb(sb1, "rw", [1, 8], F32)
                rt = sb(sb1, "rt", [1, 8], F32)
                eq1 = sb(sb1, "eq1", [1, 8], F32)
                eq2 = sb(sb1, "eq2", [1, 8], F32)
                s4 = sb(sb1, "s4", [1, 4], F32)
                cw8 = sb(sb1, "cw8", [1, 8], F32)
                cw8T = sb(sb1, "cw8T", [8, 1], F32)
                b2mix = sb(sb1, "b2mix", [1, D], F32)
                eidx = sb(sb1, "eidx", [1, 2], F32)
                cwrow = sb(sb1, "cwrow", [1, 2], F32)
                with contextlib.ExitStack() as pp1:
                    ps_m = pp1.enter_context(nc.psum_tensor("ps_m", [128, 512], F32))
                    ps_r = pp1.enter_context(nc.psum_tensor("ps_r", [1, 512], F32))
                    ps_c8 = pp1.enter_context(nc.psum_tensor("ps_c8", [8, 8], F32))
                    ph = Phase(nc)
                    p1s = dsem("p1s")
                    for apdst, apsrc in [(wr_sb[:], Wr.rearrange("(c p) e -> p c e", p=128)),
                                         (b2sb[:], b2), (brr[:], br)]:
                        p1_all = ph.op("sync", p1s, xn_stores_l,
                                       lambda sy, apdst=apdst, apsrc=apsrc: sy.dma_start(apdst, apsrc), 16)
                    e1 = ph.op("vector", sV, [xsum_mm] + xn_stores_l,
                               lambda v: v.tensor_copy(xsum_sb[:, 0:512], ps_xsum[0:1, 0:512]))
                    e2 = ph.op("vector", sV, [e1],
                               lambda v: v.tensor_copy(xsum_sb[:, 512:1024], ps_xsum[0:1, 512:1024]))
                    mt = None
                    for k in range(8):
                        mt = ph.op("tensor", sP, [e2, const_ld],
                                   lambda t, k=k: t.matmul(ps_m[:, k:k + 1],
                                                           xsum_sb[0:1, k * 128:(k + 1) * 128],
                                                           C["ones11"][:], start=True, stop=True))
                    v1 = ph.op("vector", sV, [mt],
                               lambda v: v.tensor_scalar_mul(xmT[:], ps_m[:, 0:8], 1.0 / NT))
                    rmm = None
                    for k in range(8):
                        rmm = ph.op("tensor", sP, [v1, p1_all],
                                    lambda t, k=k: t.matmul(ps_r[0:1, 0:8], xmT[:, k:k + 1],
                                                            wr_sb[:, k * 8:(k + 1) * 8],
                                                            start=(k == 0), stop=(k == 7)))
                    c1 = ph.op("vector", sV, [rmm],
                               lambda v: v.tensor_copy(rw[:], ps_r[0:1, 0:8]))
                    c2 = ph.op("vector", sV, [c1],
                               lambda v: v.tensor_tensor(rw[:], rw[:], brr[:], ALU.add))
                    c3 = ph.op("vector", sV, [c2],
                               lambda v: v.tensor_reduce(s4[:, 0:1], rw[:], AX.X, ALU.max))
                    c4 = ph.op("vector", sV, [c3],
                               lambda v: v.tensor_scalar(rt[:], rw[:], s4[:, 0:1], None, ALU.subtract))
                    c5 = ph.op("scalar", sA, [c4],
                               lambda sc: sc.activation(rt[:], rt[:], AF.Exp))
                    c6 = ph.op("vector", sV, [c5],
                               lambda v: v.tensor_reduce(s4[:, 1:2], rt[:], AX.X, ALU.add))
                    c7 = ph.op("vector", sV, [c6],
                               lambda v: v.reciprocal(s4[:, 2:3], s4[:, 1:2]))
                    c8 = ph.op("vector", sV, [c7],
                               lambda v: v.tensor_scalar(rw[:], rt[:], s4[:, 2:3], None, ALU.mult))
                    d1 = ph.op("vector", sV, [c8],
                               lambda v: v.tensor_reduce(s4[:, 0:1], rw[:], AX.X, ALU.max))
                    d2 = ph.op("vector", sV, [d1],
                               lambda v: v.tensor_scalar(eq1[:], rw[:], s4[:, 0:1], None, ALU.is_equal))
                    d3 = ph.op("vector", sV, [d2],
                               lambda v: v.tensor_scalar_mul(rt[:], eq1[:], -3.0))
                    d4 = ph.op("vector", sV, [d3],
                               lambda v: v.tensor_tensor(rt[:], rw[:], rt[:], ALU.add))
                    d5 = ph.op("vector", sV, [d4],
                               lambda v: v.tensor_reduce(s4[:, 1:2], rt[:], AX.X, ALU.max))
                    d6 = ph.op("vector", sV, [d5],
                               lambda v: v.tensor_scalar(eq2[:], rt[:], s4[:, 1:2], None, ALU.is_equal))
                    i1 = ph.op("vector", sV, [d6],
                               lambda v: v.tensor_tensor(rt[:], eq1[:], C["iota8"][:], ALU.mult))
                    i2 = ph.op("vector", sV, [i1],
                               lambda v: v.tensor_reduce(eidx[:, 0:1], rt[:], AX.X, ALU.add))
                    i3 = ph.op("vector", sV, [i2],
                               lambda v: v.tensor_tensor(rt[:], eq2[:], C["iota8"][:], ALU.mult))
                    i4 = ph.op("vector", sV, [i3],
                               lambda v: v.tensor_reduce(eidx[:, 1:2], rt[:], AX.X, ALU.add))
                    d7 = ph.op("vector", sV, [i4],
                               lambda v: v.tensor_tensor(s4[:, 2:3], s4[:, 1:2], s4[:, 0:1], ALU.subtract))
                    d8 = ph.op("scalar", sA, [d7],
                               lambda sc: sc.activation(s4[:, 2:3], s4[:, 2:3], AF.Exp))
                    d9 = ph.op("vector", sV, [d8],
                               lambda v: v.tensor_scalar(s4[:, 3:4], s4[:, 2:3], 1.0, None, ALU.add))
                    d10 = ph.op("vector", sV, [d9],
                                lambda v: v.reciprocal(s4[:, 3:4], s4[:, 3:4]))
                    d11 = ph.op("vector", sV, [d10],
                                lambda v: v.tensor_scalar(eq1[:], eq1[:], s4[:, 3:4], None, ALU.mult))
                    d12 = ph.op("vector", sV, [d11],
                                lambda v: v.tensor_scalar(s4[:, 2:3], s4[:, 3:4], -1.0, 1.0,
                                                          ALU.mult, ALU.add))
                    d13 = ph.op("vector", sV, [d12],
                                lambda v: v.tensor_scalar(eq2[:], eq2[:], s4[:, 2:3], None, ALU.mult))
                    d14 = ph.op("vector", sV, [d13],
                                lambda v: v.tensor_tensor(cw8[:], eq1[:], eq2[:], ALU.add))
                    d15 = ph.op("vector", sV, [d12],
                                lambda v: v.tensor_copy(cwrow[:, 0:1], s4[:, 3:4]))
                    d16 = ph.op("vector", sV, [d15],
                                lambda v: v.tensor_copy(cwrow[:, 1:2], s4[:, 2:3]))
                    g1 = ph.op("tensor", sP, [d16],
                               lambda t: t.matmul(ps_m[:, 0:2], C["onesrow"][:], cwrow[:],
                                                  start=True, stop=True))
                    g1b = ph.op("tensor", sP, [d16],
                                lambda t: t.matmul(ps_m[:, 8:10], C["onesrow"][:], eidx[:],
                                                   start=True, stop=True))
                    g2 = ph.op("tensor", sP, [d14],
                               lambda t: t.matmul(ps_c8[:, 0:1], cw8[:], C["ones11"][:],
                                                  start=True, stop=True))
                    g3 = ph.op("vector", sV, [g1b],
                               lambda v: v.tensor_copy(cwslbc[:], ps_m[:, 0:2]))
                    g3b = ph.op("vector", sV, [g3],
                                lambda v: v.tensor_copy(ebc[:], ps_m[:, 8:10]))
                    g4 = ph.op("vector", sV, [g3b, g2],
                               lambda v: v.tensor_copy(cw8T[:], ps_c8[:, 0:1]))
                    h1m = ph.op("tensor", sP, [g4],
                                lambda t: t.matmul(ps_r[0:1, 0:512], cw8T[:], b2sb[:, 0:512],
                                                   start=True, stop=True))
                    h1e = ph.op("vector", sV, [h1m],
                                lambda v: v.tensor_copy(b2mix[:, 0:512], ps_r[0:1, 0:512]))
                    h2m = ph.op("tensor", sP, [h1e],
                                lambda t: t.matmul(ps_r[0:1, 0:512], cw8T[:], b2sb[:, 512:1024],
                                                   start=True, stop=True))
                    h2e = ph.op("vector", sV, [h2m],
                                lambda v: v.tensor_copy(b2mix[:, 512:1024], ps_r[0:1, 0:512]))
                    b2s = dsem("b2s")
                    b2mix_store = ph.op("sync", b2s, [h2e],
                                        lambda sy: sy.dma_start(b2mix_dram, b2mix[:]), 16)
                    ph.run("p1")

        with contextlib.ExitStack() as pp:
            ps_uv = [pp.enter_context(nc.psum_tensor(f"ps_uv{i}", [128, 512], F32)) for i in range(2)]
            with contextlib.ExitStack() as sb2:
                wt = [sb(sb2, f"uvw{i}", [128, 8 * 128], BF16) for i in range(2)]
                ev = [sb(sb2, f"uvev{i}", [128, 512], F32) for i in range(2)]
                evb = [sb(sb2, f"uvevb{i}", [128, 512], BF16) for i in range(2)]
                ph = Phase(nc)
                wts = [dsem("wt0"), dsem("wt1")]
                evs_s = [dsem("evs0"), dsem("evs1")]
                evbs_s = [dsem("evbs0"), dsem("evbs1")]
                mm_last = {}
                ev_hist2 = {}
                st_hist = {}
                for fb in range(16):
                    wdeps = [mm_last.get(fb - 2)] if fb >= 2 else [b2mix_store]
                    wl = ph.op("gpsimd", wts[fb % 2], wdeps,
                               lambda g, fb=fb: g.dma_start(
                                   wt[fb % 2][:],
                                   uv_w[:, fb * 128:(fb + 1) * 128].rearrange("(c p) f -> p c f", p=128)), 16)
                    for tc in range(8):
                        idx = fb * 8 + tc
                        mdeps = [wl]
                        if idx >= 2:
                            mdeps.append(ev_hist2[idx - 2])
                        last = None
                        for dc in range(8):
                            last = ph.op("tensor", sP, mdeps if dc == 0 else [],
                                         lambda t, fb=fb, tc=tc, dc=dc: t.matmul(
                                             ps_uv[tc % 2][:], wt[fb % 2][:, dc * 128:(dc + 1) * 128],
                                             xnT[:, dc * NT + tc * 512: dc * NT + (tc + 1) * 512],
                                             start=(dc == 0), stop=(dc == 7)))
                        mm_last[fb] = last
                        edeps = [last]
                        if idx >= 2:
                            edeps.append(st_hist[idx - 2])
                        dst = ev if fb < 8 else evb
                        em = ph.op("scalar", sA, edeps,
                                   lambda sc, tc=tc, dst=dst: sc.activation(
                                       dst[tc % 2][:], ps_uv[tc % 2][:], AF.Silu))
                        if fb < 8:
                            stm = ph.op("sync", evs_s[tc % 2], [em],
                                        lambda sy, fb=fb, tc=tc: sy.dma_start(
                                            u_dram[fb * 128:(fb + 1) * 128, tc * 512:(tc + 1) * 512],
                                            ev[tc % 2][:]), 16)
                        else:
                            stm = ph.op("sync", evbs_s[tc % 2], [em],
                                        lambda sy, fb=fb, tc=tc: sy.dma_start(
                                            v_dram[(fb - 8) * 128:(fb - 7) * 128, tc * 512:(tc + 1) * 512],
                                            evb[tc % 2][:]), 16)
                        ev_hist2[idx] = em
                        st_hist[idx] = stm
                u_stores = [evs_s[0].tok(), evs_s[1].tok()]
                v_stores = [evbs_s[0].tok(), evbs_s[1].tok()]
                ph.run("p2")

        with contextlib.ExitStack() as pp:
            ps_t32 = [pp.enter_context(nc.psum_tensor(f"ps_t32_{i}", [32, 128], F32)) for i in range(2)]
            ps_h32 = [pp.enter_context(nc.psum_tensor(f"ps_h32_{i}", [128, 32], F32)) for i in range(2)]
            ps_tg = [pp.enter_context(nc.psum_tensor(f"ps_tg{i}", [128, 512], F32)) for i in range(2)]
            with contextlib.ExitStack() as sb3:
                posx = sb(sb3, "posx", [128, 2048], F32)
                decayx = sb(sb3, "decayx", [128, 2048], F32)
                h1 = sb(sb3, "h1", [128, 2048], F32)
                h2 = sb(sb3, "h2", [128, 2048], F32)
                hsq = sb(sb3, "hsq", [128, 2048], F32)
                st64 = sb(sb3, "st64", [128, 256], F32)
                hT = sb(sb3, "hT", [32, M], BF16)
                fcw = sb(sb3, "fcw", [32, 96], BF16)
                woutb = sb(sb3, "woutb", [32, D], BF16)
                lnrow = sb(sb3, "lnrow", [1, 9 * 32], F32)
                lnbc = sb(sb3, "lnbc", [128, 9 * 32], F32)
                tg = [sb(sb3, f"tg{i}", [128, 512], BF16) for i in range(2)]
                ph = Phase(nc)
                p3s = dsem("p3s")
                xds = dsem("xds")
                tgs = [dsem("tgs0"), dsem("tgs1")]
                xnT_dump = None
                for k in range(8):
                    xnT_dump = ph.op("sync", xds, [],
                                     lambda sy, k=k: sy.dma_start(
                                         xnT_dram[:, k * NT:(k + 1) * NT],
                                         xnT[:, k * NT:(k + 1) * NT]), 16)
                ph.op("sync", p3s, u_stores + v_stores,
                      lambda sy: sy.dma_start(posx[:], ci["posx"]), 16)
                ph.op("sync", p3s, [], lambda sy: sy.dma_start(decayx[:], ci["decayx"]), 16)
                p3g = dsem("p3g")
                for i in range(3):
                    ph.op("gpsimd", p3g, u_stores + v_stores if i == 0 else [],
                          lambda g, i=i: g.dma_start(fcw[:, i * 32:(i + 1) * 32],
                                                     rpe_fc_w[i]), 16)
                ph.op("gpsimd", p3g, [], lambda g: g.dma_start(woutb[:], rpe_out_w), 16)
                rows = [(0, rpe_ln_w, 0), (1, rpe_ln_w, 1), (2, rpe_ln_w, 2),
                        (3, rpe_ln_b, 0), (4, rpe_ln_b, 1), (5, rpe_ln_b, 2),
                        (6, rpe_oln_w, None), (7, rpe_oln_b, None), (8, rpe_pos_w, None)]
                p3_all = None
                for slot, src, ri in rows:
                    ap_src = src[ri:ri + 1, :] if ri is not None else src
                    p3_all = ph.op("sync", p3s, [], lambda sy, slot=slot, ap_src=ap_src: sy.dma_start(
                        lnrow[:, slot * 32:(slot + 1) * 32], ap_src), 16)
                bcd = None
                for slot in range(9):
                    bdeps = [p3_all]
                    if bcd is not None:
                        bdeps.append(bcd)
                    bm = ph.op("tensor", sP, bdeps,
                               lambda t, slot=slot: t.matmul(ps_tg[slot % 2][:, 0:32], C["onesrow"][:],
                                                             lnrow[0:1, slot * 32:(slot + 1) * 32],
                                                             start=True, stop=True))
                    bcd = ph.op("vector", sV, [bm],
                                lambda v, slot=slot: v.tensor_copy(lnbc[:, slot * 32:(slot + 1) * 32],
                                                                   ps_tg[slot % 2][:, 0:32]))

                def r3ap(t_ap):
                    return t_ap.rearrange("p (g j) -> p g j", j=32)

                def ln_silu(hin, wslot, bslot, dep0):
                    q1 = ph.op("vector", sV, [dep0, bcd],
                               lambda v: v.tensor_reduce(st64[:, 0:64], r3ap(hin.ap()), AX.X, ALU.add))
                    q2 = ph.op("vector", sV, [q1],
                               lambda v: v.tensor_scalar_mul(st64[:, 0:64], st64[:, 0:64], 1.0 / 32))
                    q3 = ph.op("vector", sV, [q2],
                               lambda v: v.tensor_tensor(
                                   r3ap(hin.ap()), r3ap(hin.ap()),
                                   st64[:, 0:64].unsqueeze(2).broadcast_to([128, 64, 32]), ALU.subtract))
                    q4 = ph.op("scalar", sA, [q3],
                               lambda sc: sc.activation(hsq[:], hin[:], AF.Square))
                    q5 = ph.op("vector", sV, [q4],
                               lambda v: v.tensor_reduce(st64[:, 64:128], r3ap(hsq.ap()), AX.X, ALU.add))
                    q6 = ph.op("vector", sV, [q5],
                               lambda v: v.tensor_scalar(st64[:, 64:128], st64[:, 64:128], 1.0 / 32,
                                                         EPS, ALU.mult, ALU.add))
                    q7 = ph.op("scalar", sA, [q6],
                               lambda sc: sc.activation(st64[:, 128:192], st64[:, 64:128], AF.Sqrt))
                    q8 = ph.op("vector", sV, [q7],
                               lambda v: v.reciprocal(st64[:, 192:256], st64[:, 128:192]))
                    q9 = ph.op("vector", sV, [q8],
                               lambda v: v.tensor_tensor(
                                   r3ap(hin.ap()), r3ap(hin.ap()),
                                   st64[:, 192:256].unsqueeze(2).broadcast_to([128, 64, 32]), ALU.mult))
                    q10 = ph.op("vector", sV, [q9],
                                lambda v: v.tensor_tensor(
                                    r3ap(hin.ap()), r3ap(hin.ap()),
                                    lnbc[:, wslot * 32:(wslot + 1) * 32].unsqueeze(1).broadcast_to([128, 64, 32]),
                                    ALU.mult))
                    q11 = ph.op("vector", sV, [q10],
                                lambda v: v.tensor_tensor(
                                    r3ap(hin.ap()), r3ap(hin.ap()),
                                    lnbc[:, bslot * 32:(bslot + 1) * 32].unsqueeze(1).broadcast_to([128, 64, 32]),
                                    ALU.add))
                    q12 = ph.op("scalar", sA, [q11],
                                lambda sc: sc.activation(hin[:], hin[:], AF.Silu))
                    return q12

                h0 = ph.op("vector", sV, [bcd, p3_all, p3g.tok()],
                           lambda v: v.tensor_tensor(
                               r3ap(h1.ap()), r3ap(posx.ap()),
                               lnbc[:, 8 * 32:9 * 32].unsqueeze(1).broadcast_to([128, 64, 32]), ALU.mult))
                dep = h0
                hin = h1
                for layer in range(3):
                    hout = h2 if hin is h1 else h1
                    sdone = ln_silu(hin, layer, 3 + layer, dep)
                    evl = {}
                    for pt in range(64):
                        pd = [sdone] if pt == 0 else []
                        if pt >= 2:
                            pd.append(evl[pt - 2])
                        pm = ph.op("tensor", sP, pd,
                                   lambda t, pt=pt, hin=hin: t.transpose(
                                       ps_t32[pt % 2][:], hin[:, pt * 32:(pt + 1) * 32], C["identf"][:]))
                        evl[pt] = ph.op("scalar", sA, [pm],
                                        lambda sc, pt=pt: sc.activation(
                                            hT[:, pt * 128:(pt + 1) * 128], ps_t32[pt % 2][:], AF.Copy))
                    ev2 = {}
                    for pt in range(64):
                        pd = [evl[63]] if pt == 0 else []
                        if pt >= 2:
                            pd.append(ev2[pt - 2])
                        pm = ph.op("tensor", sP, pd,
                                   lambda t, pt=pt, layer=layer: t.matmul(
                                       ps_h32[pt % 2][:], hT[:, pt * 128:(pt + 1) * 128],
                                       fcw[:, layer * 32:(layer + 1) * 32], start=True, stop=True))
                        ev2[pt] = ph.op("scalar", sA, [pm],
                                        lambda sc, pt=pt, hout=hout: sc.activation(
                                            hout[:, pt * 32:(pt + 1) * 32], ps_h32[pt % 2][:], AF.Copy))
                    dep = ev2[63]
                    hin = hout
                sdone = ln_silu(hin, 6, 7, dep)
                pg = ph.op("vector", sV, [sdone],
                           lambda v, hin=hin: v.tensor_tensor(hin[:], hin[:], decayx[:], ALU.mult))
                evl = {}
                for pt in range(64):
                    pd = [pg] if pt == 0 else []
                    if pt >= 2:
                        pd.append(evl[pt - 2])
                    pm = ph.op("tensor", sP, pd,
                               lambda t, pt=pt, hin=hin: t.transpose(
                                   ps_t32[pt % 2][:], hin[:, pt * 32:(pt + 1) * 32], C["identf"][:]))
                    evl[pt] = ph.op("scalar", sA, [pm],
                                    lambda sc, pt=pt: sc.activation(
                                        hT[:, pt * 128:(pt + 1) * 128], ps_t32[pt % 2][:], AF.Copy))
                tev = {}
                tstv = {}
                for cb in range(8):
                    for mc2 in range(16):
                        i = cb * 16 + mc2
                        pd = [evl[63]] if i == 0 else []
                        if i >= 2:
                            pd.append(tev[i - 2])
                        pm = ph.op("tensor", sP, pd,
                                   lambda t, cb=cb, mc2=mc2, i=i: t.matmul(
                                       ps_tg[i % 2][:], woutb[:, cb * 128:(cb + 1) * 128],
                                       hT[:, mc2 * 512:(mc2 + 1) * 512], start=True, stop=True))
                        ed = [pm]
                        if i >= 2:
                            ed.append(tstv[i - 2])
                        em = ph.op("scalar", sA, ed,
                                   lambda sc, i=i: sc.activation(tg[i % 2][:], ps_tg[i % 2][:], AF.Copy))
                        sm = ph.op("sync", tgs[i % 2], [em],
                                   lambda sy, cb=cb, mc2=mc2, i=i: sy.dma_start(
                                       t_dram[cb * 128:(cb + 1) * 128, mc2 * 512:(mc2 + 1) * 512],
                                       tg[i % 2][:]), 16)
                        tev[i] = em
                        tstv[i] = sm
                t_stores = [tgs[0].tok(), tgs[1].tok()]
                xd_total = xds.tok()
                ph.run("p3")

        xnT_stack.close()
        with contextlib.ExitStack() as pp:
            psE4 = [pp.enter_context(nc.psum_tensor(f"psE{i}", [128, 512], F32)) for i in range(4)]
            psY4 = pp.enter_context(nc.psum_tensor("psY4", [64, 2048], F32))
            with contextlib.ExitStack() as sb4:
                vsl = [sb(sb4, f"vsl{i}", [64, 4096], BF16) for i in range(2)]
                tsl = [sb(sb4, f"tsl{i}", [128, 4096], BF16) for i in range(2)]
                bA = [[sb(sb4, f"bA{p}_{i}", [128, 4096], BF16) for i in range(2)] for p in range(2)]
                bB = [[sb(sb4, f"bB{p}_{i}", [128, 4096], BF16) for i in range(2)] for p in range(2)]
                bC = [[sb(sb4, f"bC{p}_{i}", [128, 4096], BF16) for i in range(2)] for p in range(2)]
                ysb = [sb(sb4, f"ysb{p}", [64, 4096], F32) for p in range(2)]
                ph = Phase(nc)
                vls = [dsem("vl0"), dsem("vl1")]
                tls = [dsem("tl0"), dsem("tl1")]
                yss = dsem("ys")
                st_ = {}
                ydma_hist = {}
                g3 = lambda ap_: ap_.rearrange("p (g j) -> p g j", j=128)
                twcB = C["twc"].ap().unsqueeze(1).broadcast_to([128, 32, 128])
                twsB = C["tws"].ap().unsqueeze(1).broadcast_to([128, 32, 128])
                t2cB = C["tw2c"].ap().unsqueeze(1).broadcast_to([128, 32, 128])
                t2sB = C["tw2s"].ap().unsqueeze(1).broadcast_to([128, 32, 128])

                def step1(slab, nb, ldtok, dst, psfree, bufree):
                    evs = []
                    last = None
                    for gb in range(8):
                        pe0 = psE4[2 * (gb % 2)]
                        pe1 = psE4[2 * (gb % 2) + 1]
                        for g4 in range(4):
                            g = gb * 4 + g4
                            dep = []
                            if gb == 0 and g4 == 0:
                                dep = [ldtok] + psfree
                            if g4 == 0 and gb >= 2:
                                dep.append(evs[gb - 2])
                            lhs = slab.ap().rearrange("b (d a) -> b d a", a=64)[:, 2 * g:2 * g + 2, :]
                            ph.op("tensor", sP, dep,
                                  lambda t, lhs=lhs, g4=g4, nb=nb, pe0=pe0: t.matmul(
                                      pe0[:, g4 * 128:(g4 + 1) * 128], lhs,
                                      C["f1c"][0:nb, :], start=True, stop=True))
                            last = ph.op("tensor", sP, [],
                                         lambda t, lhs=lhs, g4=g4, nb=nb, pe1=pe1: t.matmul(
                                             pe1[:, g4 * 128:(g4 + 1) * 128], lhs,
                                             C["f1s"][0:nb, :], start=True, stop=True))
                        ed0 = [last] + (bufree if gb == 0 else [])
                        ph.op("scalar", sA, ed0,
                              lambda sc, gb=gb, dst=dst, pe0=pe0: sc.activation(
                                  dst[0][:, gb * 512:(gb + 1) * 512], pe0[:], AF.Copy))
                        e2_ = ph.op("scalar", sA, [],
                                    lambda sc, gb=gb, dst=dst, pe1=pe1: sc.activation(
                                        dst[1][:, gb * 512:(gb + 1) * 512], pe1[:], AF.Copy))
                        evs.append(e2_)
                    return last, evs[-1]

                def cmul6(deps, ar, ai, br_c, bs_c, outr, outi, tmp):
                    w1 = ph.op("vector", sV, deps,
                               lambda v: v.tensor_tensor(g3(outr.ap()), g3(ar.ap()), br_c, ALU.mult))
                    w2 = ph.op("vector", sV, [w1],
                               lambda v: v.tensor_tensor(g3(outi.ap()), g3(ai.ap()), bs_c, ALU.mult))
                    w3 = ph.op("vector", sV, [w2],
                               lambda v: v.tensor_tensor(outr[:], outr[:], outi[:], ALU.subtract))
                    w4 = ph.op("vector", sV, [w3],
                               lambda v: v.tensor_tensor(g3(outi.ap()), g3(ar.ap()), bs_c, ALU.mult))
                    w5 = ph.op("vector", sV, [w4],
                               lambda v: v.tensor_tensor(g3(ar.ap()), g3(ai.ap()), br_c, ALU.mult))
                    w6 = ph.op("vector", sV, [w5],
                               lambda v: v.tensor_tensor(outi[:], outi[:], ar[:], ALU.add))
                    return w6, w5

                def stepC(src, dst, dep0):
                    evs = []
                    for gb in range(8):
                        pe0 = psE4[2 * (gb % 2)]
                        pe1 = psE4[2 * (gb % 2) + 1]
                        last = None
                        for g4 in range(4):
                            g = gb * 4 + g4
                            dep = []
                            if g4 == 0:
                                dep = list(dep0) if gb == 0 else [evs[gb - 2]] if gb >= 2 else []
                            sl_re = src[0][:, g * 128:(g + 1) * 128]
                            sl_im = src[1][:, g * 128:(g + 1) * 128]
                            ph.op("tensor", sP, dep,
                                  lambda t, sl_re=sl_re, g4=g4, pe0=pe0: t.matmul(
                                      pe0[:, g4 * 128:(g4 + 1) * 128], C["bdec"][:], sl_re,
                                      start=True, stop=False))
                            ph.op("tensor", sP, [],
                                  lambda t, sl_im=sl_im, g4=g4, pe0=pe0: t.matmul(
                                      pe0[:, g4 * 128:(g4 + 1) * 128], C["bdes"][:], sl_im,
                                      start=False, stop=True))
                            ph.op("tensor", sP, [],
                                  lambda t, sl_im=sl_im, g4=g4, pe1=pe1: t.matmul(
                                      pe1[:, g4 * 128:(g4 + 1) * 128], C["bdec"][:], sl_im,
                                      start=True, stop=False))
                            last = ph.op("tensor", sP, [],
                                         lambda t, sl_re=sl_re, g4=g4, pe1=pe1: t.matmul(
                                             pe1[:, g4 * 128:(g4 + 1) * 128], C["bdesn"][:], sl_re,
                                             start=False, stop=True))
                        ph.op("scalar", sA, [last],
                              lambda sc, gb=gb, dst=dst, pe0=pe0: sc.activation(
                                  dst[0][:, gb * 512:(gb + 1) * 512], pe0[:], AF.Copy))
                        e2_ = ph.op("scalar", sA, [],
                                    lambda sc, gb=gb, dst=dst, pe1=pe1: sc.activation(
                                        dst[1][:, gb * 512:(gb + 1) * 512], pe1[:], AF.Copy))
                        evs.append(e2_)
                    return last, evs[-1]

                def stepA(src, dst, dep0):
                    evs = []
                    for gb in range(8):
                        pe0 = psE4[2 * (gb % 2)]
                        pe1 = psE4[2 * (gb % 2) + 1]
                        last = None
                        for g4 in range(4):
                            g = gb * 4 + g4
                            dep = []
                            if g4 == 0:
                                dep = list(dep0) if gb == 0 else [evs[gb - 2]] if gb >= 2 else []
                            yre = src[0][:, g * 128:(g + 1) * 128]
                            yim = src[1][:, g * 128:(g + 1) * 128]
                            ph.op("tensor", sP, dep,
                                  lambda t, yre=yre, g4=g4, pe0=pe0: t.matmul(
                                      pe0[:, g4 * 128:(g4 + 1) * 128], yre, C["bdec"][:],
                                      start=True, stop=False))
                            ph.op("tensor", sP, [],
                                  lambda t, yim=yim, g4=g4, pe0=pe0: t.matmul(
                                      pe0[:, g4 * 128:(g4 + 1) * 128], yim, C["bdesn"][:],
                                      start=False, stop=True))
                            ph.op("tensor", sP, [],
                                  lambda t, yre=yre, g4=g4, pe1=pe1: t.matmul(
                                      pe1[:, g4 * 128:(g4 + 1) * 128], yre, C["bdes"][:],
                                      start=True, stop=False))
                            last = ph.op("tensor", sP, [],
                                         lambda t, yim=yim, g4=g4, pe1=pe1: t.matmul(
                                             pe1[:, g4 * 128:(g4 + 1) * 128], yim, C["bdec"][:],
                                             start=False, stop=True))
                        ph.op("scalar", sA, [last],
                              lambda sc, gb=gb, dst=dst, pe0=pe0: sc.activation(
                                  dst[0][:, gb * 512:(gb + 1) * 512], pe0[:], AF.Copy))
                        e2_ = ph.op("scalar", sA, [],
                                    lambda sc, gb=gb, dst=dst, pe1=pe1: sc.activation(
                                        dst[1][:, gb * 512:(gb + 1) * 512], pe1[:], AF.Copy))
                        evs.append(e2_)
                    return last, evs[-1]

                def emit_A_B(cp):
                    pr = cp % 2
                    stc = st_[cp]
                    la, ea = stepA(bB[pr], bA[pr], [stc["p6"], stc["psfree_for_A"]])
                    stc["stA_last"] = la
                    stc["stA_ev"] = ea
                    return la, ea

                def emit_B(cp):
                    pr = cp % 2
                    stc = st_[cp]
                    for hb in range(2):
                        lastY = None
                        dep0 = [stc["iv6"]]
                        pv = ydma_hist.get((cp, hb - 1), ydma_hist.get((cp - 1, 1)))
                        if pv:
                            dep0.append(pv)
                        for g16 in range(16):
                            g = hb * 16 + g16
                            ph.op("tensor", sP, dep0 if g16 == 0 else [],
                                  lambda t, g=g, g16=g16, pr=pr: t.matmul(
                                      psY4[:, g16 * 128:(g16 + 1) * 128], C["c2c"][:],
                                      bC[pr][0][:, g * 128:(g + 1) * 128], start=True, stop=False))
                            lastY = ph.op("tensor", sP, [],
                                          lambda t, g=g, g16=g16, pr=pr: t.matmul(
                                              psY4[:, g16 * 128:(g16 + 1) * 128], C["c2s"][:],
                                              bC[pr][1][:, g * 128:(g + 1) * 128], start=False, stop=True))
                        yed = [lastY]
                        pv2 = ydma_hist.get((cp - 2, 1))
                        if pv2:
                            yed.append(pv2)
                        ye = ph.op("scalar", sA, yed,
                                   lambda sc, hb=hb, pr=pr: sc.activation(
                                       ysb[pr][:, hb * 2048:(hb + 1) * 2048], psY4[:], AF.Copy))
                        ydma = ph.op("sync", yss, [ye],
                                     lambda g_, cp=cp, hb=hb, pr=pr: g_.dma_start(
                                         bass.AP(y_dram.tensor, (cp * 64 + hb * 32) * NT,
                                                 [[64, 64], [2 * NT, 16], [NT, 2], [1, 64]]),
                                         bass.AP(ysb[pr], hb * 2048,
                                                 [[4096, 64], [128, 16], [64, 2], [1, 64]])), 16)
                        ydma_hist[(cp, hb)] = ydma

                for ct in range(16):
                    par = ct % 2
                    st_[ct] = {}
                    vdep = [st_[ct - 2]["s1v_last"]] if ct >= 2 else list(v_stores) + [xd_total]
                    vld = ph.op("gpsimd", vls[par], vdep,
                                lambda g, ct=ct, par=par: g.dma_start(
                                    vsl[par].ap().rearrange("b (d a) -> b d a", a=64),
                                    bass.AP(v_dram.tensor, ct * 64 * NT,
                                            [[64, 64], [NT, 64], [1, 64]])), 16)
                    tdep = [st_[ct - 2]["s1t_last"]] if ct >= 2 else list(t_stores) + [xd_total]
                    tld = ph.op("sync", tls[par], tdep,
                                lambda sy, ct=ct, par=par: sy.dma_start(
                                    tsl[par].ap().rearrange("b (d a) -> b d a", a=64),
                                    bass.AP(t_dram.tensor, ct * 64 * M,
                                            [[64, 128], [M, 64], [1, 64]])), 16)
                    psfree = [st_[ct - 1]["sCT_ev"]] if ct >= 1 else [const_ld]
                    bufree = [st_[ct - 2]["iv5"]] if ct >= 2 else []
                    l1, e1 = step1(vsl[par], 64, vld, bA[par], psfree, bufree)
                    st_[ct]["s1v_last"] = l1
                    twdeps = [e1]
                    if ct >= 2:
                        twdeps.append(st_[ct - 2]["stA_last"])
                    w6, _ = cmul6(twdeps, bA[par][0], bA[par][1], twcB, twsB,
                                  bB[par][0], bB[par][1], None)
                    lcv, xev = stepC(bB[par], bA[par], [w6])
                    st_[ct]["sCv_last"] = lcv
                    if ct >= 1:
                        st_[ct - 1]["psfree_for_A"] = xev
                        emit_A_B(ct - 1)
                        pr = (ct - 1) % 2
                        iv6, iv5 = cmul6([st_[ct - 1]["stA_ev"]],
                                         bA[pr][0], bA[pr][1], t2cB, t2sB,
                                         bC[pr][0], bC[pr][1], None)
                        st_[ct - 1]["iv6"] = iv6
                        st_[ct - 1]["iv5"] = iv5
                    psfreeT = [st_[ct - 1]["stA_ev"]] if ct >= 1 else [xev]
                    l2, e2 = step1(tsl[par], 128, tld, bC[par], psfreeT, [])
                    st_[ct]["s1t_last"] = l2
                    w6t, _ = cmul6([e2], bC[par][0], bC[par][1], twcB, twsB,
                                   bB[par][0], bB[par][1], None)
                    lct, aev = stepC(bB[par], bC[par], [w6t])
                    st_[ct]["sCT_last"] = lct
                    st_[ct]["sCT_ev"] = aev
                    if ct >= 1:
                        emit_B(ct - 1)
                    pwdeps = [xev, aev]
                    if ct >= 1:
                        pwdeps.append(st_[ct - 1]["stA_last"])
                    p1_ = ph.op("vector", sV, pwdeps,
                                lambda v, par=par: v.tensor_tensor(bB[par][0][:], bA[par][0][:],
                                                                   bC[par][0][:], ALU.mult))
                    p2_ = ph.op("vector", sV, [p1_],
                                lambda v, par=par: v.tensor_tensor(bB[par][1][:], bA[par][1][:],
                                                                   bC[par][1][:], ALU.mult))
                    p3_ = ph.op("vector", sV, [p2_],
                                lambda v, par=par: v.tensor_tensor(bB[par][0][:], bB[par][0][:],
                                                                   bB[par][1][:], ALU.subtract))
                    p4_ = ph.op("vector", sV, [p3_],
                                lambda v, par=par: v.tensor_tensor(bB[par][1][:], bA[par][0][:],
                                                                   bC[par][1][:], ALU.mult))
                    p5_ = ph.op("vector", sV, [p4_],
                                lambda v, par=par: v.tensor_tensor(bA[par][0][:], bA[par][1][:],
                                                                   bC[par][0][:], ALU.mult))
                    p6_ = ph.op("vector", sV, [p5_],
                                lambda v, par=par: v.tensor_tensor(bB[par][1][:], bB[par][1][:],
                                                                   bA[par][0][:], ALU.add))
                    st_[ct]["p6"] = p6_
                st_[15]["psfree_for_A"] = st_[15]["sCT_ev"]
                emit_A_B(15)
                iv6, iv5 = cmul6([st_[15]["stA_ev"]], bA[1][0], bA[1][1], t2cB, t2sB,
                                 bC[1][0], bC[1][1], None)
                st_[15]["iv6"] = iv6
                st_[15]["iv5"] = iv5
                emit_B(15)
                y_stores = yss.tok()
                ph.run("p4")

        with contextlib.ExitStack() as pp:
            ps_s5 = [pp.enter_context(nc.psum_tensor(f"ps_s5{i}", [128, 512], F32)) for i in range(2)]
            ps_t5 = [pp.enter_context(nc.psum_tensor(f"ps_t5{i}", [128, 128], F32)) for i in range(2)]
            with contextlib.ExitStack() as sb5:
                o_sb = sb(sb5, "o_sb", [128, 8 * D], BF16)
                ysl = [sb(sb5, f"ysl{i}", [128, 512], F32) for i in range(2)]
                usl = [sb(sb5, f"usl{i}", [128, 512], F32) for i in range(2)]
                prod = sb(sb5, "prod", [128, 8 * 512], BF16)
                sev = [sb(sb5, f"sev{i}", [128, 512], F32) for i in range(2)]
                stt5 = [sb(sb5, f"stt5{i}", [128, 128], F32) for i in range(2)]
                ph = Phase(nc)
                ows = dsem("ow")
                yus = [dsem("yu0"), dsem("yu1")]
                sss = [dsem("ss0"), dsem("ss1")]
                owl = ph.op("gpsimd", ows, [],
                            lambda g: g.dma_start(o_sb[:], o_w.rearrange("(cc p) d -> p cc d", p=128)), 16)
                sd_list = []
                em_by_slot = {}
                smm_last = None
                prev_pr = None
                for mc in range(8):
                    pr = {}
                    for cb in range(8):
                        ldd = []
                        if mc == 0 and cb == 0:
                            ldd = [y_stores] + u_stores
                        if cb >= 2:
                            ldd.append(pr[cb - 2])
                        elif mc >= 1:
                            ldd.append(prev_pr[6 + cb])
                        ph.op("sync", yus[cb % 2], ldd,
                              lambda sy, mc=mc, cb=cb: sy.dma_start(
                                  ysl[cb % 2][:],
                                  y_dram[cb * 128:(cb + 1) * 128, mc * 512:(mc + 1) * 512]), 16)
                        ul = ph.op("sync", yus[cb % 2], [],
                                   lambda sy, mc=mc, cb=cb: sy.dma_start(
                                       usl[cb % 2][:],
                                       u_dram[cb * 128:(cb + 1) * 128, mc * 512:(mc + 1) * 512]), 16)
                        pdep = [ul]
                        if mc >= 1 and cb == 0:
                            pdep.append(smm_last)
                        pr[cb] = ph.op("vector", sV, pdep,
                                       lambda v, cb=cb: v.tensor_tensor(
                                           prod[:, cb * 512:(cb + 1) * 512], ysl[cb % 2][:],
                                           usl[cb % 2][:], ALU.mult))
                    prev_pr = pr
                    for db in range(8):
                        last = None
                        mdep = [pr[7], owl]
                        if em_by_slot.get(db % 2):
                            mdep.append(em_by_slot[db % 2])
                        for cb in range(8):
                            last = ph.op("tensor", sP, mdep if cb == 0 else [],
                                         lambda t, db=db, cb=cb: t.matmul(
                                             ps_s5[db % 2][:],
                                             o_sb[:, cb * D + db * 128: cb * D + (db + 1) * 128],
                                             prod[:, cb * 512:(cb + 1) * 512],
                                             start=(cb == 0), stop=(cb == 7)))
                        smm_last = last
                        em = ph.op("scalar", sA, [last],
                                   lambda sc, db=db: sc.activation(sev[db % 2][:], ps_s5[db % 2][:], AF.Copy))
                        st_ev = {}
                        for sub in range(4):
                            td = [em] if sub == 0 else []
                            if sub >= 2:
                                td.append(st_ev[sub - 2])
                            pm = ph.op("tensor", sP, td,
                                       lambda t, db=db, sub=sub: t.transpose(
                                           ps_t5[sub % 2][:], sev[db % 2][:, sub * 128:(sub + 1) * 128],
                                           C["identf"][:]))
                            ed = [pm]
                            if len(sd_list) >= 2:
                                ed.append(sd_list[-2])
                            e2 = ph.op("scalar", sA, ed,
                                       lambda sc, sub=sub: sc.activation(stt5[sub % 2][:],
                                                                         ps_t5[sub % 2][:], AF.Copy))
                            st_ev[sub] = e2
                            sd = ph.op("sync", sss[sub % 2], [e2],
                                       lambda sy, mc=mc, db=db, sub=sub: sy.dma_start(
                                           s_dram[mc * 512 + sub * 128: mc * 512 + (sub + 1) * 128,
                                                  db * 128:(db + 1) * 128], stt5[sub % 2][:]), 16)
                            sd_list.append(sd)
                        em_by_slot[db % 2] = em
                s_stores = [sss[0].tok(), sss[1].tok()]
                ph.run("p5")

        with contextlib.ExitStack() as pp:
            ps_a6 = [pp.enter_context(nc.psum_tensor(f"ps_a6{i}", [128, 256], F32)) for i in range(2)]
            ps_b6 = [pp.enter_context(nc.psum_tensor(f"ps_b6{i}", [128, 512], F32)) for i in range(4)]
            ps_b1t = pp.enter_context(nc.psum_tensor("ps_b1t", [128, 32], F32))
            with contextlib.ExitStack() as sb6:
                w1sb = sb(sb6, "w1sb", [128, 8 * FF], BF16)
                w2sb = sb(sb6, "w2sb", [128, 32 * 1024], BF16)
                stage = [sb(sb6, f"stage{i}", [128, FF], F32) for i in range(2)]
                xtb = [sb(sb6, f"xtb{i}", [128, 2048], BF16) for i in range(2)]
                hcur = [sb(sb6, f"hcur{i}", [128, 256], BF16) for i in range(2)]
                b1T = sb(sb6, "b1T", [32, 128], F32)
                b1sb = sb(sb6, "b1sb", [128, 32], F32)
                mev = sb(sb6, "mev", [128, 2048], F32)
                gixf = sb(sb6, "gixf", [128, 48], F32)
                gixi = sb(sb6, "gixi", [128, 2 * 48], mybir.dt.int32)
                ph = Phase(nc)
                gst = [dsem("gst0"), dsem("gst1")]
                b1s = dsem("b1s")
                xtbs = [dsem("xtb0"), dsem("xtb1")]
                mas = dsem("mas")
                gix_done = None
                for j in range(2):
                    q1 = ph.op("vector", sV, [gix_done] if gix_done else [const_ld] + list(s_stores),
                               lambda v, j=j: v.tensor_scalar(
                                   gixf[:], C["gscale"][:], ebc[:, j:j + 1], None, ALU.mult))
                    q2 = ph.op("vector", sV, [q1],
                               lambda v: v.tensor_tensor(gixf[:], gixf[:], C["gidx_base"][:], ALU.add))
                    gix_done = ph.op("vector", sV, [q2],
                                     lambda v, j=j: v.tensor_copy(gixi[:, j * 48:(j + 1) * 48], gixf[:]))
                W1v = W1.rearrange("e d f -> (e d) f")
                W2v = W2.rearrange("e f d -> (e f) d")
                b1v = b1.rearrange("e (fc p) -> (e fc) p", p=128)
                prev_expert_pe = None
                prev_bev = None
                moe_hist = {}
                mev_done = None
                for e in range(2):
                    cast_last = None
                    gmark = {}
                    for c in range(40):
                        si = c % 2
                        gdeps = [gix_done] if c == 0 else []
                        if prev_expert_pe is not None and c <= 1:
                            gdeps.append(prev_expert_pe)
                        if c >= 2:
                            gdeps.append(gmark[c - 2])
                        if c < 8:
                            gl = ph.op("gpsimd", gst[si], gdeps,
                                       lambda g, e=e, c=c, si=si: g.indirect_dma_start(
                                           stage[si][:], None, W1v,
                                           bass.IndirectOffsetOnAxis(gixi[:, e * 48 + c: e * 48 + c + 1], 0)), 16)
                            cm = ph.op("vector" if c % 4 < 2 else "scalar",
                                       sV if c % 4 < 2 else sA, [gl],
                                       (lambda v, c=c, si=si: v.tensor_copy(
                                           w1sb[:, c * FF:(c + 1) * FF], stage[si][:]))
                                       if c % 4 < 2 else
                                       (lambda sc, c=c, si=si: sc.activation(
                                           w1sb[:, c * FF:(c + 1) * FF], stage[si][:], AF.Copy)))
                        else:
                            fc = c - 8
                            gl = ph.op("gpsimd", gst[si], gdeps,
                                       lambda g, e=e, fc=fc, si=si: g.indirect_dma_start(
                                           stage[si][:, 0:1024], None, W2v,
                                           bass.IndirectOffsetOnAxis(gixi[:, e * 48 + 8 + fc: e * 48 + 9 + fc], 0)), 16)
                            cm = ph.op("vector" if c % 4 < 2 else "scalar",
                                       sV if c % 4 < 2 else sA, [gl],
                                       (lambda v, fc=fc, si=si: v.tensor_copy(
                                           w2sb[:, fc * 1024:(fc + 1) * 1024], stage[si][:, 0:1024]))
                                       if c % 4 < 2 else
                                       (lambda sc, fc=fc, si=si: sc.activation(
                                           w2sb[:, fc * 1024:(fc + 1) * 1024], stage[si][:, 0:1024], AF.Copy)))
                        gmark[c] = cm
                        if c % 4 < 2:
                            cast_dve_last = cm
                        else:
                            cast_act_last = cm
                    wl1 = cast_dve_last
                    wl2 = cast_act_last
                    bl = ph.op("gpsimd", b1s, [gix_done] + ([prev_bev] if prev_bev else []),
                               lambda g, e=e: g.indirect_dma_start(
                                   b1T[:], None, b1v,
                                   bass.IndirectOffsetOnAxis(gixi[0:32, e * 48 + 40: e * 48 + 41], 0)), 16)
                    bdeps = [bl]
                    if prev_bev is not None:
                        bdeps.append(prev_bev)
                    btr = ph.op("tensor", sP, bdeps,
                                lambda t: t.transpose(ps_b1t[:], b1T[:], C["identf"][0:32, 0:32]))
                    bev = ph.op("vector", sV, [btr],
                                lambda v: v.tensor_copy(b1sb[:], ps_b1t[:]))
                    prev_bev = bev
                    last_b_mm = None
                    lastA_by_tb = {}
                    for TB in range(16):
                        xdeps = [xd_total] + list(s_stores) if (e == 0 and TB == 0) else []
                        if TB >= 2:
                            xdeps.append(lastA_by_tb[TB - 2])
                        elif e > 0:
                            xdeps.append(prev_expert_pe)
                        xld = ph.op("sync", xtbs[TB % 2], xdeps,
                                    lambda sy, TB=TB: sy.dma_start(
                                        xtb[TB % 2].ap().rearrange("p (dc t) -> p dc t", t=256),
                                        xnT_dram.rearrange("p (dc m) -> p dc m", m=NT)[:, :,
                                                                                       TB * 256:(TB + 1) * 256]), 16)
                        gel = {}

                        def emit_B(fc, TB=TB):
                            bdeps2 = [gel[fc]]
                            if fc == 0:
                                bdeps2.append(wl2)
                                if mev_done is not None:
                                    bdeps2.append(mev_done)
                            lastB = None
                            for tc2 in range(2):
                                for dh in range(2):
                                    j = tc2 * 2 + dh
                                    lastB = ph.op("tensor", sP, bdeps2 if j == 0 else [],
                                                  lambda t, fc=fc, tc2=tc2, dh=dh, j=j: t.matmul(
                                                      ps_b6[j][:],
                                                      hcur[fc % 2][:, tc2 * 128:(tc2 + 1) * 128],
                                                      w2sb[:, fc * 1024 + dh * 512: fc * 1024 + (dh + 1) * 512],
                                                      start=(fc == 0), stop=(fc == 31)))
                            return lastB

                        for fc in range(32):
                            adeps = []
                            if fc == 0:
                                adeps = [wl1, wl2, bev, xld]
                            if fc >= 2:
                                adeps.append(gel[fc - 2])
                            lastA = None
                            for dc in range(8):
                                lastA = ph.op("tensor", sP, adeps if dc == 0 else [],
                                              lambda t, TB=TB, fc=fc, dc=dc: t.matmul(
                                                  ps_a6[fc % 2][:],
                                                  w1sb[:, dc * FF + fc * 128: dc * FF + (fc + 1) * 128],
                                                  xtb[TB % 2][:, dc * 256:(dc + 1) * 256],
                                                  start=(dc == 0), stop=(dc == 7)))
                            lastA_by_tb[TB] = lastA
                            if fc >= 1:
                                emit_B(fc - 1)
                            gel[fc] = ph.op("scalar", sA, [lastA],
                                            lambda sc, fc=fc: sc.activation(
                                                hcur[fc % 2][:], ps_a6[fc % 2][:], AF.Gelu,
                                                bias=b1sb[:, fc:fc + 1]))
                        last_b_mm = emit_B(31)
                        for j in range(4):
                            mdep = [last_b_mm]
                            pv = moe_hist.get((e, TB - 1), moe_hist.get((e - 1, 15)))
                            if pv:
                                mdep.append(pv)
                            mev_done = ph.op("vector", sV, mdep,
                                             lambda v, j=j, e=e: v.tensor_scalar(
                                                 mev[:, j * 512:(j + 1) * 512], ps_b6[j][:],
                                                 cwslbc[:, e:e + 1], None, ALU.mult))
                        md = ph.op("gpsimd", mas, [mev_done],
                                   lambda g, e=e, TB=TB: g.dma_start(
                                       bass.AP(moe_dram.tensor, TB * 256 * D,
                                               [[D, 128], [128 * D, 2], [512, 2], [1, 512]]),
                                       mev.ap().rearrange("p (tc dh c) -> p tc dh c", dh=2, c=512),
                                       accum_op=(ALU.add if e > 0 else ALU.bypass)), 16)
                        moe_hist[(e, TB)] = md
                    prev_expert_pe = last_b_mm
                moe_done = mas.tok()
                ph.run("p6")

        with contextlib.ExitStack() as pp:
            ps_bc7 = pp.enter_context(nc.psum_tensor("ps_bc7", [128, 512], F32))
            with contextlib.ExitStack() as sb7:
                gbc = sb(sb7, "gbc", [128, D], F32)
                bbc = sb(sb7, "bbc", [128, D], F32)
                b2bc = sb(sb7, "b2bc", [128, D], F32)
                ytile = [sb(sb7, f"ytile{i}", [128, D], F32) for i in range(2)]
                snt = [sb(sb7, f"snt{i}", [128, D], F32) for i in range(2)]
                xt7 = [sb(sb7, f"xt7{i}", [128, D], F32) for i in range(2)]
                gt = [sb(sb7, f"gt{i}", [128, D], F32) for i in range(2)]
                ot = [sb(sb7, f"ot{i}", [128, D], F32) for i in range(2)]
                st7 = [sb(sb7, f"st7{i}", [128, 8], F32) for i in range(2)]
                grow = sb(sb7, "grow", [1, D], F32)
                brow = sb(sb7, "brow", [1, D], F32)
                b2r = sb(sb7, "b2r", [1, D], F32)
                ph = Phase(nc)
                p7s = dsem("p7s")
                yts = [dsem("yt70"), dsem("yt71")]
                sns = [dsem("sn0"), dsem("sn1")]
                x7s = [dsem("x70"), dsem("x71")]
                oss = [dsem("os0"), dsem("os1")]
                ph.op("sync", p7s, [], lambda sy: sy.dma_start(grow[:], gamma), 16)
                ph.op("sync", p7s, [], lambda sy: sy.dma_start(brow[:], beta), 16)
                rows_ld = ph.op("sync", p7s, [b2mix_store],
                                lambda sy: sy.dma_start(b2r[:], b2mix_dram), 16)
                bcv = None
                for row, dstt in [(grow, gbc), (brow, bbc), (b2r, b2bc)]:
                    for half in range(2):
                        bdeps = [rows_ld]
                        if bcv is not None:
                            bdeps.append(bcv)
                        bm = ph.op("tensor", sP, bdeps,
                                   lambda t, row=row, half=half: t.matmul(
                                       ps_bc7[:], C["onesrow"][:], row[0:1, half * 512:(half + 1) * 512],
                                       start=True, stop=True))
                        bcv = ph.op("vector", sV, [bm],
                                    lambda v, dstt=dstt, half=half: v.tensor_copy(
                                        dstt[:, half * 512:(half + 1) * 512], ps_bc7[:]))
                out_hist = {}
                r12_hist = {}
                r13_hist = {}
                a3_hist = {}

                def tile_chain(t_):
                    bi = t_ % 2
                    ldd = []
                    if t_ <= 1:
                        ldd.append(moe_done)
                    if t_ >= 2:
                        ldd.append(a3_hist[t_ - 2])
                    mld = ph.op("sync", yts[bi], ldd,
                                lambda sy, t_=t_, bi=bi: sy.dma_start(
                                    ytile[bi][:], moe_dram[t_ * 128:(t_ + 1) * 128, :]), 16)
                    yield
                    xld = []
                    if t_ <= 1:
                        xld = xld + xn_stores_l
                    if t_ >= 2:
                        xld.append(r12_hist[t_ - 2])
                    xl = ph.op("sync", sns[bi], xld,
                               lambda sy, t_=t_, bi=bi: sy.dma_start(
                                   snt[bi][:], xn_dram[t_ * 128:(t_ + 1) * 128, :]), 16)
                    yield
                    r1 = ph.op("vector", sV, [mld, xl, bcv],
                               lambda v, bi=bi: v.tensor_tensor(ytile[bi][:], ytile[bi][:], snt[bi][:], ALU.add))
                    yield
                    r2 = ph.op("vector", sV, [r1],
                               lambda v, bi=bi: v.tensor_tensor(ytile[bi][:], ytile[bi][:], b2bc[:], ALU.add))
                    yield
                    r3_ = ph.op("vector", sV, [r2],
                                lambda v, bi=bi: v.tensor_reduce(st7[bi][:, 0:1], ytile[bi][:], AX.X, ALU.add))
                    a1d = [r2]
                    if t_ >= 2:
                        a1d.append(r12_hist[t_ - 2])
                    a1 = ph.op("scalar", sA, a1d,
                               lambda sc, bi=bi: sc.activation(gt[bi][:], ytile[bi][:], AF.Square,
                                                               accum_out=st7[bi][:, 1:2]))
                    yield
                    r4 = ph.op("vector", sV, [r3_],
                               lambda v, bi=bi: v.tensor_scalar_mul(st7[bi][:, 2:3], st7[bi][:, 0:1], 1.0 / D))
                    yield
                    r5 = ph.op("vector", sV, [a1, r4],
                               lambda v, bi=bi: v.tensor_scalar(st7[bi][:, 3:4], st7[bi][:, 1:2], 1.0 / D, EPS,
                                                                ALU.mult, ALU.add))
                    yield
                    r6 = ph.op("vector", sV, [r4],
                               lambda v, bi=bi: v.tensor_tensor(st7[bi][:, 4:5], st7[bi][:, 2:3],
                                                                st7[bi][:, 2:3], ALU.mult))
                    yield
                    r7 = ph.op("vector", sV, [r6, r5],
                               lambda v, bi=bi: v.tensor_tensor(st7[bi][:, 3:4], st7[bi][:, 3:4],
                                                                st7[bi][:, 4:5], ALU.subtract))
                    yield
                    a2 = ph.op("scalar", sA, [r7],
                               lambda sc, bi=bi: sc.activation(st7[bi][:, 5:6], st7[bi][:, 3:4], AF.Sqrt))
                    yield
                    r8 = ph.op("vector", sV, [a2],
                               lambda v, bi=bi: v.reciprocal(st7[bi][:, 6:7], st7[bi][:, 5:6]))
                    yield
                    r9 = ph.op("vector", sV, [r8],
                               lambda v, bi=bi: v.tensor_scalar(ytile[bi][:], ytile[bi][:], st7[bi][:, 2:3],
                                                                st7[bi][:, 6:7], ALU.subtract, ALU.mult))
                    yield
                    r10 = ph.op("vector", sV, [r9],
                                lambda v, bi=bi: v.tensor_tensor(ytile[bi][:], ytile[bi][:], gbc[:], ALU.mult))
                    yield
                    r11 = ph.op("vector", sV, [r10],
                                lambda v, bi=bi: v.tensor_tensor(ytile[bi][:], ytile[bi][:], bbc[:], ALU.add))
                    yield
                    a3 = ph.op("scalar", sA, [r11],
                               lambda sc, bi=bi: sc.activation(gt[bi][:], ytile[bi][:], AF.Silu))
                    a3_hist[t_] = a3
                    yield
                    sld = [r1]
                    if t_ <= 1:
                        sld = sld + s_stores
                    sl = ph.op("sync", sns[bi], sld,
                               lambda sy, t_=t_, bi=bi: sy.dma_start(
                                   snt[bi][:], s_dram[t_ * 128:(t_ + 1) * 128, :]), 16)
                    x2d = []
                    if t_ >= 2:
                        x2d.append(r13_hist[t_ - 2])
                    xl2 = ph.op("gpsimd", x7s[bi], x2d,
                                lambda g, t_=t_, bi=bi: g.dma_start(
                                    xt7[bi][:], x[t_ * 128:(t_ + 1) * 128, :]), 16)
                    yield
                    odep = [a3, sl]
                    if t_ >= 2:
                        odep.append(out_hist[t_ - 2])
                    r12 = ph.op("vector", sV, odep,
                                lambda v, bi=bi: v.tensor_tensor(ot[bi][:], gt[bi][:], snt[bi][:], ALU.mult))
                    r12_hist[t_] = r12
                    yield
                    r13 = ph.op("vector", sV, [r12, xl2],
                                lambda v, bi=bi: v.tensor_tensor(ot[bi][:], ot[bi][:], xt7[bi][:], ALU.add))
                    r13_hist[t_] = r13
                    om = ph.op("sync", oss[bi], [r13],
                               lambda sy, t_=t_, bi=bi: sy.dma_start(
                                   out[t_ * 128:(t_ + 1) * 128, :], ot[bi][:]), 16)
                    out_hist[t_] = om

                for pr in range(16):
                    ga = tile_chain(2 * pr)
                    gb = tile_chain(2 * pr + 1)
                    alive = [ga, gb]
                    while alive:
                        nxt = []
                        for g_ in alive:
                            try:
                                next(g_)
                                nxt.append(g_)
                            except StopIteration:
                                pass
                        alive = nxt
                ph.run("p7")

        all_dma_sems = list(S.values())
        with nc.Block("fin") as blk:
            @blk.gpsimd
            def _(g):
                for s_ in all_dma_sems:
                    if s_.n > 0:
                        g.wait_ge(s_.h, s_.n)
    return nc


_NC_CACHE = {}


def kernel(**inputs):
    ins = {k: np.ascontiguousarray(np.asarray(v, dtype=np.float32)) for k, v in inputs.items()}
    if "main" not in _NC_CACHE:
        _NC_CACHE["main"] = build_nc(debug=False)
    nc = _NC_CACHE["main"]
    base = dict(
        uv_w=ins["L0_attn_uv_w"], o_w=ins["L0_attn_o_w"],
        rpe_ln_w=ins["L0_attn_rpe_ln_w"], rpe_ln_b=ins["L0_attn_rpe_ln_b"],
        rpe_fc_w=ins["L0_attn_rpe_fc_w"],
        rpe_oln_w=ins["L0_attn_rpe_out_ln_w"].reshape(1, 32),
        rpe_oln_b=ins["L0_attn_rpe_out_ln_b"].reshape(1, 32),
        rpe_pos_w=ins["L0_attn_rpe_pos_w"],
        rpe_out_w=ins["L0_attn_rpe_out_w"],
        W1=ins["L0_ffn_W1"], b1=ins["L0_ffn_b1"], W2=ins["L0_ffn_W2"], b2=ins["L0_ffn_b2"],
        Wr=ins["L0_ffn_Wr"], br=ins["L0_ffn_br"].reshape(1, NE),
        gamma=ins["L0_ffn_gamma"].reshape(1, D), beta=ins["L0_ffn_beta"].reshape(1, D),
    )
    xfull = ins["L0_attn_x"]
    in_maps = [dict(base, x=np.ascontiguousarray(xfull[i])) for i in range(8)]
    res = run_bass_kernel_spmd(nc, in_maps, core_ids=list(range(8)))
    return np.stack([res.results[i]["out"] for i in range(8)], axis=0)
